# version 02
# baseline (speedup 1.0000x reference)
import os
import numpy as np
import ml_dtypes
from contextlib import ExitStack

import concourse.bass as bass
import concourse.mybir as mybir
from concourse.bass_utils import run_bass_kernel_spmd
from concourse.library_config import mlp as _MLP_LIB

F32 = mybir.dt.float32
BF16 = mybir.dt.bfloat16
I16 = mybir.dt.int16
I32 = mybir.dt.int32
AF = mybir.ActivationFunctionType
ALU = mybir.AluOpType
AX = mybir.AxisListType

B, L, H = 2, 8192, 1024
NH, DK, DV = 4, 256, 256
E, CAPB, CAP = 8, 640, 4096
NCORE = 8
NTOK = (B * L) // NCORE
TILE = 1024
NTILE = L // TILE
NCH = TILE // 128
NCHB = NTOK // 128
WCOLS = 2052

DEBUG = []
KSTOP = 99


def make_in_maps(I):
    f32 = np.float32
    bf = ml_dtypes.bfloat16
    hs = np.ascontiguousarray(I["L0_attn_hs"], dtype=f32)
    hsf = hs.reshape(B * L, H)
    Wq, Wk, Wv, Wid = (np.asarray(I["L0_attn_Wq"]), np.asarray(I["L0_attn_Wk"]),
                       np.asarray(I["L0_attn_Wv"]), np.asarray(I["L0_attn_Wid"]))
    Wb, gW1 = np.asarray(I["L0_attn_Wb"]), np.asarray(I["L0_attn_gW1"])
    idx = np.arange(128)
    consts = dict(
        ident_bf=np.eye(128, dtype=bf),
        mask_iu=(idx[:, None] <= idx[None, :]).astype(bf),
        mask_su=(idx[:, None] < idx[None, :]).astype(bf),
        mask_sl=(idx[:, None] > idx[None, :]).astype(bf),
        tri_cum=(idx[:, None] <= idx[None, :]).astype(f32),
        ones_col=np.ones((128, 1), f32),
        ones_col_bf=np.ones((128, 1), bf),
        ones_row=np.ones((1, 128), f32),
        ones_row_bf=np.ones((1, 128), bf),
        one1=np.ones((1, 1), f32),
        iotaE=np.broadcast_to(np.arange(E, dtype=f32), (128, E)).copy(),
        tri8s=(np.arange(8)[:, None] < np.arange(8)[None, :]).astype(f32),
        tri16s=(np.arange(16)[:, None] < np.arange(16)[None, :]).astype(f32),
        epscols=np.broadcast_to(np.array([1e-5, 1e-6, 0.0, 0.0], f32), (128, 4)).copy(),
    )
    selperm = np.zeros((128, 128), f32)
    for a in range(8):
        for bb in range(16):
            selperm[16 * a + bb, 16 * a + bb] = 1.0
    consts["selperm"] = selperm
    colsel = np.zeros((128, 16 * 16), f32)
    for j in range(16):
        colsel[:, 16 * j + j] = 1.0
    consts["colsel"] = colsel
    idxdisp = np.zeros((E, 128, CAPB // 16), np.int16)
    for e in range(E):
        for i in range(CAPB):
            idxdisp[e, i % 16, i // 16] = e * CAPB + i
        idxdisp[e, 16:32] = idxdisp[e, 0:16]
    consts["idxdisp"] = idxdisp.reshape(E * 128, CAPB // 16)

    w1r = np.ascontiguousarray(np.asarray(I["L0_ffn_w1"]).reshape(E, H, 4, 1024).transpose(0, 2, 1, 3))
    w2r = np.ascontiguousarray(np.asarray(I["L0_ffn_w2"]).reshape(E, 4, 1024, H))
    b1t = np.zeros((128, E, 32), f32)
    for e in range(E):
        b1t[:, e, :] = np.asarray(I["L0_ffn_b1"])[e].reshape(32, 128).T

    in_maps = []
    for c in range(NCORE):
        b, h = c // 4, c % 4
        sl = slice(h * DK, (h + 1) * DK)
        wstack = np.zeros((H, WCOLS), f32)
        wstack[:, 0:256] = Wq[:, sl]
        wstack[:, 256:512] = Wk[:, sl]
        wstack[:, 512:768] = Wv[:, sl]
        wstack[:, 768:1024] = Wid[:, sl]
        wstack[:, 1024:2048] = gW1[:1024]
        wstack[:, 2048] = Wb[:, h]
        convw = np.zeros((128, 24), f32)
        for ti, cw in enumerate([I["L0_attn_cwq"], I["L0_attn_cwk"], I["L0_attn_cwv"]]):
            cwh = np.asarray(cw)[sl]
            for ch in range(2):
                convw[:, (ti * 2 + ch) * 4:(ti * 2 + ch) * 4 + 4] = cwh[ch * 128:(ch + 1) * 128]
        firw = np.zeros((128, 2 * 66), f32)
        for ch in range(2):
            firw[:, ch * 66:ch * 66 + 3] = np.asarray(I["L0_attn_firs"])[h][ch * 128:(ch + 1) * 128]
            firw[:, ch * 66 + 3:ch * 66 + 66] = np.asarray(I["L0_attn_firl"])[h][ch * 128:(ch + 1) * 128]
        smalls = np.zeros((1, 64), f32)
        smalls[0, 0:4] = I["L0_attn_log_temp"][h]
        smalls[0, 4:8] = I["L0_attn_floor_param"][h]
        smalls[0, 8:12] = I["L0_attn_gb2"]
        smalls[0, 12] = I["L0_attn_conv_res_logit"][h]
        smalls[0, 13] = I["L0_attn_alpha_id"][h]
        m = dict(
            hs_rows=hs[b],
            hs_slice=np.ascontiguousarray(hsf[NTOK * c:NTOK * (c + 1)]),
            wstack=wstack, convw=convw, firw=firw, smalls=smalls,
            gw1s=np.ascontiguousarray(gW1[1024:1040]),
            gb1row=np.asarray(I["L0_attn_gb1"])[None, :],
            gw2=I["L0_attn_gW2"],
            onw_row=np.asarray(I["L0_attn_o_norm_w"])[None, :],
            wo=I["L0_attn_Wo"],
            c_row=np.asarray(I["L0_c"])[b][None, :],
            wmod=I["L0_wmod"], wg=I["L0_ffn_wg"],
            w1r=w1r, w2r=w2r, b1t=b1t, b2=I["L0_ffn_b2"],
            cid=np.array([[c, b, h, 0]], np.int32),
            hotlt=(np.arange(8) < c).astype(f32)[:, None],
            **consts,
        )
        in_maps.append({k: np.ascontiguousarray(v) for k, v in m.items()})
    return in_maps


class Ctr:
    def __init__(self, sem):
        self.sem = sem
        self.v = 0

    def inc(self, n=1):
        self.v += n
        return self.v



class TE:
    def __init__(self, te):
        self._te = te
        self.last = None

    def matmul(self, *a, **k):
        self.last = self._te.matmul(*a, **k)
        return self.last

    def transpose(self, *a, **k):
        self.last = self._te.transpose(*a, **k)
        return self.last

    def wait_ge(self, *a, **k):
        return self._te.wait_ge(*a, **k)

    def fin(self, sem, n=1):
        assert self.last is not None
        self.last.then_inc(sem, n)
        self.last = None


class Prog:
    ENGS = ("sync", "gpsimd", "tensor", "scalar", "vector")

    def __init__(self):
        self.streams = {e: [] for e in self.ENGS}

    def em(self, eng, fn):
        if eng == "tensor":
            f0 = fn
            fn = lambda te, f0=f0: f0(TE(te))
        self.streams[eng].append(fn)

    def realize(self, block):
        for e in self.ENGS:
            fns = self.streams[e]
            getattr(block, e)(lambda engine, fns=fns: [f(engine) for f in fns])


def build():
    nc = bass.Bass("TRN2", target_bir_lowering=False, debug=False)
    DT = {}

    def din(name, shape, dt=F32):
        DT[name] = nc.dram_tensor(name, list(shape), dt, kind="ExternalInput").ap()

    def do_(name, shape, dt=F32):
        DT[name] = nc.dram_tensor(name, list(shape), dt, kind="ExternalOutput").ap()

    def dint(name, shape, dt=F32):
        DT[name] = nc.dram_tensor(name, tuple(shape), dt, kind="Internal").ap()

    din("hs_rows", (L, H)); din("hs_slice", (NTOK, H))
    din("wstack", (H, WCOLS)); din("convw", (128, 24)); din("firw", (128, 132))
    din("smalls", (1, 64)); din("gw1s", (16, 1024)); din("gb1row", (1, 1024))
    din("gw2", (1024, 4)); din("onw_row", (1, 256)); din("wo", (1024, 1024))
    din("c_row", (1, 1024)); din("wmod", (H, 6144)); din("wg", (H, E))
    din("w1r", (E, 4, 1024, 1024)); din("w2r", (E, 4, 1024, 1024))
    din("b1t", (128, E, 32)); din("b2", (E, 1024)); din("cid", (1, 4), I32)
    for nm in ("ident_bf", "mask_iu", "mask_su", "mask_sl"):
        din(nm, (128, 128), BF16)
    din("tri_cum", (128, 128)); din("ones_col", (128, 1)); din("ones_row", (1, 128))
    din("ones_col_bf", (128, 1), BF16); din("one1", (1, 1))
    din("ones_row_bf", (1, 128), BF16)
    din("iotaE", (128, E)); din("selperm", (128, 128)); din("colsel", (128, 256))
    din("tri8s", (8, 8)); din("tri16s", (16, 16))
    din("idxdisp", (E * 128, CAPB // 16), I16)
    din("epscols", (128, 4)); din("hotlt", (8, 1))
    do_("out", (NTOK, H), F32)

    dint("modbuf", (1, 6144))
    dint("osrc", (128, 2 * L), BF16)
    dint("odst", (4, 128, 2 * L), BF16)
    dint("cnt_src", (1, 16), F32)
    dint("cnt_dst", (8, 16), F32)
    dint("h2buf", (NTOK, H), BF16)
    dint("x2buf", (NTOK, H), F32)
    dint("bucket", (E * CAPB, H), BF16)
    dint("ybuf", (E * CAPB, H), BF16)
    dint("w1b", (E, 4, 1024, 1024), BF16)
    dint("w2b", (E, 4, 1024, 1024), BF16)
    dint("rsc", (1, 256))
    dint("rsc2", (16, 16))

    dbg_shapes = {
        "qfm": ((128, 2 * TILE), BF16), "kfm": ((128, 2 * TILE), BF16),
        "vfm": ((128, 2 * TILE), BF16), "idfm": ((128, 2 * TILE), BF16),
        "beta": ((1, TILE), F32), "rnq": ((128, NCH), F32), "rnk": ((1, TILE), F32),
        "dout": ((128, NCH * 256), BF16),
        "ls": ((128, NCH * 256), BF16), "ll": ((128, NCH * 256), BF16),
        "stats": ((128, NCH * 16), F32),
        "probs": ((128, NCH * 4), F32),
        "ofm": ((128, 2 * TILE), BF16),
        "osrcL": ((128, 2 * L), BF16),
        "x2": ((128, H), F32),
        "h2": ((128, NCHB * H), BF16),
        "scores": ((128, NCHB * E), F32),
        "route": ((128, NCHB * 8), F32),
        "moe": ((128, NCHB * H), BF16),
    }
    for nm in DEBUG:
        do_("dbg_" + nm, *dbg_shapes[nm])

    P = Prog()
    em = P.em
    with ExitStack() as ctx:
        def sbuf(name, shape, dt):
            return ctx.enter_context(nc.sbuf_tensor(name, list(shape), dt))

        def psum(name, shape, dt=F32):
            return ctx.enter_context(nc.psum_tensor(name, list(shape), dt))

        def mksem(name):
            return Ctr(ctx.enter_context(nc.semaphore(name)))

        ident = sbuf("ident", (128, 128), BF16)
        maskiu = sbuf("maskiu", (128, 128), BF16)
        masksu = sbuf("masksu", (128, 128), BF16)
        masksl = sbuf("masksl", (128, 128), BF16)
        tricum = sbuf("tricum", (128, 128), F32)
        onescol = sbuf("onescol", (128, 1), F32)
        onescolb = sbuf("onescolb", (128, 1), BF16)
        onesrow = sbuf("onesrow", (1, 128), F32)
        onesrowb = sbuf("onesrowb", (1, 128), BF16)
        one1 = sbuf("one1s", (1, 1), F32)
        iotaEt = sbuf("iotaEt", (128, E), F32)
        selpermt = sbuf("selpermt", (128, 128), F32)
        colselt = sbuf("colselt", (128, 256), F32)
        tri8t = sbuf("tri8t", (8, 8), F32)
        tri16t = sbuf("tri16t", (16, 16), F32)
        totE = sbuf("totE", (16, 16), F32)
        coffs = sbuf("coffs", (16, 16), F32)
        epst = sbuf("epst", (128, 4), F32)
        idxdispt = sbuf("idxdispt", (128, E * (CAPB // 16)), I16)
        convwt = sbuf("convwt", (128, 24), F32)
        firwt = sbuf("firwt", (128, 132), F32)
        smallst = sbuf("smallst", (1, 64), F32)
        cidt = sbuf("cidt", (1, 4), I32)
        gw1st = sbuf("gw1st", (16, 1024), BF16)
        gw2t = sbuf("gw2t", (128, 8 * 4), BF16)
        wgt = sbuf("wgt", (128, 8 * 8), F32)
        b1c = sbuf("b1c", (128, E, 32), F32)
        svec = sbuf("svec", (128, 17), F32)
        scfm = sbuf("scfm", (128, 8), F32)
        colsc = sbuf("colsc", (128, 8), F32)
        colscb = sbuf("colscb", (128, 8), BF16)
        colsc2 = sbuf("colsc2", (128, 8), F32)
        caP = sbuf("caP", (128, 8), F32)
        onw_t = sbuf("onw_t", (128, 256), BF16)
        gate_c = sbuf("gate_c", (128, 32), F32)
        gb2col = sbuf("gb2col", (4, 1), F32)
        tauinvcol = sbuf("tauinvcol", (4, 1), F32)
        betacol = sbuf("betacol", (128, NCH), F32)
        rnqcol = sbuf("rnqcol", (128, NCH), F32)
        red_a = sbuf("red_a", (128, NCHB), F32)
        red_b = sbuf("red_b", (128, NCHB), F32)
        red_c = sbuf("red_c", (128, NCHB), F32)
        probst = sbuf("probst", (128, NCH, 4), F32)
        mixc = sbuf("mixc", (128, NCH), F32)
        rowsc = sbuf("rowsc", (1, 1024), F32)
        rowsc2 = sbuf("rowsc2", (1, 1024), F32)
        rowscb = sbuf("rowscb", (1, 1024), BF16)
        betarow = sbuf("betarow", (1, TILE), F32)
        rnkrow = sbuf("rnkrow", (1, TILE), F32)
        rowscr8 = sbuf("rowscr8", (8, 512), F32)
        rowscr4 = rowscr8[0:4, :]
        cntrow = sbuf("cntrow", (1, 16), F32)
        agbuf = sbuf("agbuf", (8, 16), F32)
        prefx = sbuf("prefx", (16, 8), F32)
        myp = sbuf("myp", (16, 1), F32)
        ptile = sbuf("ptile", (128, 32), F32)
        hotlt_t = sbuf("hotlt_t", (8, 1), F32)
        scrt = sbuf("scrt", (128, NCHB, 8), F32)
        csum1 = sbuf("csum1", (128, NCHB, 8), F32)
        csum2 = sbuf("csum2", (128, NCHB, 8), F32)
        routt = sbuf("routt", (128, NCHB, 12), F32)
        slot_f = sbuf("slot_f", (128, 2, NCHB), F32)
        gatecols = sbuf("gatecols", (128, NCHB, 2), F32)
        idx16 = sbuf("idx16", (128, 2, 128), I16)
        outroll = sbuf("outroll", (128, 1, 1024), F32)

        big1 = sbuf("big1", (128, 16, 1024), BF16)
        xtok = big1[:, 0:8, :]
        xfm = big1[:, 8:16, :]
        hala = sbuf("hala", (128, 18432), BF16)
        def hview(off, shape):
            n = int(np.prod(shape))
            ap = hala[:, off:off + n]
            if len(shape) == 2:
                ap = ap.rearrange("p (a b) -> p a b", a=shape[0])
            return ap
        qraw = hview(0, (2, 1040))
        kraw = hview(2080, (2, 1040))
        vraw = hview(4160, (2, 1040))
        qfm = hview(6240, (2, 1024))
        ofm = qfm
        kfm = hview(8288, (2, 1024))
        vfm = hview(10336, (2, 1088))
        idfm = hview(12512, (2, 1024))
        scr1 = hala[:, 14560:15584]
        sqscr = hala[:, 15584:16608]
        statsfm = hala[0:16, 15584:16608]
        firslab = hala[:, 16608:17632]
        rnkbc = firslab
        h2_t = hala[:, 0:16384].rearrange("p (a b) -> p a b", a=16)
        h2fmc = hala[:, 16384:18432].bitcast(F32).rearrange("p (a b) -> p a b", a=8)
        big3 = sbuf("big3", (128, 10240), BF16)
        def b3(off, shape):
            n = int(np.prod(shape))
            ap = big3[:, off:off + n]
            if len(shape) == 2:
                ap = ap.rearrange("p (a b) -> p a b", a=shape[0])
            return ap
        ktok = b3(0, (8, 256))
        vtok = b3(2048, (8, 256))
        u_st = b3(4096, (8, 256))
        wT_st = b3(6144, (8, 256))
        S_st = big3[:, 8192:8704].rearrange("p (a b) -> p a b", a=2)
        Mt_s = big3[:, 8704:8832]
        MtT_s = big3[:, 8832:8960]
        Y_s = big3[:, 8960:9088]
        Pp_s = big3[:, 9088:9344]
        TTb_s = big3[:, 9344:9472]
        uadj = big3[:, 9472:9728]
        id_t = big3[:, 9728:9984]
        stats_c = big3[:, 9984:10240].bitcast(F32).rearrange("p (a b) -> p a b", a=8)
        b1f = big1[:].rearrange("p a b -> p (a b)")
        ls_t = b1f[:, 0:2048].rearrange("p (a b) -> p a b", a=8)
        ll_t = b1f[:, 2048:4096].rearrange("p (a b) -> p a b", a=8)
        dout_t = b1f[:, 4096:6144].rearrange("p (a b) -> p a b", a=8)
        attnT = b1f[:, 6144:7168].rearrange("p (a b) -> p a b", a=8)
        oall = b1f[:, 0:16384].rearrange("p (a b) -> p a b", a=8)
        x2roll = big3[:, 0:4096].bitcast(F32).rearrange("p (a b) -> p a b", a=2)
        x2c = b3(0, (2, 1024))
        big4 = sbuf("big4", (128, 25600), BF16)
        xe_b = big4[:, 0:5120].rearrange("p (a b) -> p a b", a=8)
        hid_b = big4[:, 5120:25600].rearrange("p (a b) -> p a b", a=32)
        zerot = big4[:, 4096:8192]
        ga_t = big4[:, 0:1024]
        cb1_t = big4[:, 1024:2048]
        sbt_t = big4[:, 2048:3072]
        gb_t = hala[:, 1024:2048]
        b2t = hala[:, 0:1024]
        g0h = big4[:, 0:8192].rearrange("p (a b) -> p a b", a=8)
        g1h = big4[:, 8192:16384].rearrange("p (a b) -> p a b", a=8)
        wstk = sbuf("wstk", (128, 8, 2052), BF16)
        wqbuf = wstk[:].rearrange("p a b -> p (a b)")
        sld = mksem("sld")
        spe = mksem("spe")
        sve = mksem("sve")
        ssc = mksem("ssc")
        sgp = mksem("sgp")
        sdm = mksem("sdm")
        ssy = mksem("ssy")
        sst = mksem("sst")
        scc = mksem("scc")
        swa = mksem("swa")
        swb = mksem("swb")

        PB0 = psum("PB0", (128, 1024))
        PB1 = psum("PB1", (128, 1024))
        PB2 = psum("PB2", (128, 1024))
        PB3 = psum("PB3", (128, 1024))
        PB0b = PB0[:].bitcast(BF16)
        PB1b = PB1[:].bitcast(BF16)
        PB2b = PB2[:].bitcast(BF16)
        PB3b = PB3[:].bitcast(BF16)

        def vseq(ops, waits=()):
            base = sve.v
            n = len(ops)

            def v(ve):
                for s, val in waits:
                    ve.wait_ge(s.sem, val)
                for i, op in enumerate(ops):
                    if i > 0:
                        ve.wait_ge(sve.sem, base + i)
                    op(ve).then_inc(sve.sem, 1)
            em("vector", v)
            sve.inc(n)
            return sve.v

        def sseq(ops, waits=()):
            base = ssc.v
            n = len(ops)

            def s(se):
                for sm, val in waits:
                    se.wait_ge(sm.sem, val)
                for i, op in enumerate(ops):
                    if i > 0:
                        se.wait_ge(ssc.sem, base + i)
                    op(se).then_inc(ssc.sem, 1)
            em("scalar", s)
            ssc.inc(n)
            return ssc.v

        def row_to_col(rowap, colap, n, psrc, waits=()):
            def t(te):
                for s, val in waits:
                    te.wait_ge(s.sem, val)
                for j in range(n):
                    te.transpose(psrc[:, j:j + 1], rowap[0:1, 128 * j:128 * (j + 1)], one1[:])
                te.fin(spe.sem, 1)
            em("tensor", t)
            pv = spe.inc()
            return vseq([lambda ve: ve.tensor_copy(colap, psrc[:, 0:n])],
                        waits=[(spe, pv)])

        def dbg_store(name, src_ap, waits=()):
            if name not in DEBUG:
                return
            def sy(e, t=DT["dbg_" + name], s=src_ap, w=tuple(waits)):
                for sm, val in w:
                    e.wait_ge(sm.sem, val)
                e.dma_start(out=t, in_=s).then_inc(sst.sem, 16)
            em("sync", sy)
            sst.inc(16)

        em("vector", lambda ve: ve.memset(outroll[:].rearrange("p a b -> p (a b)"), 0.0)
           .then_inc(sve.sem, 1))
        sve.inc()
        for nm, t in [("ident_bf", ident), ("mask_iu", maskiu), ("mask_su", masksu),
                      ("mask_sl", masksl), ("tri_cum", tricum), ("ones_col", onescol),
                      ("ones_row", onesrow), ("one1", one1), ("iotaE", iotaEt),
                      ("selperm", selpermt), ("colsel", colselt), ("tri8s", tri8t),
                      ("convw", convwt), ("firw", firwt), ("smalls", smallst),
                      ("cid", cidt), ("ones_col_bf", onescolb), ("epscols", epst),
                      ("ones_row_bf", onesrowb), ("tri16s", tri16t),
                      ("hotlt", hotlt_t)]:
            em("sync", lambda e, t=t, nm=nm: e.dma_start(out=t[:], in_=DT[nm])
               .then_inc(sld.sem, 16))
            sld.inc(16)
        em("sync", lambda e: e.dma_start(
            out=idxdispt[:].rearrange("p (e c) -> p e c", e=E),
            in_=DT["idxdisp"].rearrange("(e p) c -> p e c", p=128))
            .then_inc(sld.sem, 16))
        sld.inc(16)
        em("sync", lambda e: e.dma_start(out=b1c[:], in_=DT["b1t"])
           .then_inc(sld.sem, 16))
        sld.inc(16)

        def g0(gp):
            gp.load_library(_MLP_LIB)
            gp.dma_start(out=gw1st[:], in_=DT["gw1s"]).then_inc(sdm.sem, 16)
            gp.dma_start(out=gw2t[:].rearrange("p (k f) -> p k f", k=8),
                         in_=DT["gw2"].rearrange("(k p) f -> p k f", p=128)).then_inc(sdm.sem, 16)
            for kt in range(8):
                gp.dma_start(out=wstk[:, kt, :],
                             in_=DT["wstack"][128 * kt:128 * (kt + 1), :]).then_inc(sdm.sem, 16)
            gp.dma_start(out=wgt[:].rearrange("p (a b) -> p a b", a=8),
                         in_=DT["wg"].rearrange("(k p) f -> p k f", p=128)).then_inc(sdm.sem, 16)
        em("gpsimd", g0)
        sdm.inc(16 * 11)

        def g_pre(gp):
            for e2 in range(E):
                for q2 in range(4):
                    gp.dma_start(out=DT["w1b"][e2, q2], in_=DT["w1r"][e2, q2]
                                 ).then_inc(sdm.sem, 16)
                    gp.dma_start(out=DT["w2b"][e2, q2], in_=DT["w2r"][e2, q2]
                                 ).then_inc(sdm.sem, 16)
        em("gpsimd", g_pre)
        PRE_V = sdm.inc(16 * 64)
        LD0G = sdm.v
        LD0 = sld.v

        if KSTOP >= 1:
            em("sync", lambda e: e.dma_start(out=rowsc[:], in_=DT["c_row"]).then_inc(sld.sem, 16))
            c_l = sld.inc(16)
            sv = sseq([lambda s: s.activation(rowsc2[:], rowsc[:], AF.Silu)],
                      waits=[(sld, c_l)])
            row_to_col(rowsc2, scfm[:], 8, PB3[:, 0:8], waits=[(ssc, sv)])
            scfm_v = sve.v

            wmod_stg = big1[:].rearrange("p a b -> p (a b)").bitcast(F32)
            for rnd in range(2):
                for kt in range(8):
                    def g(gp, kt=kt, rnd=rnd, pv=spe.v):
                        gp.wait_ge(spe.sem, pv)
                        gp.dma_start(out=wmod_stg[:, 0:3072],
                                     in_=DT["wmod"][128 * kt:128 * (kt + 1),
                                                    3072 * rnd:3072 * (rnd + 1)]).then_inc(sdm.sem, 16)
                    em("gpsimd", g)
                    dv = sdm.inc(16)

                    def t(te, kt=kt, dv=dv, vw=(scfm_v if rnd == 0 else sve.v)):
                        te.wait_ge(sdm.sem, dv)
                        if kt == 0:
                            te.wait_ge(sve.sem, vw)
                        for gq in range(6):
                            te.matmul([PB0, PB0, PB1, PB1, PB2, PB2][gq][0:1, 512 * (gq % 2):512 * (gq % 2 + 1)],
                                      scfm[:, kt:kt + 1],
                                      wmod_stg[:, 512 * gq:512 * (gq + 1)],
                                      start=(kt == 0), stop=(kt == 7))
                        te.fin(spe.sem, 1)
                    em("tensor", t)
                    spe.inc()
                for gq in range(3):
                    gv = vseq([lambda ve, gq=gq: ve.tensor_copy(
                        rowsc[:], [PB0, PB1, PB2][gq][0:1, 0:1024])],
                        waits=[(spe, spe.v), (ssy, ssy.v)])
                    def sy(e, rnd=rnd, gq=gq, gv=gv):
                        e.wait_ge(sve.sem, gv)
                        e.dma_start(out=DT["modbuf"][0:1, 3072 * rnd + 1024 * gq:
                                                     3072 * rnd + 1024 * (gq + 1)],
                                    in_=rowsc[:]).then_inc(ssy.sem, 16)
                    em("sync", sy)
                    ssy.inc(16)
            MODB_V = ssy.v

            def modload(off, dst):
                def sy(e, w=ssy.v, pv=sld.v, pe=spe.v, vv=sve.v, sc=ssc.v):
                    e.wait_ge(ssy.sem, w)
                    e.wait_ge(sld.sem, pv)
                    e.wait_ge(spe.sem, pe)
                    e.wait_ge(sve.sem, vv)
                    e.wait_ge(ssc.sem, sc)
                    e.dma_start(out=dst, in_=DT["modbuf"][0:1, off:off + 1024]).then_inc(sld.sem, 16)
                em("sync", sy)
                return sld.inc(16)

        if KSTOP >= 2:
            sa_l = modload(0, rowsc[:])
            sa_v = row_to_col(rowsc, colscb[:], 8, PB3[:, 0:8], waits=[(sld, sa_l)])

            def t_svec(te, sa_v=sa_v):
                te.wait_ge(sve.sem, sa_v)
                for m in range(17):
                    w = min(128, WCOLS - 128 * m)
                    for kt in range(8):
                        te.matmul(PB3[0:w, 16 + m:17 + m], wstk[:, kt, 128 * m:128 * m + w],
                                  colscb[:, kt:kt + 1], start=(kt == 0), stop=(kt == 7))
                te.fin(spe.sem, 1)
            em("tensor", t_svec)
            svec_pe = spe.inc()
            em("sync", lambda e, pv=sld.v, pe=spe.v, vv=sve.v: (
                e.wait_ge(sld.sem, pv), e.wait_ge(spe.sem, pe), e.wait_ge(sve.sem, vv),
                e.dma_start(out=rowsc2[:], in_=DT["gb1row"]).then_inc(sld.sem, 16)))
            gb1_l = sld.inc(16)
            gb1_v = row_to_col(rowsc2, colsc2[:], 8, PB3[:, 8:16], waits=[(sld, gb1_l)])
            vseq([lambda ve: ve.tensor_copy(svec[:, 0:16], PB3[:, 16:32]),
                  lambda ve: ve.tensor_copy(svec[0:4, 16:17], PB3[0:4, 32:33]),
                  lambda ve: ve.tensor_add(svec[:, 8:16], svec[:, 8:16], colsc2[:])],
                 waits=[(spe, svec_pe), (sve, gb1_v)])

            ca_l = modload(1024, rowsc[:])
            ca_v = vseq([lambda ve: ve.tensor_scalar_add(rowsc[:], rowsc[:], 1.0)],
                        waits=[(sld, ca_l)])
            cap_v = row_to_col(rowsc, caP[:], 8, PB3[:, 0:8], waits=[(sve, ca_v)])
            vseq([(lambda ve, kt=kt: ve.tensor_scalar(wstk[:, kt, :], wstk[:, kt, :],
                                                      caP[:, kt:kt + 1], None, ALU.mult))
                  for kt in range(8)], waits=[(sve, cap_v)])
            WSTK_V = sve.v

            def plain_bcast(off, dst_tile, plus1=False, width=1024):
                ld = modload(off, rowsc[:]) if off >= 0 else None
                cb_v = vseq([lambda ve, width=width: ve.tensor_copy(
                    rowscb[0:1, 0:width], rowsc[0:1, 0:width])],
                    waits=([(sld, ld)] if ld is not None else []) + [(sve, sve.v)])
                def t_b(te, cb_v=cb_v, width=width):
                    te.wait_ge(sve.sem, cb_v)
                    for g in range(0, width, 512):
                        gw = min(512, width - g)
                        te.matmul(PB0[:, g:g + gw], onesrowb[:], rowscb[0:1, g:g + gw],
                                  start=True, stop=True)
                    te.fin(spe.sem, 1)
                em("tensor", t_b)
                pv = spe.inc()
                if plus1:
                    return vseq([lambda ve, width=width: ve.tensor_scalar_add(
                        dst_tile, PB0[:, 0:width], 1.0)], waits=[(spe, pv)])
                return vseq([lambda ve, width=width: ve.tensor_copy(dst_tile, PB0[:, 0:width])],
                            waits=[(spe, pv)])

            def signmax_store(off, dst_tile):
                ld = modload(off, rowsc[:])
                s_v = sseq([lambda s: s.activation(rowsc2[:], rowsc[:], AF.Sign)],
                           waits=[(sld, ld)])
                m_v = vseq([lambda ve: ve.tensor_scalar_mul(rnkrow[:], rowsc[:], -1.0),
                            lambda ve: ve.tensor_max(rowsc[:], rowsc[:], rnkrow[:]),
                            lambda ve: ve.tensor_scalar_max(rowsc[:], rowsc[:], 0.25),
                            lambda ve: ve.tensor_mul(rowsc[:], rowsc[:], rowsc2[:])],
                           waits=[(ssc, s_v)])
                def sy(e, off=off, m_v=m_v):
                    e.wait_ge(sve.sem, m_v)
                    e.dma_start(out=DT["modbuf"][0:1, off:off + 1024], in_=rowsc[:]).then_inc(ssy.sem, 16)
                em("sync", sy)
                ssy.inc(16)
                if dst_tile is not None:
                    cb_v = vseq([lambda ve: ve.tensor_copy(rowscb[:], rowsc[:])],
                                waits=[(sve, m_v)])
                    def t_b(te, cb_v=cb_v, vv=sve.v):
                        te.wait_ge(sve.sem, max(cb_v, vv))
                        te.matmul(PB0[:, 0:512], onesrowb[:], rowscb[0:1, 0:512], start=True, stop=True)
                        te.matmul(PB0[:, 512:1024], onesrowb[:], rowscb[0:1, 512:1024], start=True, stop=True)
                        te.fin(spe.sem, 1)
                    em("tensor", t_b)
                    pv = spe.inc()
                    vseq([lambda ve, dst_tile=dst_tile: ve.tensor_copy(dst_tile, PB0[:, 0:1024])],
                         waits=[(spe, pv)])
            signmax_store(2048, ga_t)
            signmax_store(5120, None)
            plain_bcast(4096, cb1_t, plus1=True)
            plain_bcast(3072, sbt_t)
            em("sync", lambda e, pv=sld.v, pe=spe.v, vv=sve.v: (
                e.wait_ge(sld.sem, pv), e.wait_ge(spe.sem, pe), e.wait_ge(sve.sem, vv),
                e.dma_start(out=rowsc[0:1, 0:256], in_=DT["onw_row"]).then_inc(sld.sem, 16)))
            onw_l = sld.inc(16)
            ocb_v = vseq([lambda ve: ve.tensor_copy(rowscb[0:1, 0:256], rowsc[0:1, 0:256])],
                         waits=[(sld, onw_l), (sve, sve.v)])
            def t_onw(te, ocb_v=ocb_v):
                te.wait_ge(sve.sem, ocb_v)
                te.matmul(PB0[:, 0:256], onesrowb[:], rowscb[0:1, 0:256], start=True, stop=True)
                te.fin(spe.sem, 1)
            em("tensor", t_onw)
            vseq([lambda ve: ve.tensor_copy(onw_t[:], PB0[:, 0:256])], waits=[(spe, spe.inc())])

            s_v = sseq([lambda s: s.activation(rowsc[0:1, 0:4], smallst[0:1, 0:4], AF.Exp),
                        lambda s: s.activation(rowsc[0:1, 4:8], smallst[0:1, 4:8], AF.Sigmoid),
                        lambda s: s.activation(rowsc[0:1, 9:10], smallst[0:1, 12:13], AF.Sigmoid)],
                       waits=[(sld, sld.v), (sve, sve.v)])
            gv = vseq([lambda ve: ve.reciprocal(rowsc[0:1, 0:4], rowsc[0:1, 0:4]),
                       lambda ve: ve.tensor_scalar_mul(rowsc[0:1, 4:8], rowsc[0:1, 4:8], 0.05),
                       lambda ve: ve.reduce_sum(rowsc[0:1, 8:9], rowsc[0:1, 4:8], AX.X, negate=True),
                       lambda ve: ve.tensor_scalar_add(rowsc[0:1, 8:9], rowsc[0:1, 8:9], 1.0),
                       lambda ve: ve.tensor_copy(rowsc[0:1, 10:11], smallst[0:1, 13:14]),
                       lambda ve: ve.tensor_copy(rowsc[0:1, 11:15], smallst[0:1, 8:12])],
                      waits=[(ssc, s_v)])
            def t_gc(te, w=gv):
                te.wait_ge(sve.sem, w)
                te.transpose(PB3[0:4, 40:41], rowsc[0:1, 0:4], one1[:])
                te.transpose(PB3[0:4, 41:42], rowsc[0:1, 11:15], one1[:])
                te.fin(spe.sem, 1)
            em("tensor", t_gc)
            gcp = spe.inc()
            vseq([lambda ve: ve.tensor_copy(tauinvcol[:], PB3[0:4, 40:41]),
                  lambda ve: ve.tensor_copy(gb2col[:], PB3[0:4, 41:42])],
                 waits=[(spe, gcp)])
            gcb_v = vseq([lambda ve: ve.tensor_copy(rowscb[0:1, 0:32], rowsc[0:1, 0:32])],
                         waits=[(sve, gv)])
            def t_bc3(te, gcb_v=gcb_v):
                te.wait_ge(sve.sem, gcb_v)
                te.matmul(PB1[:, 0:32], onesrowb[:], rowscb[0:1, 0:32], start=True, stop=True)
                te.fin(spe.sem, 1)
            em("tensor", t_bc3)
            vseq([lambda ve: ve.tensor_copy(gate_c[:], PB1[:, 0:32])], waits=[(spe, spe.inc())])
            SETUP_DONE = dict(ve=sve.v, pe=spe.v)

        o_dma_ct = 0

        def tileA(t_i):
            nonlocal o_dma_ct
            tok0 = TILE * t_i

            def g(gp, xw=sve.v):
                gp.wait_ge(sve.sem, xw)
                for chn in range(NCH):
                    gp.dma_start(out=xtok[:, chn, :],
                                 in_=DT["hs_rows"][tok0 + 128 * chn:tok0 + 128 * (chn + 1), :]).then_inc(sdm.sem, 16)
            em("gpsimd", g)
            ld_v = sdm.inc(16 * NCH)

            sq_v = sseq([(lambda s, chn=chn: s.activation(sqscr[:, 0:1024], xtok[:, chn, :],
                                                          AF.Square, accum_out=red_b[:, chn:chn + 1]))
                         for chn in range(NCH)],
                        waits=[(sdm, ld_v), (sgp, sgp.v), (sve, sve.v)])
            ra_v = vseq([(lambda ve, chn=chn: ve.reduce_sum(red_a[:, chn:chn + 1], xtok[:, chn, :], AX.X))
                  for chn in range(NCH)], waits=[(sdm, ld_v)])
            ln_v = vseq([
                lambda ve: ve.tensor_scalar_mul(red_a[:, 0:NCH], red_a[:, 0:NCH], 1.0 / H),
                lambda ve: ve.tensor_scalar_mul(red_b[:, 0:NCH], red_b[:, 0:NCH], 1.0 / H),
                lambda ve: ve.tensor_mul(red_c[:, 0:NCH], red_a[:, 0:NCH], red_a[:, 0:NCH]),
                lambda ve: ve.tensor_sub(red_b[:, 0:NCH], red_b[:, 0:NCH], red_c[:, 0:NCH]),
            ], waits=[(ssc, sq_v), (sve, ra_v)])
            rs_v = sseq([lambda s: s.activation(red_b[:, 0:NCH], red_b[:, 0:NCH],
                                                AF.Sqrt, bias=epst[:, 0:1])], waits=[(sve, ln_v)])
            rc_v = vseq([lambda ve: ve.reciprocal(red_b[:, 0:NCH], red_b[:, 0:NCH])],
                        waits=[(ssc, rs_v)])
            xt_v = vseq([(lambda ve, chn=chn: ve.tensor_scalar(
                            xtok[:, chn, :], xtok[:, chn, :],
                            red_a[:, chn:chn + 1], red_b[:, chn:chn + 1],
                            ALU.subtract, ALU.mult)) for chn in range(NCH)],
                        waits=[(sve, rc_v)])

            for fc in range(8):
                for grp in range(2):
                    def t_b(te, fc=fc, grp=grp, xw=xt_v, vv=sve.v):
                        te.wait_ge(sve.sem, max(xw, vv))
                        for j in range(4):
                            chn = grp * 4 + j
                            te.transpose(PB2b[:, 128 * j:128 * (j + 1)],
                                         xtok[:, chn, 128 * fc:128 * (fc + 1)], ident[:])
                        te.fin(spe.sem, 1)
                    em("tensor", t_b)
                    pv = spe.inc()
                    vseq([lambda ve, fc=fc, grp=grp: ve.tensor_copy(
                        xfm[:, fc, 512 * grp:512 * (grp + 1)], PB2b[:, 0:512])],
                        waits=[(spe, pv)])
            XFM_V = sve.v

            mlist = [0, 1, 2, 3, 4, 5, 6, 7, 16]
            raw_dst = {0: (qraw, 0), 1: (qraw, 1), 2: (kraw, 0), 3: (kraw, 1),
                       4: (vraw, 0), 5: (vraw, 1), 6: (idfm, 0), 7: (idfm, 1)}
            evac_hist = []
            for mi, m in enumerate(mlist):
                pb = PB0 if mi % 2 == 0 else PB1
                wait_evac = evac_hist[mi - 2] if mi >= 2 else None

                def t_m(te, m=m, pb=pb, xv=XFM_V, we=wait_evac, sw=ssc.v):
                    te.wait_ge(sve.sem, xv)
                    te.wait_ge(ssc.sem, we if we is not None else sw)
                    w = min(128, WCOLS - 128 * m)
                    for kt in range(8):
                        for sub in range(2):
                            te.matmul(pb[0:w, 512 * sub:512 * (sub + 1)],
                                      wstk[:, kt, 128 * m:128 * m + w],
                                      xfm[:, kt, 512 * sub:512 * (sub + 1)],
                                      start=(kt == 0), stop=(kt == 7))
                    te.fin(spe.sem, 1)
                em("tensor", t_m)
                pv = spe.inc()
                if m == 16:
                    ev = sseq([lambda s, pb=pb: s.activation(
                        betarow[:], pb[0:1, 0:TILE], AF.Sigmoid, bias=svec[0:1, 16:17])],
                        waits=[(spe, pv)])
                else:
                    dst, half = raw_dst[m]
                    if dst is idfm:
                        tgt = dst[:, half, :]
                    else:
                        tgt = dst[:, half, 8:8 + TILE]
                    ev = sseq([lambda s, tgt=tgt, pb=pb, m=m: s.activation(
                        tgt, pb[:, 0:TILE], AF.Identity, bias=svec[:, m:m + 1])],
                        waits=[(spe, pv)])
                evac_hist.append(ev)
            PROJ_V = ssc.v

            bcol_v = row_to_col(betarow, betacol[:, 0:NCH], NCH, PB2[:, 0:NCH],
                                waits=[(ssc, PROJ_V)])

            conv_done = []
            prev_sv2 = PROJ_V
            conv_extra = [(sst, sst.v), (sgp, sgp.v)]
            for ti_n, (raw, dstf, coff) in enumerate(
                    [(qraw, qfm, 0), (kraw, kfm, 8), (vraw, vfm, 16)]):
                for half in range(2):
                    wcol = coff + half * 4
                    if dstf is vfm:
                        tgt = vfm[:, half, 64:64 + TILE]
                    else:
                        tgt = dstf[:, half, :]
                    vv = vseq([
                        lambda ve, raw=raw, half=half, wcol=wcol: ve.tensor_scalar(
                            sqscr[:, 0:TILE], raw[:, half, 5:5 + TILE],
                            convwt[:, wcol:wcol + 1], None, ALU.mult),
                        lambda ve, raw=raw, half=half, wcol=wcol: ve.scalar_tensor_tensor(
                            sqscr[:, 0:TILE], raw[:, half, 6:6 + TILE],
                            convwt[:, wcol + 1:wcol + 2], sqscr[:, 0:TILE],
                            ALU.mult, ALU.add),
                        lambda ve, raw=raw, half=half, wcol=wcol: ve.scalar_tensor_tensor(
                            sqscr[:, 0:TILE], raw[:, half, 7:7 + TILE],
                            convwt[:, wcol + 2:wcol + 3], sqscr[:, 0:TILE],
                            ALU.mult, ALU.add),
                        lambda ve, raw=raw, half=half, wcol=wcol: ve.scalar_tensor_tensor(
                            sqscr[:, 0:TILE], raw[:, half, 8:8 + TILE],
                            convwt[:, wcol + 3:wcol + 4], sqscr[:, 0:TILE],
                            ALU.mult, ALU.add),
                    ], waits=[(ssc, prev_sv2)] + conv_extra)
                    sv2 = sseq([lambda s, tgt=tgt: s.activation(tgt, sqscr[:, 0:TILE], AF.Silu)],
                               waits=[(sve, vv)])
                    prev_sv2 = sv2
                    conv_done.append(sv2)
            CONV_V = ssc.v

            halo_v = vseq([
                (lambda ve, raw=raw, half=half: ve.tensor_copy(
                    raw[:, half, 5:8], raw[:, half, 5 + TILE:8 + TILE]))
                for raw in (qraw, kraw, vraw) for half in range(2)
            ], waits=[(ssc, CONV_V)])
            fhalo_v = None

            for tname, src in (("q", qfm), ("k", kfm)):
                for half in range(2):
                    sv3 = sseq([lambda s, src=src, half=half: s.activation(
                        sqscr[:, 0:TILE], src[:, half, :], AF.Square)],
                        waits=[(ssc, CONV_V), (spe, spe.v)])
                    def t_q(te, half=half, sv3=sv3):
                        te.wait_ge(ssc.sem, sv3)
                        for sub in range(2):
                            te.matmul(PB3[0:1, 512 * sub:512 * (sub + 1)], onescolb[:],
                                      sqscr[:, 512 * sub:512 * (sub + 1)],
                                      start=(half == 0), stop=(half == 1))
                        te.fin(spe.sem, 1)
                    em("tensor", t_q)
                    spe.inc()
                rowdst = rowsc2[0:1, 0:TILE] if tname == "q" else rnkrow[:]
                rv = sseq([lambda s, rowdst=rowdst: s.activation(
                    rowdst, PB3[0:1, 0:TILE], AF.Sqrt, bias=epst[0:1, 1:2])],
                    waits=[(spe, spe.v)])
                if tname == "q":
                    cv = row_to_col(rowsc2, rnqcol[:, 0:NCH], NCH, PB2[:, 0:NCH],
                                    waits=[(ssc, rv)])
                    vseq([lambda ve: ve.reciprocal(rnqcol[:, 0:NCH], rnqcol[:, 0:NCH])],
                         waits=[(sve, cv)])
                else:
                    vseq([lambda ve: ve.reciprocal(rnkrow[:], rnkrow[:])],
                         waits=[(ssc, rv)])
            RN_V = sve.v
            rq_v = RN_V
            rkcb_v = vseq([lambda ve: ve.tensor_copy(rowscb[0:1, 0:TILE], rnkrow[:])],
                          waits=[(sve, RN_V)])
            def t_rk(te, w=rkcb_v):
                te.wait_ge(sve.sem, w)
                te.matmul(PB3[:, 0:512], onesrowb[:], rowscb[0:1, 0:512], start=True, stop=True)
                te.matmul(PB3[:, 512:1024], onesrowb[:], rowscb[0:1, 512:1024], start=True, stop=True)
                te.fin(spe.sem, 1)
            em("tensor", t_rk)
            prk = spe.inc()
            rkb_v = vseq([lambda ve: ve.tensor_copy(rnkbc[:, 0:TILE], PB3[:, 0:TILE])],
                         waits=[(spe, prk)])
            kh_v = vseq([
                lambda ve: ve.tensor_mul(kfm[:, 0, :], kfm[:, 0, :], rnkbc[:, 0:TILE]),
                lambda ve: ve.tensor_mul(kfm[:, 1, :], kfm[:, 1, :], rnkbc[:, 0:TILE]),
            ], waits=[(sve, rkb_v)])

            bcb_v = vseq([lambda ve: ve.tensor_copy(rowscb[0:1, 0:TILE], betarow[:])],
                         waits=[(ssc, PROJ_V), (sve, kh_v)])

            def t_bb(te, w=bcb_v, vv=sve.v):
                te.wait_ge(sve.sem, max(w, vv))
                te.matmul(PB3[:, 0:512], onesrowb[:], rowscb[0:1, 0:512],
                          start=True, stop=True)
                te.matmul(PB3[:, 512:1024], onesrowb[:], rowscb[0:1, 512:1024],
                          start=True, stop=True)
                te.fin(spe.sem, 1)
            em("tensor", t_bb)
            pv = spe.inc()
            bbc_v = vseq([lambda ve: ve.tensor_copy(scr1[:, 0:TILE], PB3[:, 0:TILE])],
                         waits=[(spe, pv)])
            betabc = scr1

            PB3f = PB3
            for cc in range(NCH):
                C = slice(128 * cc, 128 * (cc + 1))
                def t_tp(te, cc=cc, C=C, vv=sve.v, kw=kh_v):
                    te.wait_ge(sve.sem, max(vv, kw))
                    te.transpose(PB2b[:, 0:128], kfm[:, 0, C], ident[:])
                    te.transpose(PB2b[:, 128:256], kfm[:, 1, C], ident[:])
                    te.transpose(PB2b[:, 256:384], vfm[:, 0, 64 + 128 * cc:64 + 128 * (cc + 1)], ident[:])
                    te.transpose(PB2b[:, 384:512], vfm[:, 1, 64 + 128 * cc:64 + 128 * (cc + 1)], ident[:])
                    te.fin(spe.sem, 1)
                em("tensor", t_tp)
                pv = spe.inc()
                tp_v = vseq([lambda ve, cc=cc: ve.tensor_copy(ktok[:, cc, :], PB2b[:, 0:256]),
                             lambda ve, cc=cc: ve.tensor_copy(vtok[:, cc, :], PB2b[:, 256:512])],
                            waits=[(spe, pv)])
                def t_ga(te, C=C, vv=tp_v):
                    te.wait_ge(sve.sem, vv)
                    for kt in range(2):
                        te.matmul(PB3f[:, 0:128], kfm[:, kt, C], kfm[:, kt, C],
                                  start=(kt == 0), stop=(kt == 1))
                    for kt in range(2):
                        te.matmul(PB3f[:, 128:256], kfm[:, kt, C], qfm[:, kt, C],
                                  start=(kt == 0), stop=(kt == 1))
                    te.fin(spe.sem, 1)
                em("tensor", t_ga)
                pv = spe.inc()
                ga_v = vseq([
                    lambda ve, cc=cc: ve.tensor_mul(id_t[:, 0:128], masksu[:], betabc[:, 128 * cc:128 * (cc + 1)]),
                    lambda ve: ve.tensor_mul(Mt_s[:], PB3f[:, 0:128], id_t[:, 0:128]),
                    lambda ve, cc=cc: ve.scalar_tensor_tensor(
                        MtT_s[:], PB3f[:, 0:128], betacol[:, cc:cc + 1], masksl[:],
                        ALU.mult, ALU.mult),
                    lambda ve, cc=cc: ve.tensor_mul(attnT[:, cc, :], PB3f[:, 128:256], maskiu[:]),
                ], waits=[(spe, pv)])
                y0_v = vseq([lambda ve: ve.scalar_tensor_tensor(
                    Y_s[:], MtT_s[:], -1.0, ident[:], ALU.mult, ALU.add)],
                    waits=[(sve, ga_v)])

                def t_p1(te, w=y0_v):
                    te.wait_ge(sve.sem, w)
                    te.matmul(PB3f[:, 256:384], MtT_s[:], Mt_s[:], start=True, stop=True)
                    te.matmul(PB3f[:, 384:512], Mt_s[:], MtT_s[:], start=True, stop=True)
                    te.fin(spe.sem, 1)
                em("tensor", t_p1)
                pv = spe.inc()
                pp_v = vseq([lambda ve: ve.tensor_copy(Pp_s[:], PB3f[:, 256:512])],
                            waits=[(spe, pv)])
                for k in range(1, 7):
                    def t_pr(te, w=pp_v, k=k):
                        te.wait_ge(sve.sem, w)
                        te.matmul(PB3f[:, 512:640], Pp_s[:, 0:128], Y_s[:],
                                  start=True, stop=True)
                        if k < 6:
                            te.matmul(PB3f[:, 256:384], Pp_s[:, 128:256], Pp_s[:, 0:128],
                                      start=True, stop=True)
                            te.matmul(PB3f[:, 384:512], Pp_s[:, 0:128], Pp_s[:, 128:256],
                                      start=True, stop=True)
                        te.fin(spe.sem, 1)
                    em("tensor", t_pr)
                    pv = spe.inc()
                    ops = [lambda ve: ve.tensor_add(Y_s[:], PB3f[:, 512:640], Y_s[:])]
                    if k < 6:
                        ops.append(lambda ve: ve.tensor_copy(Pp_s[:], PB3f[:, 256:512]))
                    pp_v = vseq(ops, waits=[(spe, pv)])
                def t_tt(te, w=pp_v):
                    te.wait_ge(sve.sem, w)
                    te.transpose(PB2b[:, 0:128], Y_s[:], ident[:])
                    te.fin(spe.sem, 1)
                em("tensor", t_tt)
                pv = spe.inc()
                tt_v = vseq([lambda ve, cc=cc: ve.tensor_scalar(
                    TTb_s[:], PB2b[:, 0:128], betacol[:, cc:cc + 1], None, ALU.mult)],
                    waits=[(spe, pv)])
                def t_uw(te, cc=cc, w=tt_v):
                    te.wait_ge(sve.sem, w)
                    te.matmul(PB3f[:, 0:256], TTb_s[:], vtok[:, cc, :], start=True, stop=True)
                    te.matmul(PB3f[:, 256:384], ktok[:, cc, 0:128], TTb_s[:], start=True, stop=True)
                    te.matmul(PB3f[:, 384:512], ktok[:, cc, 128:256], TTb_s[:], start=True, stop=True)
                    te.fin(spe.sem, 1)
                em("tensor", t_uw)
                pv = spe.inc()
                vseq([lambda ve, cc=cc: ve.tensor_copy(u_st[:, cc, :], PB3f[:, 0:256]),
                      lambda ve, cc=cc: ve.tensor_copy(wT_st[:, cc, :], PB3f[:, 256:512])],
                     waits=[(spe, pv)])
            PRE_V = sve.v

            for cc in range(NCH):
                C = slice(128 * cc, 128 * (cc + 1))
                def t_ws(te, cc=cc, w=sve.v):
                    te.wait_ge(sve.sem, w)
                    for kt in range(2):
                        te.matmul(PB0[:, 0:256], wT_st[:, cc, 128 * kt:128 * (kt + 1)],
                                  S_st[:, kt, :], start=(kt == 0), stop=(kt == 1))
                    te.fin(spe.sem, 1)
                em("tensor", t_ws)
                pv = spe.inc()
                ua_v = vseq([lambda ve, cc=cc: ve.tensor_sub(
                    uadj[:], u_st[:, cc, :], PB0[:, 0:256])], waits=[(spe, pv)])

                def t_o(te, cc=cc, C=C, w=ua_v):
                    te.wait_ge(sve.sem, w)
                    for kt in range(2):
                        te.matmul(PB0[:, 256:512], qfm[:, kt, C], S_st[:, kt, :],
                                  start=(kt == 0), stop=False)
                    te.matmul(PB0[:, 256:512], attnT[:, cc, :], uadj[:],
                              start=False, stop=True)
                    for kt in range(2):
                        te.matmul(PB0[:, 512 + 256 * kt:768 + 256 * kt],
                                  ktok[:, cc, 128 * kt:128 * (kt + 1)], uadj[:],
                                  start=True, stop=True)
                    te.fin(spe.sem, 1)
                em("tensor", t_o)
                pv = spe.inc()
                vseq([lambda ve, cc=cc: ve.tensor_scalar(
                        dout_t[:, cc, :], PB0[:, 256:512], rnqcol[:, cc:cc + 1],
                        None, ALU.mult),
                      lambda ve: ve.tensor_add(S_st[:, 0, :], PB0[:, 512:768], S_st[:, 0, :]),
                      lambda ve: ve.tensor_add(S_st[:, 1, :], PB0[:, 768:1024], S_st[:, 1, :])],
                     waits=[(spe, pv)])
            SCAN_V = sve.v

            FSPLIT = 63
            for half in range(2):
                src = vfm[:, half, :]
                ls_ops = [
                    lambda ve, half=half: ve.tensor_scalar(
                        firslab[:, 0:TILE], vfm[:, half, 62:62 + TILE],
                        firwt[:, 66 * half + 0:66 * half + 1], None, ALU.mult),
                    lambda ve, half=half: ve.scalar_tensor_tensor(
                        firslab[:, 0:TILE], vfm[:, half, 63:63 + TILE],
                        firwt[:, 66 * half + 1:66 * half + 2], firslab[:, 0:TILE],
                        ALU.mult, ALU.add),
                    lambda ve, half=half: ve.scalar_tensor_tensor(
                        firslab[:, 0:TILE], vfm[:, half, 64:64 + TILE],
                        firwt[:, 66 * half + 2:66 * half + 3], firslab[:, 0:TILE],
                        ALU.mult, ALU.add),
                ]
                lsv = vseq(ls_ops, waits=[(sve, sve.v), (ssc, ssc.v)])
                for cc2 in range(0, NCH, 4):
                    def t_f(te, cc2=cc2, half=half, w=lsv, vv=sve.v):
                        te.wait_ge(sve.sem, max(w, vv))
                        for j in range(4):
                            te.transpose(PB2b[:, 128 * j:128 * (j + 1)],
                                         firslab[:, 128 * (cc2 + j):128 * (cc2 + j + 1)],
                                         ident[:])
                        te.fin(spe.sem, 1)
                    em("tensor", t_f)
                    pv = spe.inc()
                    vseq([lambda ve, cc2=cc2, half=half: ve.tensor_copy(
                        ls_t[:, cc2:cc2 + 4, 128 * half:128 * (half + 1)]
                        .rearrange("p a b -> p (a b)" if False else "p a b -> p a b"),
                        PB2b[:, 0:512].rearrange("p (a b) -> p a b", a=4))],
                        waits=[(spe, pv)])
                ll_ops_v = []
                for j in range(FSPLIT):
                    if j == 0:
                        ll_ops_v.append(lambda ve, half=half: ve.tensor_scalar(
                            firslab[:, 0:TILE], vfm[:, half, 2:2 + TILE],
                            firwt[:, 66 * half + 3:66 * half + 4], None, ALU.mult))
                    else:
                        ll_ops_v.append(lambda ve, half=half, j=j: ve.scalar_tensor_tensor(
                            firslab[:, 0:TILE], vfm[:, half, 2 + j:2 + j + TILE],
                            firwt[:, 66 * half + 3 + j:66 * half + 4 + j],
                            firslab[:, 0:TILE], ALU.mult, ALU.add))
                llv = vseq(ll_ops_v, waits=[(sve, sve.v)])
                gbase = sgp.v
                def g_f(gp, half=half, w=llv, gbase=gbase):
                    if FSPLIT >= 63:
                        return
                    gp.wait_ge(sve.sem, w)
                    k = 0
                    for j in range(FSPLIT, 63):
                        if j == FSPLIT:
                            ins = gp.tensor_scalar(
                                sqscr[:, 0:TILE], vfm[:, half, 2 + j:2 + j + TILE],
                                firwt[:, 66 * half + 3 + j:66 * half + 4 + j],
                                None, ALU.mult)
                        else:
                            ins = gp.scalar_tensor_tensor(
                                sqscr[:, 0:TILE], vfm[:, half, 2 + j:2 + j + TILE],
                                firwt[:, 66 * half + 3 + j:66 * half + 4 + j],
                                sqscr[:, 0:TILE], ALU.mult, ALU.add)
                        k += 1
                        ins.then_inc(sgp.sem, 1)
                        gp.wait_ge(sgp.sem, gbase + k)
                em("gpsimd", g_f)
                gv2 = sgp.inc(63 - FSPLIT)
                if FSPLIT < 63:
                    llv2 = vseq([lambda ve: ve.tensor_add(firslab[:, 0:TILE],
                                                          firslab[:, 0:TILE], sqscr[:, 0:TILE])],
                                waits=[(sgp, gv2)])
                else:
                    llv2 = llv
                for cc2 in range(0, NCH, 4):
                    def t_f2(te, cc2=cc2, half=half, w=llv2, vv=sve.v):
                        te.wait_ge(sve.sem, max(w, vv))
                        for j in range(4):
                            te.transpose(PB2b[:, 128 * j:128 * (j + 1)],
                                         firslab[:, 128 * (cc2 + j):128 * (cc2 + j + 1)],
                                         ident[:])
                        te.fin(spe.sem, 1)
                    em("tensor", t_f2)
                    pv = spe.inc()
                    vseq([lambda ve, cc2=cc2, half=half: ve.tensor_copy(
                        ll_t[:, cc2:cc2 + 4, 128 * half:128 * (half + 1)],
                        PB2b[:, 0:512].rearrange("p (a b) -> p a b", a=4))],
                        waits=[(spe, pv)])
            fh_v = vseq([lambda ve: ve.tensor_copy(vfm[:, 0, 0:64], vfm[:, 0, TILE:TILE + 64]),
                         lambda ve: ve.tensor_copy(vfm[:, 1, 0:64], vfm[:, 1, TILE:TILE + 64])],
                        waits=[(sve, sve.v)])
            FIR_V = sve.v

            sq2 = hala[:, 14560:16608]
            st_ops = []
            for bi, br in enumerate([ls_t, ll_t, dout_t, vtok]):
                br3 = br
                sv_sq = sseq([lambda s, br3=br3: s.activation(
                    sq2.rearrange("p (a b) -> p a b", a=NCH), br3[:], AF.Square)],
                    waits=[(sve, sve.v), (ssc, ssc.v)])
                stv = vseq([
                    lambda ve, br3=br3: ve.reduce_sum(red_a[:, 0:NCH], br3[:], AX.X),
                    lambda ve, br3=br3: ve.reduce_sum(red_c[:, 0:NCH], br3[:], AX.X,
                                                      apply_absolute_value=True),
                    lambda ve: ve.reduce_sum(red_b[:, 0:NCH],
                                             sq2.rearrange("p (a b) -> p a b", a=NCH), AX.X),
                ], waits=[(ssc, sv_sq), (sve, sve.v)])
                s_as = vseq([
                    lambda ve, bi=bi: ve.tensor_scalar_mul(
                        stats_c[:, :, 4 * bi + 0], red_a[:, 0:NCH], 1.0 / 256),
                    lambda ve, bi=bi: ve.tensor_scalar_mul(
                        stats_c[:, :, 4 * bi + 2], red_c[:, 0:NCH], 1.0 / 256),
                    lambda ve, bi=bi: ve.scalar_tensor_tensor(
                        stats_c[:, :, 4 * bi + 1], stats_c[:, :, 4 * bi + 0], -1.0,
                        stats_c[:, :, 4 * bi + 0], ALU.mult, ALU.mult),
                    lambda ve, bi=bi: ve.scalar_tensor_tensor(
                        stats_c[:, :, 4 * bi + 1], red_b[:, 0:NCH], 1.0 / 256,
                        stats_c[:, :, 4 * bi + 1], ALU.mult, ALU.add),
                ], waits=[(sve, stv)])
                sq_l2 = sseq([lambda s, bi=bi: s.activation(
                    stats_c[:, :, 4 * bi + 3], red_b[:, 0:NCH], AF.Sqrt)],
                    waits=[(sve, s_as)])
                st_ops.append(sq_l2)
            STATS_V = ssc.v
            for cc in range(NCH):
                cst_v = vseq([lambda ve, cc=cc: ve.tensor_copy(
                    uadj[0:128, 0:16], stats_c[:, cc, :])],
                    waits=[(ssc, STATS_V), (sve, sve.v)])
                def t_st(te, cc=cc, w=cst_v):
                    te.wait_ge(sve.sem, w)
                    te.transpose(PB2b[0:16, 0:128], uadj[0:128, 0:16], ident[:])
                    te.fin(spe.sem, 1)
                em("tensor", t_st)
                pv = spe.inc()
                vseq([lambda ve, cc=cc: ve.tensor_copy(
                    statsfm[0:16, 128 * cc:128 * (cc + 1)], PB2b[0:16, 0:128])],
                    waits=[(spe, pv)])
            STF_V = sve.v

            ghid = sqscr
            prev_ev = [None, None]
            for mi in range(8):
                m = 8 + mi
                pb = PB0 if mi % 2 == 0 else PB1
                def t_g(te, m=m, pb=pb, w=STF_V, we=prev_ev[mi % 2], xv=XFM_V):
                    te.wait_ge(sve.sem, max(w, xv))
                    if we is not None:
                        te.wait_ge(spe.sem, we)
                    for kt in range(8):
                        for sub in range(2):
                            te.matmul(pb[:, 512 * sub:512 * (sub + 1)],
                                      wstk[:, kt, 128 * m:128 * (m + 1)],
                                      xfm[:, kt, 512 * sub:512 * (sub + 1)],
                                      start=(kt == 0), stop=False)
                    for sub in range(2):
                        te.matmul(pb[:, 512 * sub:512 * (sub + 1)],
                                  gw1st[:, 128 * (m - 8):128 * (m - 7)],
                                  statsfm[:, 512 * sub:512 * (sub + 1)],
                                  start=False, stop=True)
                    te.fin(spe.sem, 1)
                em("tensor", t_g)
                pv = spe.inc()
                gel_v = sseq([lambda s, pb=pb, m=m: s.activation(
                    ghid[:, 0:TILE], pb[:, 0:TILE], AF.Gelu, bias=svec[:, m:m + 1])],
                    waits=[(spe, pv)])
                def t_l(te, mi=mi, w=gel_v):
                    te.wait_ge(ssc.sem, w)
                    for sub in range(2):
                        te.matmul(PB2[0:4, 512 * sub:512 * (sub + 1)],
                                  gw2t[:, 4 * mi:4 * (mi + 1)], ghid[:, 512 * sub:512 * (sub + 1)],
                                  start=(mi == 0), stop=(mi == 7))
                    te.fin(spe.sem, 1)
                em("tensor", t_l)
                prev_ev[mi % 2] = spe.inc()
            LOG_PE = spe.v
            lg_v = vseq([lambda ve: ve.tensor_scalar(
                scr1[0:4, 0:TILE], PB2[0:4, 0:TILE], gb2col[0:4, 0:1],
                tauinvcol[0:4, 0:1], ALU.add, ALU.mult)], waits=[(spe, LOG_PE)])
            def t_lt(te, w=lg_v):
                te.wait_ge(sve.sem, w)
                for cc in range(NCH):
                    te.transpose(PB2b[:, 4 * cc:4 * (cc + 1)],
                                 scr1[0:4, 128 * cc:128 * (cc + 1)], ident[0:4, 0:4])
                te.fin(spe.sem, 1)
            em("tensor", t_lt)
            pv = spe.inc()
            pl_v = vseq([lambda ve: ve.tensor_copy(
                probst[:].rearrange("p a b -> p (a b)"), PB2b[:, 0:32])],
                waits=[(spe, pv)])
            p3 = probst[:]
            sm1_v = vseq([lambda ve: ve.reduce_max(red_a[:, 0:NCH], p3, AX.X)],
                         waits=[(sve, pl_v)])
            sm2_v = vseq([lambda ve: ve.tensor_sub(
                p3, p3, red_a[:, 0:NCH].unsqueeze(2).broadcast_to([128, NCH, 4]))],
                waits=[(sve, sm1_v)])
            sm3_v = sseq([lambda s: s.activation(
                probst[:].rearrange("p a b -> p (a b)"),
                probst[:].rearrange("p a b -> p (a b)"), AF.Exp)], waits=[(sve, sm2_v)])
            sm4_v = vseq([lambda ve: ve.reduce_sum(red_a[:, 0:NCH], p3, AX.X),
                          lambda ve: ve.reciprocal(red_a[:, 0:NCH], red_a[:, 0:NCH]),
                          lambda ve: ve.tensor_mul(
                              p3, p3, red_a[:, 0:NCH].unsqueeze(2).broadcast_to([128, NCH, 4])),
                          lambda ve: ve.tensor_scalar(
                              p3, p3, gate_c[:, 8:9], None, ALU.mult),
                          lambda ve: ve.tensor_add(
                              p3, p3, gate_c[:, 4:8].unsqueeze(1).broadcast_to([128, NCH, 4])),
                          lambda ve: ve.tensor_scalar(
                              probst[:, :, 0:1], probst[:, :, 0:1], gate_c[:, 9:10],
                              None, ALU.add)],
                         waits=[(ssc, sm3_v)])
            PROBS_V = sm4_v

            prev_rms = ssc.v
            for cc in range(NCH):
                C = slice(128 * cc, 128 * (cc + 1))
                def t_id(te, cc=cc, C=C, w=PROBS_V, vv=sve.v, sw=prev_rms):
                    te.wait_ge(sve.sem, max(w, vv))
                    te.wait_ge(ssc.sem, sw)
                    te.transpose(PB2b[:, 0:128], idfm[:, 0, C], ident[:])
                    te.transpose(PB2b[:, 128:256], idfm[:, 1, C], ident[:])
                    te.fin(spe.sem, 1)
                em("tensor", t_id)
                pv = spe.inc()
                mx_v = vseq([
                    lambda ve, cc=cc: ve.tensor_copy(id_t[:, 0:256], PB2b[:, 0:256]),
                    lambda ve, cc=cc: ve.tensor_scalar(
                        uadj[:, 0:256], ls_t[:, cc, :], probst[:, cc, 0:1], None, ALU.mult),
                    lambda ve, cc=cc: ve.scalar_tensor_tensor(
                        uadj[:, 0:256], ll_t[:, cc, :], probst[:, cc, 1:2],
                        uadj[:, 0:256], ALU.mult, ALU.add),
                    lambda ve, cc=cc: ve.scalar_tensor_tensor(
                        uadj[:, 0:256], dout_t[:, cc, :], probst[:, cc, 2:3],
                        uadj[:, 0:256], ALU.mult, ALU.add),
                    lambda ve, cc=cc: ve.scalar_tensor_tensor(
                        uadj[:, 0:256], vtok[:, cc, :], probst[:, cc, 3:4],
                        uadj[:, 0:256], ALU.mult, ALU.add),
                    lambda ve, cc=cc: ve.scalar_tensor_tensor(
                        ls_t[:, cc, :], id_t[:, 0:256], gate_c[:, 10:11],
                        uadj[:, 0:256], ALU.mult, ALU.add),
                ], waits=[(spe, pv)])
                rms_v = sseq([lambda s, cc=cc: s.activation(
                    uadj[:, 0:256], ls_t[:, cc, :], AF.Square,
                    accum_out=mixc[:, cc:cc + 1])], waits=[(sve, mx_v)])
                prev_rms = rms_v
            RMSS_V = ssc.v
            rmsc_v = sseq([lambda s: s.activation(
                mixc[:, 0:NCH], mixc[:, 0:NCH], AF.Sqrt, bias=epst[:, 0:1],
                scale=1.0 / 256)], waits=[(ssc, RMSS_V), (sve, sve.v)])
            rmsr_v = vseq([lambda ve: ve.reciprocal(mixc[:, 0:NCH], mixc[:, 0:NCH])],
                          waits=[(ssc, rmsc_v)])
            for cc in range(NCH):
                o_v = vseq([
                    lambda ve, cc=cc: ve.tensor_scalar(
                        ls_t[:, cc, :], ls_t[:, cc, :], mixc[:, cc:cc + 1], None, ALU.mult),
                    lambda ve, cc=cc: ve.tensor_mul(ls_t[:, cc, :], ls_t[:, cc, :], onw_t[:]),
                ], waits=[(sve, rmsr_v if cc == 0 else sve.v)])
                def t_of(te, cc=cc, w=sve.v):
                    te.wait_ge(sve.sem, w)
                    te.transpose(PB2b[:, 0:128], ls_t[:, cc, 0:128], ident[:])
                    te.transpose(PB2b[:, 128:256], ls_t[:, cc, 128:256], ident[:])
                    te.fin(spe.sem, 1)
                em("tensor", t_of)
                pv = spe.inc()
                vseq([lambda ve, cc=cc: ve.tensor_copy(ofm[:, 0, 128 * cc:128 * (cc + 1)],
                                                       PB2b[:, 0:128]),
                      lambda ve, cc=cc: ve.tensor_copy(ofm[:, 1, 128 * cc:128 * (cc + 1)],
                                                       PB2b[:, 128:256])],
                     waits=[(spe, pv)])
            OFM_V = sve.v
            def sy_o(e, t_i=t_i, w=OFM_V, pw=sst.v):
                e.wait_ge(sve.sem, w)
                e.wait_ge(sst.sem, pw)
                e.dma_start(out=DT["osrc"][:, TILE * t_i:TILE * (t_i + 1)],
                            in_=ofm[:, 0, :]).then_inc(sst.sem, 16)
                e.dma_start(out=DT["osrc"][:, L + TILE * t_i:L + TILE * (t_i + 1)],
                            in_=ofm[:, 1, :]).then_inc(sst.sem, 16)
            em("sync", sy_o)
            sst.inc(32)
            return dict(ld_v=ld_v, XFM_V=XFM_V, PROJ_V=PROJ_V, CONV_V=CONV_V,
                        bcol_v=bcol_v, rq_v=rq_v, kh_v=kh_v, halo_v=halo_v,
                        OFM_V=OFM_V)

        if KSTOP >= 3:
            vseq([lambda ve: ve.memset(qraw[:, 0, 0:8], 0.0),
                  lambda ve: ve.memset(qraw[:, 1, 0:8], 0.0),
                  lambda ve: ve.memset(kraw[:, 0, 0:8], 0.0),
                  lambda ve: ve.memset(kraw[:, 1, 0:8], 0.0),
                  lambda ve: ve.memset(vraw[:, 0, 0:8], 0.0),
                  lambda ve: ve.memset(vraw[:, 1, 0:8], 0.0),
                  lambda ve: ve.memset(vfm[:, 0, 0:64], 0.0),
                  lambda ve: ve.memset(vfm[:, 1, 0:64], 0.0),
                  lambda ve: ve.memset(S_st[:].rearrange("p a b -> p (a b)"), 0.0)])

        if KSTOP >= 3:
            for _t in range(NTILE):
                tileA(_t)
            dbg_store("dout", dout_t, waits=[(sve, sve.v), (ssc, ssc.v)])
            dbg_store("stats", stats_c, waits=[(ssc, ssc.v), (sve, sve.v)])
            dbg_store("probs", probst[:], waits=[(sve, sve.v)])
            dbg_store("ofm", ofm, waits=[(sve, sve.v)])
            if "osrcL" in DEBUG:
                def sy_oL(e, w=sst.v):
                    e.wait_ge(sst.sem, w)
                    e.dma_start(out=DT["dbg_osrcL"], in_=DT["osrc"][:, :]).then_inc(sst.sem, 16)
                em("sync", sy_oL)
                sst.inc(16)
            dbg_store("qfm", qfm, waits=[(ssc, ssc.v), (sve, sve.v)])
            dbg_store("kfm", kfm, waits=[(sve, sve.v)])
            dbg_store("vfm", vfm[:, :, 64:64 + TILE], waits=[(ssc, ssc.v)])
            dbg_store("idfm", idfm, waits=[(ssc, ssc.v)])
            dbg_store("beta", betarow[:], waits=[(ssc, ssc.v)])
            dbg_store("rnq", rnqcol[:, 0:NCH], waits=[(sve, sve.v)])
            dbg_store("rnk", rnkrow[:], waits=[(sve, sve.v)])


        if KSTOP >= 4:
            NOCC = False
            def g_cc(gp, w=sst.v, NOCC=NOCC):
                gp.wait_ge(sst.sem, w)
                if NOCC:
                    for j in range(4):
                        gp.dma_start(out=DT["odst"][j], in_=DT["osrc"][:, :]).then_inc(scc.sem, 16)
                else:
                    gp.collective_compute(
                        "AllGather", ALU.bypass, ins=[DT["osrc"]], outs=[DT["odst"]],
                        replica_groups=[[0, 1, 2, 3], [4, 5, 6, 7]],
                    ).then_inc(scc.sem, 1)
            em("gpsimd", g_cc)
            CC_V = scc.inc(64 if NOCC else 1)

            wo_b = wqbuf[:, 0:8192].rearrange("p (a b) -> p a b", a=8)
            def g_wo(gp, w=sve.v):
                gp.wait_ge(sve.sem, w)
                for kt in range(8):
                    gp.dma_start(out=wo_b[:, kt, :],
                                 in_=DT["wo"][128 * kt:128 * (kt + 1), :]).then_inc(sdm.sem, 16)
            em("gpsimd", g_wo)
            WO_L = sdm.inc(16 * 8)

            def g_oall(gp, w=CC_V):
                gp.wait_ge(scc.sem, w)
                qreg = gp.value_load(cidt[0:1, 0:1])
                r_off = (qreg % 4) * NTOK
                for j in range(4):
                    for d2 in range(2):
                        gp.dma_start(
                            out=oall[:, 2 * j + d2, :],
                            in_=DT["odst"][j][:, d2 * L:(d2 + 1) * L]
                            [:, bass.ds(r_off, NTOK)]
                        ).then_inc(sdm.sem, 16)
            em("gpsimd", g_oall)
            OALL_L = sdm.inc(16 * 8)

            x2f = x2c.bitcast(F32).rearrange("p a b -> p (a b)")
            h2c = big3[:, 4608:5632]
            sc_b = big3[0:8, 5632:6144]
            hsf = big3[:, 6144:8192].bitcast(F32)
            wg_b = big3[:, 8192:8256].rearrange("p (a b) -> p a b", a=8)
            wgc_v = vseq([lambda ve: ve.tensor_copy(big3[:, 8192:8256], wgt[:])],
                waits=[(sve, sve.v)])
            prev_h2dma = ssy.v
            for n in range(NCHB):
                def t_wo(te, n=n, w=OALL_L, w2=WO_L, vv=sve.v):
                    te.wait_ge(sdm.sem, max(w, w2))
                    te.wait_ge(sve.sem, vv)
                    for kt in range(8):
                        for sub in range(2):
                            te.matmul(PB0[:, 512 * sub:512 * (sub + 1)],
                                      oall[:, kt, 128 * n:128 * (n + 1)],
                                      wo_b[:, kt, 512 * sub:512 * (sub + 1)],
                                      start=(kt == 0), stop=(kt == 7))
                    te.fin(spe.sem, 1)
                em("tensor", t_wo)
                pv = spe.inc()
                def g_hs(gp, n=n, w=sve.v, pd=sdm.v):
                    gp.wait_ge(sve.sem, w)
                    gp.wait_ge(sdm.sem, pd)
                    gp.dma_start(out=hsf[:, 0:1024],
                                 in_=DT["hs_slice"][128 * n:128 * (n + 1), :]).then_inc(sdm.sem, 16)
                em("gpsimd", g_hs)
                hs_l = sdm.inc(16)
                x2_v = vseq([
                    lambda ve: ve.tensor_mul(x2f[:, 0:1024], PB0[:, 0:1024], ga_t[:]),
                    lambda ve: ve.tensor_add(x2f[:, 0:1024], x2f[:, 0:1024], hsf[:, 0:1024]),
                    lambda ve: ve.reduce_sum(red_a[:, 0:1], x2f[:, 0:1024], AX.X),
                ], waits=[(spe, pv), (sdm, hs_l)])
                def sy_x2(e, n=n, w=x2_v, pw=prev_h2dma):
                    e.wait_ge(sve.sem, w)
                    e.wait_ge(ssy.sem, pw)
                    e.dma_start(out=DT["x2buf"][128 * n:128 * (n + 1), :],
                                in_=x2f[:, 0:1024]).then_inc(ssy.sem, 16)
                em("sync", sy_x2)
                x2dma = ssy.inc(16)
                sq_v2 = sseq([lambda s: s.activation(h2c[:, 0:1024], x2f[:, 0:1024],
                                                     AF.Square, accum_out=red_b[:, 0:1])],
                             waits=[(sve, x2_v), (ssy, prev_h2dma)])
                ln_v2 = vseq([
                    lambda ve: ve.tensor_scalar_mul(red_a[:, 0:1], red_a[:, 0:1], 1.0 / H),
                    lambda ve: ve.tensor_scalar_mul(red_b[:, 0:1], red_b[:, 0:1], 1.0 / H),
                    lambda ve: ve.tensor_mul(red_c[:, 0:1], red_a[:, 0:1], red_a[:, 0:1]),
                    lambda ve: ve.tensor_sub(red_b[:, 0:1], red_b[:, 0:1], red_c[:, 0:1]),
                ], waits=[(ssc, sq_v2)])
                rs_v2 = sseq([lambda s: s.activation(red_b[:, 0:1], red_b[:, 0:1],
                                                     AF.Sqrt, bias=epst[:, 0:1])],
                             waits=[(sve, ln_v2)])
                h2_v = vseq([
                    lambda ve: ve.reciprocal(red_b[:, 0:1], red_b[:, 0:1]),
                    lambda ve: ve.tensor_scalar(x2f[:, 0:1024], x2f[:, 0:1024],
                                                red_a[:, 0:1], red_b[:, 0:1],
                                                ALU.subtract, ALU.mult),
                    lambda ve: ve.tensor_mul(h2c[:, 0:1024], x2f[:, 0:1024], cb1_t[:]),
                    lambda ve: ve.tensor_add(h2c[:, 0:1024], h2c[:, 0:1024], sbt_t[:]),
                ], waits=[(ssc, rs_v2), (ssy, x2dma)])
                def sy_h2(e, n=n, w=sve.v, pw=ssy.v):
                    e.wait_ge(sve.sem, w)
                    e.wait_ge(ssy.sem, pw)
                    e.dma_start(out=DT["h2buf"][128 * n:128 * (n + 1), :],
                                in_=h2c[:, 0:1024]).then_inc(ssy.sem, 16)
                em("sync", sy_h2)
                prev_h2dma = ssy.inc(16)
                def t_h2(te, n=n, w=sve.v, wg_w=wgc_v):
                    te.wait_ge(sve.sem, max(w, wg_w))
                    for kt in range(8):
                        te.transpose(PB2b[:, 128 * kt:128 * (kt + 1)],
                                     h2c[:, 128 * kt:128 * (kt + 1)], ident[:])
                    te.fin(spe.sem, 1)
                em("tensor", t_h2)
                pv2 = spe.inc()
                h2f_v = vseq([lambda ve: ve.tensor_copy(big3[:, 8256:9280], PB2b[:, 0:1024])],
                             waits=[(spe, pv2)])
                h2fm = big3[:, 8256:9280].rearrange("p (a b) -> p a b", a=8)
                def t_sc(te, n=n, w=h2f_v):
                    te.wait_ge(sve.sem, w)
                    for kt in range(8):
                        te.matmul(PB3[0:8, 128 * (n % 4):128 * (n % 4 + 1)],
                                  wg_b[:, kt, :], h2fm[:, kt, :],
                                  start=(kt == 0), stop=(kt == 7))
                    te.fin(spe.sem, 1)
                em("tensor", t_sc)
                pv3 = spe.inc()
                if n % 4 == 3:
                    se_v = vseq([lambda ve: ve.tensor_copy(sc_b, PB3[0:8, 0:512])],
                                waits=[(spe, pv3)])
                    def t_sct(te, n=n, w=se_v):
                        te.wait_ge(sve.sem, w)
                        for j in range(4):
                            te.transpose(PB2b[:, 8 * j:8 * (j + 1)],
                                         sc_b[:, 128 * j:128 * (j + 1)], ident[0:8, 0:8])
                        te.fin(spe.sem, 1)
                    em("tensor", t_sct)
                    pv4 = spe.inc()
                    vseq([lambda ve, n=n: ve.tensor_copy(
                        scrt[:, n - 3:n + 1, :].rearrange("p a b -> p (a b)"),
                        PB2b[:, 0:32])], waits=[(spe, pv4)])
            X2_done = sve.v
            dbg_store("x2", x2f[:, 0:1024], waits=[(sve, sve.v)])
            dbg_store("scores", scrt[:], waits=[(sve, sve.v)])

        if KSTOP >= 5:
            PE_B1 = spe.v
            rt = big3[:, 0:8192].bitcast(F32)

            def rtv(off, shape):
                n = int(np.prod(shape))
                return rt[:, off:off + n].rearrange("p (a b) -> p a b", a=shape[0])
            ohA = rtv(0, (NCHB, 8))
            ohB = rtv(128, (NCHB, 8))
            Rt3 = rtv(256, (NCHB, 8))
            stl = rtv(384, (NCHB, 8))
            iotaBC = iotaEt[:].unsqueeze(1).broadcast_to([128, NCHB, 8])
            mo_s = big3[:, 4608:5632]

            z_v = vseq([lambda ve: ve.memset(zerot, 0.0),
                        lambda ve: ve.memset(idx16[:].rearrange("p a b -> p (a b)"), 0)])

            def g_zero(gp, w=z_v, pd=sdm.v):
                gp.wait_ge(sve.sem, w)
                gp.wait_ge(sdm.sem, pd)
                bkf = DT["bucket"].rearrange("a b -> (a b)").rearrange(
                    "(p f) -> p f", p=128)
                for j in range(10):
                    gp.dma_start(out=bkf[:, 4096 * j:4096 * (j + 1)],
                                 in_=zerot).then_inc(sdm.sem, 16)
            em("gpsimd", g_zero)
            ZERO_L = sdm.inc(160)

            va = vseq([
                lambda ve: ve.reduce_max(red_a[:, 0:NCHB], scrt[:], AX.X),
                lambda ve: ve.tensor_tensor(
                    ohA, scrt[:],
                    red_a[:, 0:NCHB].unsqueeze(2).broadcast_to([128, NCHB, 8]),
                    ALU.is_equal),
                lambda ve: ve.tensor_mul(stl, ohA, iotaBC),
                lambda ve: ve.reduce_sum(red_b[:, 0:NCHB], stl, AX.X),
                lambda ve: ve.tensor_copy(routt[:, :, 0:1],
                                          red_b[:, 0:NCHB].unsqueeze(2)),
                lambda ve: ve.tensor_scalar_mul(stl, ohA, -1e9),
                lambda ve: ve.tensor_add(scrt[:], scrt[:], stl),
                lambda ve: ve.reduce_max(red_c[:, 0:NCHB], scrt[:], AX.X),
                lambda ve: ve.tensor_tensor(
                    ohB, scrt[:],
                    red_c[:, 0:NCHB].unsqueeze(2).broadcast_to([128, NCHB, 8]),
                    ALU.is_equal),
                lambda ve: ve.tensor_mul(stl, ohB, iotaBC),
                lambda ve: ve.reduce_sum(red_b[:, 0:NCHB], stl, AX.X),
                lambda ve: ve.tensor_copy(routt[:, :, 1:2],
                                          red_b[:, 0:NCHB].unsqueeze(2)),
                lambda ve: ve.tensor_sub(red_b[:, 0:NCHB], red_a[:, 0:NCHB],
                                         red_c[:, 0:NCHB]),
            ], waits=[(sve, sve.v), (sst, sst.v)])
            sg = sseq([lambda s: s.activation(red_a[:, 0:NCHB], red_b[:, 0:NCHB],
                                              AF.Sigmoid)], waits=[(sve, va)])
            vb = vseq([
                lambda ve: ve.tensor_copy(routt[:, :, 4:5],
                                          red_a[:, 0:NCHB].unsqueeze(2)),
                lambda ve: ve.tensor_scalar_mul(red_b[:, 0:NCHB],
                                                red_b[:, 0:NCHB], -1.0),
            ], waits=[(ssc, sg)])
            sg2 = sseq([lambda s: s.activation(red_a[:, 0:NCHB], red_b[:, 0:NCHB],
                                               AF.Sigmoid)], waits=[(sve, vb)])
            vg2 = vseq([lambda ve: ve.tensor_copy(routt[:, :, 5:6],
                                                  red_a[:, 0:NCHB].unsqueeze(2))],
                       waits=[(ssc, sg2)])

            def t_cs(te, w=vg2):
                te.wait_ge(sve.sem, w)
                te.matmul(PB3[:, 0:128], tricum[:], rt[:, 0:128], start=True, stop=True)
                te.matmul(PB3[:, 128:256], tricum[:], rt[:, 128:256], start=True, stop=True)
                te.matmul(PB3[0:1, 256:384], onescol[:], rt[:, 0:128], start=True, stop=True)
                te.matmul(PB3[0:1, 384:512], onescol[:], rt[:, 128:256], start=True, stop=True)
                te.fin(spe.sem, 1)
            em("tensor", t_cs)
            pcs = spe.inc()
            ev1 = vseq([
                lambda ve: ve.tensor_copy(csum1[:].rearrange("p a b -> p (a b)"),
                                          PB3[:, 0:128]),
                lambda ve: ve.tensor_copy(csum2[:].rearrange("p a b -> p (a b)"),
                                          PB3[:, 128:256]),
                lambda ve: ve.tensor_copy(rowscr8[0:1, 0:128], PB3[0:1, 256:384]),
                lambda ve: ve.tensor_copy(rowscr8[0:1, 128:256], PB3[0:1, 384:512]),
            ], waits=[(spe, pcs)])

            def sy_tot(e, w=ev1, pw=ssy.v):
                e.wait_ge(sve.sem, w)
                e.wait_ge(ssy.sem, pw)
                e.dma_start(out=DT["rsc"], in_=rowscr8[0:1, 0:256]).then_inc(ssy.sem, 16)
            em("sync", sy_tot)
            t1 = ssy.inc(16)

            def sy_tot2(e, w=t1):
                e.wait_ge(ssy.sem, w)
                e.dma_start(out=totE[0:16, 0:8],
                            in_=DT["rsc"][0:1, 0:128].rearrange(
                                "o (a b) -> (o a) b", a=16)).then_inc(ssy.sem, 16)
            em("sync", sy_tot2)
            t2 = ssy.inc(16)

            def sy_tot3(e, w=t2):
                e.wait_ge(ssy.sem, w)
                e.dma_start(out=totE[0:16, 8:16],
                            in_=DT["rsc"][0:1, 128:256].rearrange(
                                "o (a b) -> (o a) b", a=16)).then_inc(ssy.sem, 16)
            em("sync", sy_tot3)
            t3 = ssy.inc(16)

            def t_pf(te, w=t3):
                te.wait_ge(ssy.sem, w)
                te.matmul(PB1[0:16, 0:16], tri16t[:], totE[:], start=True, stop=True)
                te.matmul(PB1[0:1, 16:32], onescol[0:16, 0:1], totE[:],
                          start=True, stop=True)
                te.fin(spe.sem, 1)
            em("tensor", t_pf)
            ppf = spe.inc()
            ev2 = vseq([
                lambda ve: ve.tensor_copy(coffs[:], PB1[0:16, 0:16]),
                lambda ve: ve.tensor_copy(cntrow[:], PB1[0:1, 16:32]),
            ], waits=[(spe, ppf)])

            def sy_cnt(e, w=ev2, pw=ssy.v):
                e.wait_ge(sve.sem, w)
                e.wait_ge(ssy.sem, pw)
                e.dma_start(out=DT["cnt_src"], in_=cntrow[:]).then_inc(ssy.sem, 16)
            em("sync", sy_cnt)
            c1 = ssy.inc(16)
            NOCC = False

            def g_ccnt(gp, w=c1, NOCC=NOCC):
                gp.wait_ge(ssy.sem, w)
                if NOCC:
                    for j in range(8):
                        gp.dma_start(out=DT["cnt_dst"][j:j + 1, :],
                                     in_=DT["cnt_src"]).then_inc(scc.sem, 16)
                else:
                    gp.collective_compute(
                        "AllGather", ALU.bypass, ins=[DT["cnt_src"]],
                        outs=[DT["cnt_dst"]],
                        replica_groups=[[0, 1, 2, 3, 4, 5, 6, 7]],
                    ).then_inc(scc.sem, 1)
            em("gpsimd", g_ccnt)
            CC2 = scc.inc(128 if NOCC else 1)

            def sy_ag(e, w=CC2, pw=sld.v):
                e.wait_ge(scc.sem, w)
                e.wait_ge(sld.sem, pw)
                e.dma_start(out=agbuf[:], in_=DT["cnt_dst"]).then_inc(sld.sem, 16)
            em("sync", sy_ag)
            agl = sld.inc(16)

            def sy_cf(e, w=ev2, pw=t3):
                e.wait_ge(sve.sem, w)
                e.wait_ge(ssy.sem, max(pw, c1))
                e.dma_start(out=DT["rsc2"], in_=coffs[:]).then_inc(ssy.sem, 16)
            em("sync", sy_cf)
            cf1 = ssy.inc(16)

            def sy_cf2(e, w=cf1):
                e.wait_ge(ssy.sem, w)
                e.dma_start(out=rowscr8[0:1, 0:256],
                            in_=DT["rsc2"].rearrange("a b -> (a b)").rearrange(
                                "(o f) -> o f", o=1)).then_inc(ssy.sem, 16)
            em("sync", sy_cf2)
            cf2 = ssy.inc(16)

            def t_pf2(te, w=agl):
                te.wait_ge(sld.sem, w)
                te.matmul(PB1[0:1, 32:48], hotlt_t[:], agbuf[:], start=True, stop=True)
                te.matmul(PB1[0:1, 48:64], onescol[0:8, 0:1], agbuf[:],
                          start=True, stop=True)
                te.fin(spe.sem, 1)
            em("tensor", t_pf2)
            pp2 = spe.inc()
            ev3 = vseq([
                lambda ve: ve.tensor_copy(rowscr8[0:1, 256:272], PB1[0:1, 32:48]),
                lambda ve: ve.tensor_copy(rowscr8[0:1, 272:280], PB1[0:1, 48:56]),
                lambda ve: ve.tensor_copy(rowscr8[0:1, 280:288], cntrow[0:1, 0:8]),
            ], waits=[(spe, pp2), (ssy, cf2)])

            def t_bc(te, w=ev3, wd=cf2):
                te.wait_ge(sve.sem, w)
                te.wait_ge(ssy.sem, wd)
                te.matmul(PB2[:, 0:256], onesrow[:], rowscr8[0:1, 0:256],
                          start=True, stop=True)
                te.matmul(PB2[:, 256:288], onesrow[:], rowscr8[0:1, 256:288],
                          start=True, stop=True)
                te.fin(spe.sem, 1)
            em("tensor", t_bc)
            pbc = spe.inc()
            coT = PB2[:, 0:256].rearrange("p (a b) -> p a b", a=16)

            ev4 = vseq([
                lambda ve: ve.tensor_copy(ptile[:], PB2[:, 256:288]),
                lambda ve: ve.tensor_add(ptile[:, 8:16], ptile[:, 8:16],
                                         ptile[:, 16:24]),
                lambda ve: ve.tensor_add(Rt3, csum1[:], coT[:, :, 0:8]),
                lambda ve: ve.tensor_mul(stl, ohA, Rt3),
                lambda ve: ve.reduce_sum(red_b[:, 0:NCHB], stl, AX.X),
                lambda ve: ve.tensor_scalar(red_c[:, 0:NCHB].unsqueeze(2),
                                            routt[:, :, 0:1], 640.0, -1.0,
                                            ALU.mult, ALU.add),
                lambda ve: ve.tensor_add(slot_f[:, 0, :], red_c[:, 0:NCHB],
                                         red_b[:, 0:NCHB]),
                lambda ve: ve.tensor_add(
                    Rt3, Rt3, ptile[:, 0:8].unsqueeze(1).broadcast_to([128, NCHB, 8])),
                lambda ve: ve.tensor_mul(stl, ohA, Rt3),
                lambda ve: ve.reduce_sum(red_b[:, 0:NCHB], stl, AX.X),
                lambda ve: ve.tensor_copy(routt[:, :, 2:3],
                                          red_b[:, 0:NCHB].unsqueeze(2)),
                lambda ve: ve.tensor_scalar(red_b[:, 0:NCHB], red_b[:, 0:NCHB],
                                            4096.5, None, ALU.is_lt),
                lambda ve: ve.tensor_mul(routt[:, :, 4:5], routt[:, :, 4:5],
                                         red_b[:, 0:NCHB].unsqueeze(2)),
                lambda ve: ve.tensor_add(Rt3, csum2[:], coT[:, :, 8:16]),
                lambda ve: ve.tensor_add(
                    Rt3, Rt3, ptile[:, 24:32].unsqueeze(1).broadcast_to([128, NCHB, 8])),
                lambda ve: ve.tensor_mul(stl, ohB, Rt3),
                lambda ve: ve.reduce_sum(red_b[:, 0:NCHB], stl, AX.X),
                lambda ve: ve.tensor_scalar(red_c[:, 0:NCHB].unsqueeze(2),
                                            routt[:, :, 1:2], 640.0, -1.0,
                                            ALU.mult, ALU.add),
                lambda ve: ve.tensor_add(slot_f[:, 1, :], red_c[:, 0:NCHB],
                                         red_b[:, 0:NCHB]),
                lambda ve: ve.tensor_sub(ptile[:, 16:24], ptile[:, 8:16],
                                         ptile[:, 24:32]),
                lambda ve: ve.tensor_add(
                    Rt3, Rt3, ptile[:, 16:24].unsqueeze(1).broadcast_to([128, NCHB, 8])),
                lambda ve: ve.tensor_mul(stl, ohB, Rt3),
                lambda ve: ve.reduce_sum(red_b[:, 0:NCHB], stl, AX.X),
                lambda ve: ve.tensor_copy(routt[:, :, 3:4],
                                          red_b[:, 0:NCHB].unsqueeze(2)),
                lambda ve: ve.tensor_scalar(red_b[:, 0:NCHB], red_b[:, 0:NCHB],
                                            4096.5, None, ALU.is_lt),
                lambda ve: ve.tensor_mul(routt[:, :, 5:6], routt[:, :, 5:6],
                                         red_b[:, 0:NCHB].unsqueeze(2)),
                lambda ve: ve.tensor_scalar_min(
                    slot_f[:].rearrange("p a b -> p (a b)"),
                    slot_f[:].rearrange("p a b -> p (a b)"), float(E * CAPB - 1)),
                lambda ve: ve.tensor_copy(routt[:, :, 6:7],
                                          slot_f[:, 0, :].unsqueeze(2)),
                lambda ve: ve.tensor_copy(routt[:, :, 7:8],
                                          slot_f[:, 1, :].unsqueeze(2)),
            ], waits=[(spe, pbc)])
            dbg_store("route", routt[:, :, 0:8], waits=[(sve, ev4)])

            def t_sel(te, w=ev4):
                te.wait_ge(sve.sem, w)
                for k in range(2):
                    for a in range(8):
                        te.matmul(PB1[0:16, 128 * k + 16 * a:128 * k + 16 * (a + 1)],
                                  selpermt[:, 16 * a:16 * (a + 1)], slot_f[:, k, :],
                                  start=True, stop=True)
                te.fin(spe.sem, 1)
            em("tensor", t_sel)
            psel = spe.inc()
            idxf = idx16[:].rearrange("p a b -> p (a b)")
            sel_ops = []
            for k in range(2):
                for a in range(8):
                    sel_ops.append(lambda ve, k=k, a=a: ve.tensor_copy(
                        idxf[0:16, 128 * k:128 * (k + 1)].rearrange(
                            "p (n s) -> p n s", s=8)[:, :, a:a + 1],
                        PB1[0:16, 128 * k + 16 * a:128 * k + 16 * (a + 1)].unsqueeze(2)))
            ev5 = vseq(sel_ops, waits=[(spe, psel)])

            def sy_rep(e, w=ev5, pw=ssy.v):
                e.wait_ge(sve.sem, w)
                e.wait_ge(ssy.sem, pw)
                e.dma_start(out=idx16[16:32, :, :],
                            in_=idx16[0:16, :, :]).then_inc(ssy.sem, 16)
            em("sync", sy_rep)
            REP_L = ssy.inc(16)

            if KSTOP >= 6:
                def sy_h2r(e, w=ssy.v, pe=PE_B1):
                    e.wait_ge(ssy.sem, w)
                    e.wait_ge(spe.sem, pe)
                    e.dma_start(out=big1[:, :, :],
                                in_=DT["h2buf"].rearrange("(c p) f -> p c f", p=128)
                                ).then_inc(ssy.sem, 16)
                em("sync", sy_h2r)
                H2R_L = ssy.inc(16)
                dbg_store("h2", big1[:].rearrange("p a b -> p (a b)"),
                          waits=[(ssy, H2R_L)])

                def g_scat(gp, wz=ZERO_L, wrep=REP_L, wh2=H2R_L, pd=sdm.v):
                    gp.wait_ge(sdm.sem, max(wz, pd))
                    gp.wait_ge(ssy.sem, max(wrep, wh2))
                    gp.dma_scatter_add(DT["bucket"], big1[:, :, :], idx16[:, 0, :],
                                       NTOK, NTOK, H).then_inc(sdm.sem, 16)
                em("gpsimd", g_scat)
                SC1 = sdm.inc(16)

                def g_scat2(gp, w=SC1):
                    gp.wait_ge(sdm.sem, w)
                    gp.dma_scatter_add(DT["bucket"], big1[:, :, :], idx16[:, 1, :],
                                       NTOK, NTOK, H).then_inc(sdm.sem, 16)
                em("gpsimd", g_scat2)
                SC2 = sdm.inc(16)

                ysb = b1f[:, 0:10240].bitcast(F32).rearrange("p (a b) -> p a b", a=5)
                idispv = idxdispt[:].rearrange("p (e c) -> p e c", e=E)

                def slab(s_):
                    return wqbuf[:, 8192 * s_:8192 * (s_ + 1)].rearrange(
                        "p (a b) -> p a b", a=8)
                lastfin = {0: PE_B1, 1: PE_B1}
                YEVAC_V = sve.v
                YDMA_L = sdm.v
                gw1done = {}
                pb_act = {0: None, 1: None}
                bank_ev = {0: None, 1: None}
                yacc_prev = {}
                for e_ in range(E):
                    def g_gx(gp, e_=e_, wsc=SC2, wpe=gw1done.get(e_ - 1), pd=sdm.v):
                        gp.wait_ge(sdm.sem, max(wsc, pd))
                        if wpe is not None:
                            gp.wait_ge(spe.sem, wpe)
                        gp.dma_gather(xe_b[:, :, :], DT["bucket"], idispv[:, e_, :],
                                      CAPB, CAPB, H, elem_step=H,
                                      transpose=True).then_inc(sdm.sem, 16)
                    em("gpsimd", g_gx)
                    GX = sdm.inc(16)

                    W1L = {}

                    def em_wload(which, e2, q2):
                        sl_ = q2 % 2

                        def g_w(gp, e2=e2, q2=q2, sl_=sl_, which=which,
                                wfree=lastfin[sl_], pd=sdm.v):
                            gp.wait_ge(spe.sem, wfree)
                            gp.wait_ge(sdm.sem, pd)
                            sb_ = slab(sl_)
                            for kk in range(8):
                                gp.dma_start(
                                    out=sb_[:, kk, :],
                                    in_=DT[which][e2, q2, 128 * kk:128 * (kk + 1), :]
                                ).then_inc(sdm.sem, 16)
                        em("gpsimd", g_w)
                        return sdm.inc(128)

                    W1L[0] = em_wload("w1r", e_, 0)
                    W1L[1] = em_wload("w1r", e_, 1)
                    for q in range(4):
                        sl_ = q % 2
                        for dd in range(8):
                            par = dd % 2
                            pb = PB3 if par == 0 else PB0

                            def t_w1(te, e_=e_, q=q, dd=dd, sl_=sl_, pb=pb,
                                     wl=W1L[q], wg=GX, wev=pb_act[par], wy=YEVAC_V):
                                te.wait_ge(sdm.sem, max(wl, wg))
                                if wev is not None:
                                    te.wait_ge(ssc.sem, wev)
                                te.wait_ge(sve.sem, wy)
                                sb_ = slab(sl_)
                                for kk in range(8):
                                    te.matmul(pb[:, 0:512],
                                              sb_[:, kk, 128 * dd:128 * (dd + 1)],
                                              xe_b[:, kk, 0:512],
                                              start=(kk == 0), stop=(kk == 7))
                                for kk in range(8):
                                    te.matmul(pb[:, 512:640],
                                              sb_[:, kk, 128 * dd:128 * (dd + 1)],
                                              xe_b[:, kk, 512:640],
                                              start=(kk == 0), stop=(kk == 7))
                                te.fin(spe.sem, 1)
                            em("tensor", t_w1)
                            pw1 = spe.inc()
                            act_v = sseq([lambda s, e_=e_, q=q, dd=dd, pb=pb:
                                          s.activation(hid_b[:, 8 * q + dd, :],
                                                       pb[:, 0:640], AF.Relu,
                                                       bias=b1c[:, e_, 8 * q + dd:8 * q + dd + 1])],
                                         waits=[(spe, pw1)])
                            pb_act[par] = act_v
                        lastfin[sl_] = spe.v
                        if q + 2 < 4:
                            W1L[q + 2] = em_wload("w1b", e_, q + 2)
                    gw1done[e_] = spe.v
                    ACT_LAST = ssc.v
                    def sy_b2(e2, e_=e_, pw=sld.v, wv=YEVAC_V):
                        e2.wait_ge(sld.sem, pw)
                        e2.wait_ge(sve.sem, wv)
                        e2.dma_start(out=rowsc[:], in_=DT["b2"][e_:e_ + 1, :]
                                     ).then_inc(sld.sem, 16)
                    em("sync", sy_b2)
                    b2l = sld.inc(16)
                    b2c_v = vseq([lambda ve: ve.tensor_copy(rowscb[:], rowsc[:])],
                                 waits=[(sld, b2l)])

                    def t_b2(te, w=b2c_v, wy=YEVAC_V):
                        te.wait_ge(sve.sem, max(w, wy))
                        te.matmul(PB2[:, 0:512], onesrowb[:], rowscb[0:1, 0:512],
                                  start=True, stop=True)
                        te.matmul(PB2[:, 512:1024], onesrowb[:], rowscb[0:1, 512:1024],
                                  start=True, stop=True)
                        te.fin(spe.sem, 1)
                    em("tensor", t_b2)
                    pb2v = spe.inc()
                    b2e_v = vseq([lambda ve: ve.tensor_copy(b2t[:], PB2[:, 0:1024])],
                                 waits=[(spe, pb2v)])

                    W2L = {0: em_wload("w2b", e_, 0), 1: em_wload("w2b", e_, 1)}
                    for q in range(4):
                        sl_ = q % 2
                        for st in range(5):
                            for sub in range(2):
                                bk = (st * 2 + sub) % 2
                                poff = 512 * bk

                                def t_w2(te, e_=e_, q=q, st=st, sub=sub, sl_=sl_,
                                         poff=poff, wl=W2L[q], whid=ACT_LAST,
                                         wb2=b2e_v, wev=bank_ev[bk], wy=YEVAC_V):
                                    te.wait_ge(sdm.sem, wl)
                                    te.wait_ge(ssc.sem, whid)
                                    te.wait_ge(sve.sem, max(wb2, wev or 0, wy))
                                    sb_ = slab(sl_)
                                    for kq in range(8):
                                        te.matmul(PB0[:, poff:poff + 512],
                                                  hid_b[:, 8 * q + kq, 128 * st:128 * (st + 1)],
                                                  sb_[:, kq, 512 * sub:512 * (sub + 1)],
                                                  start=(kq == 0), stop=(kq == 7))
                                    te.fin(spe.sem, 1)
                                em("tensor", t_w2)
                                pw2 = spe.inc()
                                if q == 0:
                                    bank_ev[bk] = vseq(
                                        [lambda ve, st=st, sub=sub, poff=poff:
                                         ve.tensor_add(ysb[:, st, 512 * sub:512 * (sub + 1)],
                                                       PB0[:, poff:poff + 512],
                                                       b2t[:, 512 * sub:512 * (sub + 1)])],
                                        waits=[(spe, pw2), (sdm, YDMA_L)])
                                else:
                                    bank_ev[bk] = vseq(
                                        [lambda ve, st=st, sub=sub, poff=poff:
                                         ve.tensor_add(ysb[:, st, 512 * sub:512 * (sub + 1)],
                                                       ysb[:, st, 512 * sub:512 * (sub + 1)],
                                                       PB0[:, poff:poff + 512])],
                                        waits=[(spe, pw2), (sve, yacc_prev[(st, sub)])])
                                yacc_prev[(st, sub)] = bank_ev[bk]
                        lastfin[sl_] = spe.v
                        if q + 2 < 4:
                            W2L[q + 2] = em_wload("w2b", e_, q + 2)
                    YEVAC_V = sve.v
                    def g_y(gp, e_=e_, w=YEVAC_V, pd=sdm.v):
                        gp.wait_ge(sve.sem, w)
                        gp.wait_ge(sdm.sem, pd)
                        gp.dma_start(out=DT["ybuf"][e_ * CAPB:(e_ + 1) * CAPB, :]
                                     .rearrange("(c p) f -> p c f", p=128),
                                     in_=ysb[:, :, :]).then_inc(sdm.sem, 16)
                    em("gpsimd", g_y)
                    YDMA_L = sdm.inc(16)

                gbv = plain_bcast(5120, gb_t)
                prev_out = sst.v
                for half in range(2):
                    def g_ret(gp, half=half, wpe=spe.v, wv=sve.v, pd=max(sdm.v, YDMA_L)):
                        gp.wait_ge(spe.sem, wpe)
                        gp.wait_ge(sve.sem, wv)
                        gp.wait_ge(sdm.sem, pd)
                        gp.dma_gather(g0h[:, :, :], DT["ybuf"],
                                      idx16[:, 0, 64 * half:64 * half + 64],
                                      1024, 1024, H, elem_step=H).then_inc(sdm.sem, 16)
                        gp.dma_gather(g1h[:, :, :], DT["ybuf"],
                                      idx16[:, 1, 64 * half:64 * half + 64],
                                      1024, 1024, H, elem_step=H).then_inc(sdm.sem, 16)
                    em("gpsimd", g_ret)
                    RET_L = sdm.inc(32)
                    for nn in range(8):
                        m = 8 * half + nn

                        def sy_x2r(e2, m=m, pw=ssy.v, wv=sve.v):
                            e2.wait_ge(ssy.sem, pw)
                            e2.wait_ge(sve.sem, wv)
                            e2.dma_start(out=hsf[:, 0:1024],
                                         in_=DT["x2buf"][128 * m:128 * (m + 1), :]
                                         ).then_inc(ssy.sem, 16)
                        em("sync", sy_x2r)
                        x2l = ssy.inc(16)
                        cmb = vseq([
                            lambda ve, nn=nn, m=m: ve.tensor_scalar(
                                mo_s, g0h[:, nn, :], routt[:, m, 4:5], None, ALU.mult),
                            lambda ve, nn=nn, m=m: ve.scalar_tensor_tensor(
                                mo_s, g1h[:, nn, :], routt[:, m, 5:6], mo_s,
                                ALU.mult, ALU.add),
                            lambda ve: ve.tensor_mul(outroll[:, 0, :], mo_s, gb_t[:]),
                            lambda ve: ve.tensor_add(outroll[:, 0, :], outroll[:, 0, :],
                                                     hsf[:, 0:1024]),
                        ], waits=[(sdm, RET_L), (ssy, x2l), (sve, gbv), (sst, prev_out)])

                        def sy_out(e2, m=m, w=cmb, pw=sst.v):
                            e2.wait_ge(sve.sem, w)
                            e2.wait_ge(sst.sem, pw)
                            e2.dma_start(out=DT["out"][128 * m:128 * (m + 1), :],
                                         in_=outroll[:, 0, :]).then_inc(sst.sem, 16)
                        em("sync", sy_out)
                        prev_out = sst.inc(16)
                        if "moe" in DEBUG:
                            def sy_dm(e2, m=m, w=cmb, pw=sst.v):
                                e2.wait_ge(sve.sem, w)
                                e2.wait_ge(sst.sem, pw)
                                e2.dma_start(out=DT["dbg_moe"][:, 1024 * m:1024 * (m + 1)],
                                             in_=mo_s).then_inc(sst.sem, 16)
                            em("sync", sy_dm)
                            sst.inc(16)

        def fin(e, w=sst.v, vv=sve.v, w_ssy=ssy.v, w_sdm=sdm.v, w_sld=sld.v,
                w_scc=scc.v, w_gp=sgp.v, w_sc=ssc.v, w_pe=spe.v):
            e.wait_ge(sve.sem, vv)
            e.wait_ge(sst.sem, w)
            e.wait_ge(ssy.sem, w_ssy)
            e.wait_ge(sdm.sem, w_sdm)
            e.wait_ge(sld.sem, w_sld)
            e.wait_ge(scc.sem, w_scc)
            e.wait_ge(sgp.sem, w_gp)
            e.wait_ge(ssc.sem, w_sc)
            e.wait_ge(spe.sem, w_pe)
            e.wait_ge(swa.sem, swa.v)
            e.wait_ge(swb.sem, swb.v)
            if KSTOP < 6:
                for cc in range(NCHB):
                    e.dma_start(out=DT["out"][128 * cc:128 * (cc + 1), :],
                                in_=outroll[:, 0, :]).then_inc(sst.sem, 16)
                e.wait_ge(sst.sem, w + 16 * NCHB)
        em("sync", fin)
        if KSTOP < 6:
            sst.inc(16 * NCHB)

        P.realize(ctx.enter_context(nc.Block()))
    return nc


def kernel(**inputs):
    in_maps = make_in_maps(inputs)
    nc = build()
    mybir.codegen_inst_isa_subclasses(nc)
    res = run_bass_kernel_spmd(nc, in_maps, core_ids=list(range(NCORE)))
    out = np.zeros((B * L, H), np.float32)
    for c in range(NCORE):
        out[NTOK * c:NTOK * (c + 1)] = res.results[c]["out"]
    return out.reshape(B, L, H)


# version 03 (final)
# speedup vs baseline: 1.1429x; 1.1429x over previous
import os
import numpy as np
import ml_dtypes
from contextlib import ExitStack

import concourse.bass as bass
import concourse.mybir as mybir
from concourse.bass_utils import run_bass_kernel_spmd
from concourse.library_config import mlp as _MLP_LIB

F32 = mybir.dt.float32
BF16 = mybir.dt.bfloat16
I16 = mybir.dt.int16
I32 = mybir.dt.int32
AF = mybir.ActivationFunctionType
ALU = mybir.AluOpType
AX = mybir.AxisListType

B, L, H = 2, 8192, 1024
NH, DK, DV = 4, 256, 256
E, CAPB, CAP = 8, 640, 4096
NCORE = 8
NTOK = (B * L) // NCORE
TILE = 1024
NTILE = L // TILE
NCH = TILE // 128
NCHB = NTOK // 128
WCOLS = 2052

DEBUG = []
KSTOP = 99


def make_in_maps(I):
    f32 = np.float32
    bf = ml_dtypes.bfloat16
    hs = np.ascontiguousarray(I["L0_attn_hs"], dtype=f32)
    hsf = hs.reshape(B * L, H)
    Wq, Wk, Wv, Wid = (np.asarray(I["L0_attn_Wq"]), np.asarray(I["L0_attn_Wk"]),
                       np.asarray(I["L0_attn_Wv"]), np.asarray(I["L0_attn_Wid"]))
    Wb, gW1 = np.asarray(I["L0_attn_Wb"]), np.asarray(I["L0_attn_gW1"])
    idx = np.arange(128)
    consts = dict(
        ident_bf=np.eye(128, dtype=bf),
        mask_iu=(idx[:, None] <= idx[None, :]).astype(bf),
        mask_su=(idx[:, None] < idx[None, :]).astype(bf),
        mask_sl=(idx[:, None] > idx[None, :]).astype(bf),
        tri_cum=(idx[:, None] <= idx[None, :]).astype(f32),
        ones_col=np.ones((128, 1), f32),
        ones_col_bf=np.ones((128, 1), bf),
        ones_row=np.ones((1, 128), f32),
        ones_row_bf=np.ones((1, 128), bf),
        one1=np.ones((1, 1), f32),
        iotaE=np.broadcast_to(np.arange(E, dtype=f32), (128, E)).copy(),
        tri8s=(np.arange(8)[:, None] < np.arange(8)[None, :]).astype(f32),
        tri16s=(np.arange(16)[:, None] < np.arange(16)[None, :]).astype(f32),
        epscols=np.broadcast_to(np.array([1e-5, 1e-6, 0.0, 0.0], f32), (128, 4)).copy(),
    )
    selperm = np.zeros((128, 128), f32)
    for a in range(8):
        for bb in range(16):
            selperm[16 * a + bb, 16 * a + bb] = 1.0
    consts["selperm"] = selperm
    colsel = np.zeros((128, 16 * 16), f32)
    for j in range(16):
        colsel[:, 16 * j + j] = 1.0
    consts["colsel"] = colsel
    idxdisp = np.zeros((E, 128, CAPB // 16), np.int16)
    for e in range(E):
        for i in range(CAPB):
            idxdisp[e, i % 16, i // 16] = e * CAPB + i
        idxdisp[e, 16:32] = idxdisp[e, 0:16]
    consts["idxdisp"] = idxdisp.reshape(E * 128, CAPB // 16)

    w1r = np.ascontiguousarray(np.asarray(I["L0_ffn_w1"]).reshape(E, H, 4, 1024).transpose(0, 2, 1, 3))
    w2r = np.ascontiguousarray(np.asarray(I["L0_ffn_w2"]).reshape(E, 4, 1024, H))
    b1t = np.zeros((128, E, 32), f32)
    for e in range(E):
        b1t[:, e, :] = np.asarray(I["L0_ffn_b1"])[e].reshape(32, 128).T

    in_maps = []
    for c in range(NCORE):
        b, h = c // 4, c % 4
        sl = slice(h * DK, (h + 1) * DK)
        wstack = np.zeros((H, WCOLS), f32)
        wstack[:, 0:256] = Wq[:, sl]
        wstack[:, 256:512] = Wk[:, sl]
        wstack[:, 512:768] = Wv[:, sl]
        wstack[:, 768:1024] = Wid[:, sl]
        wstack[:, 1024:2048] = gW1[:1024]
        wstack[:, 2048] = Wb[:, h]
        convw = np.zeros((128, 24), f32)
        for ti, cw in enumerate([I["L0_attn_cwq"], I["L0_attn_cwk"], I["L0_attn_cwv"]]):
            cwh = np.asarray(cw)[sl]
            for ch in range(2):
                convw[:, (ti * 2 + ch) * 4:(ti * 2 + ch) * 4 + 4] = cwh[ch * 128:(ch + 1) * 128]
        firw = np.zeros((128, 2 * 66), f32)
        for ch in range(2):
            firw[:, ch * 66:ch * 66 + 3] = np.asarray(I["L0_attn_firs"])[h][ch * 128:(ch + 1) * 128]
            firw[:, ch * 66 + 3:ch * 66 + 66] = np.asarray(I["L0_attn_firl"])[h][ch * 128:(ch + 1) * 128]
        smalls = np.zeros((1, 64), f32)
        smalls[0, 0:4] = I["L0_attn_log_temp"][h]
        smalls[0, 4:8] = I["L0_attn_floor_param"][h]
        smalls[0, 8:12] = I["L0_attn_gb2"]
        smalls[0, 12] = I["L0_attn_conv_res_logit"][h]
        smalls[0, 13] = I["L0_attn_alpha_id"][h]
        m = dict(
            hs_rows=hs[b],
            hs_slice=np.ascontiguousarray(hsf[NTOK * c:NTOK * (c + 1)]),
            wstack=wstack, convw=convw, firw=firw, smalls=smalls,
            gw1s=np.ascontiguousarray(gW1[1024:1040]),
            gb1row=np.asarray(I["L0_attn_gb1"])[None, :],
            gw2=I["L0_attn_gW2"],
            onw_row=np.asarray(I["L0_attn_o_norm_w"])[None, :],
            wo=I["L0_attn_Wo"],
            c_row=np.asarray(I["L0_c"])[b][None, :],
            wmod=I["L0_wmod"], wg=I["L0_ffn_wg"],
            w1r=w1r, w2r=w2r, b1t=b1t, b2=I["L0_ffn_b2"],
            cid=np.array([[c, b, h, 0]], np.int32),
            hotlt=(np.arange(8) < c).astype(f32)[:, None],
            **consts,
        )
        in_maps.append({k: np.ascontiguousarray(v) for k, v in m.items()})
    return in_maps


class Ctr:
    def __init__(self, sem):
        self.sem = sem
        self.v = 0

    def inc(self, n=1):
        self.v += n
        return self.v



class TE:
    def __init__(self, te):
        self._te = te
        self.last = None

    def matmul(self, *a, **k):
        self.last = self._te.matmul(*a, **k)
        return self.last

    def transpose(self, *a, **k):
        self.last = self._te.transpose(*a, **k)
        return self.last

    def wait_ge(self, *a, **k):
        return self._te.wait_ge(*a, **k)

    def fin(self, sem, n=1):
        assert self.last is not None
        self.last.then_inc(sem, n)
        self.last = None


class Prog:
    ENGS = ("sync", "gpsimd", "tensor", "scalar", "vector")

    def __init__(self):
        self.streams = {e: [] for e in self.ENGS}

    def em(self, eng, fn):
        if eng == "tensor":
            f0 = fn
            fn = lambda te, f0=f0: f0(TE(te))
        self.streams[eng].append(fn)

    def realize(self, block):
        for e in self.ENGS:
            fns = self.streams[e]
            getattr(block, e)(lambda engine, fns=fns: [f(engine) for f in fns])


def build():
    nc = bass.Bass("TRN2", target_bir_lowering=False, debug=False)
    DT = {}

    def din(name, shape, dt=F32):
        DT[name] = nc.dram_tensor(name, list(shape), dt, kind="ExternalInput").ap()

    def do_(name, shape, dt=F32):
        DT[name] = nc.dram_tensor(name, list(shape), dt, kind="ExternalOutput").ap()

    def dint(name, shape, dt=F32):
        DT[name] = nc.dram_tensor(name, tuple(shape), dt, kind="Internal").ap()

    din("hs_rows", (L, H)); din("hs_slice", (NTOK, H))
    din("wstack", (H, WCOLS)); din("convw", (128, 24)); din("firw", (128, 132))
    din("smalls", (1, 64)); din("gw1s", (16, 1024)); din("gb1row", (1, 1024))
    din("gw2", (1024, 4)); din("onw_row", (1, 256)); din("wo", (1024, 1024))
    din("c_row", (1, 1024)); din("wmod", (H, 6144)); din("wg", (H, E))
    din("w1r", (E, 4, 1024, 1024)); din("w2r", (E, 4, 1024, 1024))
    din("b1t", (128, E, 32)); din("b2", (E, 1024)); din("cid", (1, 4), I32)
    for nm in ("ident_bf", "mask_iu", "mask_su", "mask_sl"):
        din(nm, (128, 128), BF16)
    din("tri_cum", (128, 128)); din("ones_col", (128, 1)); din("ones_row", (1, 128))
    din("ones_col_bf", (128, 1), BF16); din("one1", (1, 1))
    din("ones_row_bf", (1, 128), BF16)
    din("iotaE", (128, E)); din("selperm", (128, 128)); din("colsel", (128, 256))
    din("tri8s", (8, 8)); din("tri16s", (16, 16))
    din("idxdisp", (E * 128, CAPB // 16), I16)
    din("epscols", (128, 4)); din("hotlt", (8, 1))
    do_("out", (NTOK, H), F32)

    dint("modbuf", (1, 6144))
    dint("osrc", (128, 2 * L), BF16)
    dint("odst", (4, 128, 2 * L), BF16)
    dint("cnt_src", (1, 16), F32)
    dint("cnt_dst", (8, 16), F32)
    dint("h2buf", (NTOK, H), BF16)
    dint("x2buf", (NTOK, H), F32)
    dint("bucket", (E * CAPB, H), BF16)
    dint("ybuf", (E * CAPB, H), BF16)
    dint("w1b", (E, 4, 1024, 1024), BF16)
    dint("w2b", (E, 4, 1024, 1024), BF16)
    dint("rsc", (1, 256))
    dint("rsc2", (16, 16))

    dbg_shapes = {
        "qfm": ((128, 2 * TILE), BF16), "kfm": ((128, 2 * TILE), BF16),
        "vfm": ((128, 2 * TILE), BF16), "idfm": ((128, 2 * TILE), BF16),
        "beta": ((1, TILE), F32), "rnq": ((128, NCH), F32), "rnk": ((1, TILE), F32),
        "dout": ((128, NCH * 256), BF16),
        "ls": ((128, NCH * 256), BF16), "ll": ((128, NCH * 256), BF16),
        "stats": ((128, NCH * 16), F32),
        "probs": ((128, NCH * 4), F32),
        "ofm": ((128, 2 * TILE), BF16),
        "osrcL": ((128, 2 * L), BF16),
        "x2": ((128, H), F32),
        "h2": ((128, NCHB * H), BF16),
        "scores": ((128, NCHB * E), F32),
        "route": ((128, NCHB * 8), F32),
        "moe": ((128, NCHB * H), BF16),
    }
    for nm in DEBUG:
        do_("dbg_" + nm, *dbg_shapes[nm])

    P = Prog()
    em = P.em
    with ExitStack() as ctx:
        def sbuf(name, shape, dt):
            return ctx.enter_context(nc.sbuf_tensor(name, list(shape), dt))

        def psum(name, shape, dt=F32):
            return ctx.enter_context(nc.psum_tensor(name, list(shape), dt))

        def mksem(name):
            return Ctr(ctx.enter_context(nc.semaphore(name)))

        ident = sbuf("ident", (128, 128), BF16)
        maskiu = sbuf("maskiu", (128, 128), BF16)
        masksu = sbuf("masksu", (128, 128), BF16)
        masksl = sbuf("masksl", (128, 128), BF16)
        tricum = sbuf("tricum", (128, 128), F32)
        onescol = sbuf("onescol", (128, 1), F32)
        onescolb = sbuf("onescolb", (128, 1), BF16)
        onesrow = sbuf("onesrow", (1, 128), F32)
        onesrowb = sbuf("onesrowb", (1, 128), BF16)
        one1 = sbuf("one1s", (1, 1), F32)
        iotaEt = sbuf("iotaEt", (128, E), F32)
        selpermt = sbuf("selpermt", (128, 128), F32)
        colselt = sbuf("colselt", (128, 256), F32)
        tri8t = sbuf("tri8t", (8, 8), F32)
        tri16t = sbuf("tri16t", (16, 16), F32)
        totE = sbuf("totE", (16, 16), F32)
        coffs = sbuf("coffs", (16, 16), F32)
        epst = sbuf("epst", (128, 4), F32)
        idxdispt = sbuf("idxdispt", (128, E * (CAPB // 16)), I16)
        convwt = sbuf("convwt", (128, 24), F32)
        firwt = sbuf("firwt", (128, 132), F32)
        smallst = sbuf("smallst", (1, 64), F32)
        cidt = sbuf("cidt", (1, 4), I32)
        gw1st = sbuf("gw1st", (16, 1024), BF16)
        gw2t = sbuf("gw2t", (128, 8 * 4), BF16)
        wgt = sbuf("wgt", (128, 8 * 8), F32)
        b1c = sbuf("b1c", (128, E, 32), F32)
        svec = sbuf("svec", (128, 17), F32)
        scfm = sbuf("scfm", (128, 8), F32)
        colsc = sbuf("colsc", (128, 8), F32)
        colscb = sbuf("colscb", (128, 8), BF16)
        colsc2 = sbuf("colsc2", (128, 8), F32)
        caP = sbuf("caP", (128, 8), F32)
        onw_t = sbuf("onw_t", (128, 256), BF16)
        gate_c = sbuf("gate_c", (128, 32), F32)
        gb2col = sbuf("gb2col", (4, 1), F32)
        tauinvcol = sbuf("tauinvcol", (4, 1), F32)
        betacol = sbuf("betacol", (128, NCH), F32)
        rnqcol = sbuf("rnqcol", (128, NCH), F32)
        red_a = sbuf("red_a", (128, NCHB), F32)
        red_b = sbuf("red_b", (128, NCHB), F32)
        red_c = sbuf("red_c", (128, NCHB), F32)
        probst = sbuf("probst", (128, NCH, 4), F32)
        mixc = sbuf("mixc", (128, NCH), F32)
        rowsc = sbuf("rowsc", (1, 1024), F32)
        rowsc2 = sbuf("rowsc2", (1, 1024), F32)
        rowscb = sbuf("rowscb", (1, 1024), BF16)
        betarow = sbuf("betarow", (1, TILE), F32)
        rnkrow = sbuf("rnkrow", (1, TILE), F32)
        rowscr8 = sbuf("rowscr8", (8, 512), F32)
        rowscr4 = rowscr8[0:4, :]
        cntrow = sbuf("cntrow", (1, 16), F32)
        agbuf = sbuf("agbuf", (8, 16), F32)
        prefx = sbuf("prefx", (16, 8), F32)
        myp = sbuf("myp", (16, 1), F32)
        ptile = sbuf("ptile", (128, 32), F32)
        hotlt_t = sbuf("hotlt_t", (8, 1), F32)
        scrt = sbuf("scrt", (128, NCHB, 8), F32)
        csum1 = sbuf("csum1", (128, NCHB, 8), F32)
        csum2 = sbuf("csum2", (128, NCHB, 8), F32)
        routt = sbuf("routt", (128, NCHB, 12), F32)
        slot_f = sbuf("slot_f", (128, 2, NCHB), F32)
        gatecols = sbuf("gatecols", (128, NCHB, 2), F32)
        idx16 = sbuf("idx16", (128, 2, 128), I16)
        outroll = sbuf("outroll", (128, 1, 1024), F32)

        big1 = sbuf("big1", (128, 16, 1024), BF16)
        xtok = big1[:, 0:8, :]
        xfm = big1[:, 8:16, :]
        hala = sbuf("hala", (128, 18432), BF16)
        def hview(off, shape):
            n = int(np.prod(shape))
            ap = hala[:, off:off + n]
            if len(shape) == 2:
                ap = ap.rearrange("p (a b) -> p a b", a=shape[0])
            return ap
        qraw = hview(0, (2, 1040))
        kraw = hview(2080, (2, 1040))
        vraw = hview(4160, (2, 1040))
        qfm = hview(6240, (2, 1024))
        ofm = qfm
        kfm = hview(8288, (2, 1024))
        vfm = hview(10336, (2, 1088))
        idfm = hview(12512, (2, 1024))
        scr1 = hala[:, 14560:15584]
        sqscr = hala[:, 15584:16608]
        statsfm = hala[0:16, 15584:16608]
        firslab = hala[:, 16608:17632]
        rnkbc = firslab
        h2_t = hala[:, 0:16384].rearrange("p (a b) -> p a b", a=16)
        h2fmc = hala[:, 16384:18432].bitcast(F32).rearrange("p (a b) -> p a b", a=8)
        big3 = sbuf("big3", (128, 10240), BF16)
        def b3(off, shape):
            n = int(np.prod(shape))
            ap = big3[:, off:off + n]
            if len(shape) == 2:
                ap = ap.rearrange("p (a b) -> p a b", a=shape[0])
            return ap
        ktok = b3(0, (8, 256))
        vtok = b3(2048, (8, 256))
        u_st = b3(4096, (8, 256))
        wT_st = b3(6144, (8, 256))
        S_st = big3[:, 8192:8704].rearrange("p (a b) -> p a b", a=2)
        Mt_s = big3[:, 8704:8832]
        MtT_s = big3[:, 8832:8960]
        Y_s = big3[:, 8960:9088]
        Pp_s = big3[:, 9088:9344]
        TTb_s = big3[:, 9344:9472]
        uadj = big3[:, 9472:9728]
        id_t = big3[:, 9728:9984]
        stats_c = big3[:, 9984:10240].bitcast(F32).rearrange("p (a b) -> p a b", a=8)
        b1f = big1[:].rearrange("p a b -> p (a b)")
        ls_t = b1f[:, 0:2048].rearrange("p (a b) -> p a b", a=8)
        ll_t = b1f[:, 2048:4096].rearrange("p (a b) -> p a b", a=8)
        dout_t = b1f[:, 4096:6144].rearrange("p (a b) -> p a b", a=8)
        attnT = b1f[:, 6144:7168].rearrange("p (a b) -> p a b", a=8)
        oall = b1f[:, 0:16384].rearrange("p (a b) -> p a b", a=8)
        x2roll = big3[:, 0:4096].bitcast(F32).rearrange("p (a b) -> p a b", a=2)
        x2c = b3(0, (2, 1024))
        big4 = sbuf("big4", (128, 25600), BF16)
        xe_b = big4[:, 0:5120].rearrange("p (a b) -> p a b", a=8)
        hid_b = big4[:, 5120:25600].rearrange("p (a b) -> p a b", a=32)
        zerot = big4[:, 4096:8192]
        ga_t = big4[:, 0:1024]
        cb1_t = big4[:, 1024:2048]
        sbt_t = big4[:, 2048:3072]
        gb_t = hala[:, 1024:2048]
        b2t = hala[:, 0:1024]
        g0h = big4[:, 0:8192].rearrange("p (a b) -> p a b", a=8)
        g1h = big4[:, 8192:16384].rearrange("p (a b) -> p a b", a=8)
        wstk = sbuf("wstk", (128, 8, 2052), BF16)
        wqbuf = wstk[:].rearrange("p a b -> p (a b)")
        sld = mksem("sld")
        spe = mksem("spe")
        sve = mksem("sve")
        ssc = mksem("ssc")
        sgp = mksem("sgp")
        sdm = mksem("sdm")
        ssy = mksem("ssy")
        sst = mksem("sst")
        scc = mksem("scc")
        swa = mksem("swa")
        swb = mksem("swb")

        PB0 = psum("PB0", (128, 1024))
        PB1 = psum("PB1", (128, 1024))
        PB2 = psum("PB2", (128, 1024))
        PB3 = psum("PB3", (128, 1024))
        PB0b = PB0[:].bitcast(BF16)
        PB1b = PB1[:].bitcast(BF16)
        PB2b = PB2[:].bitcast(BF16)
        PB3b = PB3[:].bitcast(BF16)

        def vseq(ops, waits=()):
            base = sve.v
            n = len(ops)

            def v(ve):
                for s, val in waits:
                    ve.wait_ge(s.sem, val)
                for i, op in enumerate(ops):
                    if i > 0:
                        ve.wait_ge(sve.sem, base + i)
                    op(ve).then_inc(sve.sem, 1)
            em("vector", v)
            sve.inc(n)
            return sve.v

        def sseq(ops, waits=()):
            base = ssc.v
            n = len(ops)

            def s(se):
                for sm, val in waits:
                    se.wait_ge(sm.sem, val)
                for i, op in enumerate(ops):
                    if i > 0:
                        se.wait_ge(ssc.sem, base + i)
                    op(se).then_inc(ssc.sem, 1)
            em("scalar", s)
            ssc.inc(n)
            return ssc.v

        def row_to_col(rowap, colap, n, psrc, waits=()):
            def t(te):
                for s, val in waits:
                    te.wait_ge(s.sem, val)
                for j in range(n):
                    te.transpose(psrc[:, j:j + 1], rowap[0:1, 128 * j:128 * (j + 1)], one1[:])
                te.fin(spe.sem, 1)
            em("tensor", t)
            pv = spe.inc()
            return vseq([lambda ve: ve.tensor_copy(colap, psrc[:, 0:n])],
                        waits=[(spe, pv)])

        def dbg_store(name, src_ap, waits=()):
            if name not in DEBUG:
                return
            def sy(e, t=DT["dbg_" + name], s=src_ap, w=tuple(waits)):
                for sm, val in w:
                    e.wait_ge(sm.sem, val)
                e.dma_start(out=t, in_=s).then_inc(sst.sem, 16)
            em("sync", sy)
            sst.inc(16)

        em("vector", lambda ve: ve.memset(outroll[:].rearrange("p a b -> p (a b)"), 0.0)
           .then_inc(sve.sem, 1))
        sve.inc()
        for nm, t in [("ident_bf", ident), ("mask_iu", maskiu), ("mask_su", masksu),
                      ("mask_sl", masksl), ("tri_cum", tricum), ("ones_col", onescol),
                      ("ones_row", onesrow), ("one1", one1), ("iotaE", iotaEt),
                      ("selperm", selpermt), ("colsel", colselt), ("tri8s", tri8t),
                      ("convw", convwt), ("firw", firwt), ("smalls", smallst),
                      ("cid", cidt), ("ones_col_bf", onescolb), ("epscols", epst),
                      ("ones_row_bf", onesrowb), ("tri16s", tri16t),
                      ("hotlt", hotlt_t)]:
            em("sync", lambda e, t=t, nm=nm: e.dma_start(out=t[:], in_=DT[nm])
               .then_inc(sld.sem, 16))
            sld.inc(16)
        em("sync", lambda e: e.dma_start(
            out=idxdispt[:].rearrange("p (e c) -> p e c", e=E),
            in_=DT["idxdisp"].rearrange("(e p) c -> p e c", p=128))
            .then_inc(sld.sem, 16))
        sld.inc(16)
        em("sync", lambda e: e.dma_start(out=b1c[:], in_=DT["b1t"])
           .then_inc(sld.sem, 16))
        sld.inc(16)

        def g0(gp):
            gp.load_library(_MLP_LIB)
            gp.dma_start(out=gw1st[:], in_=DT["gw1s"]).then_inc(sdm.sem, 16)
            gp.dma_start(out=gw2t[:].rearrange("p (k f) -> p k f", k=8),
                         in_=DT["gw2"].rearrange("(k p) f -> p k f", p=128)).then_inc(sdm.sem, 16)
            for kt in range(8):
                gp.dma_start(out=wstk[:, kt, :],
                             in_=DT["wstack"][128 * kt:128 * (kt + 1), :]).then_inc(sdm.sem, 16)
            gp.dma_start(out=wgt[:].rearrange("p (a b) -> p a b", a=8),
                         in_=DT["wg"].rearrange("(k p) f -> p k f", p=128)).then_inc(sdm.sem, 16)
        em("gpsimd", g0)
        sdm.inc(16 * 11)

        PRE_JOBS = [(w, e2, q2) for e2 in range(E) for q2 in range(4)
                    for w in ("w1b", "w2b")]
        PRE_V = 0

        def em_pre(jobs):
            def g_p(gp, jobs=tuple(jobs)):
                for wch, e2, q2 in jobs:
                    src = "w1r" if wch == "w1b" else "w2r"
                    gp.dma_start(out=DT[wch][e2, q2], in_=DT[src][e2, q2]
                                 ).then_inc(sdm.sem, 16)
            em("gpsimd", g_p)
            return sdm.inc(16 * len(jobs))
        LD0G = sdm.v
        LD0 = sld.v

        if KSTOP >= 1:
            em("sync", lambda e: e.dma_start(out=rowsc[:], in_=DT["c_row"]).then_inc(sld.sem, 16))
            c_l = sld.inc(16)
            sv = sseq([lambda s: s.activation(rowsc2[:], rowsc[:], AF.Silu)],
                      waits=[(sld, c_l)])
            row_to_col(rowsc2, scfm[:], 8, PB3[:, 0:8], waits=[(ssc, sv)])
            scfm_v = sve.v

            wmod_stg = big1[:].rearrange("p a b -> p (a b)").bitcast(F32)
            for rnd in range(2):
                for kt in range(8):
                    def g(gp, kt=kt, rnd=rnd, pv=spe.v):
                        gp.wait_ge(spe.sem, pv)
                        gp.dma_start(out=wmod_stg[:, 0:3072],
                                     in_=DT["wmod"][128 * kt:128 * (kt + 1),
                                                    3072 * rnd:3072 * (rnd + 1)]).then_inc(sdm.sem, 16)
                    em("gpsimd", g)
                    dv = sdm.inc(16)

                    def t(te, kt=kt, dv=dv, vw=(scfm_v if rnd == 0 else sve.v)):
                        te.wait_ge(sdm.sem, dv)
                        if kt == 0:
                            te.wait_ge(sve.sem, vw)
                        for gq in range(6):
                            te.matmul([PB0, PB0, PB1, PB1, PB2, PB2][gq][0:1, 512 * (gq % 2):512 * (gq % 2 + 1)],
                                      scfm[:, kt:kt + 1],
                                      wmod_stg[:, 512 * gq:512 * (gq + 1)],
                                      start=(kt == 0), stop=(kt == 7))
                        te.fin(spe.sem, 1)
                    em("tensor", t)
                    spe.inc()
                for gq in range(3):
                    gv = vseq([lambda ve, gq=gq: ve.tensor_copy(
                        rowsc[:], [PB0, PB1, PB2][gq][0:1, 0:1024])],
                        waits=[(spe, spe.v), (ssy, ssy.v)])
                    def sy(e, rnd=rnd, gq=gq, gv=gv):
                        e.wait_ge(sve.sem, gv)
                        e.dma_start(out=DT["modbuf"][0:1, 3072 * rnd + 1024 * gq:
                                                     3072 * rnd + 1024 * (gq + 1)],
                                    in_=rowsc[:]).then_inc(ssy.sem, 16)
                    em("sync", sy)
                    ssy.inc(16)
            MODB_V = ssy.v

            def modload(off, dst):
                def sy(e, w=ssy.v, pv=sld.v, pe=spe.v, vv=sve.v, sc=ssc.v):
                    e.wait_ge(ssy.sem, w)
                    e.wait_ge(sld.sem, pv)
                    e.wait_ge(spe.sem, pe)
                    e.wait_ge(sve.sem, vv)
                    e.wait_ge(ssc.sem, sc)
                    e.dma_start(out=dst, in_=DT["modbuf"][0:1, off:off + 1024]).then_inc(sld.sem, 16)
                em("sync", sy)
                return sld.inc(16)

        if KSTOP >= 2:
            sa_l = modload(0, rowsc[:])
            sa_v = row_to_col(rowsc, colscb[:], 8, PB3[:, 0:8], waits=[(sld, sa_l)])

            def t_svec(te, sa_v=sa_v):
                te.wait_ge(sve.sem, sa_v)
                for m in range(17):
                    w = min(128, WCOLS - 128 * m)
                    for kt in range(8):
                        te.matmul(PB3[0:w, 16 + m:17 + m], wstk[:, kt, 128 * m:128 * m + w],
                                  colscb[:, kt:kt + 1], start=(kt == 0), stop=(kt == 7))
                te.fin(spe.sem, 1)
            em("tensor", t_svec)
            svec_pe = spe.inc()
            em("sync", lambda e, pv=sld.v, pe=spe.v, vv=sve.v: (
                e.wait_ge(sld.sem, pv), e.wait_ge(spe.sem, pe), e.wait_ge(sve.sem, vv),
                e.dma_start(out=rowsc2[:], in_=DT["gb1row"]).then_inc(sld.sem, 16)))
            gb1_l = sld.inc(16)
            gb1_v = row_to_col(rowsc2, colsc2[:], 8, PB3[:, 8:16], waits=[(sld, gb1_l)])
            vseq([lambda ve: ve.tensor_copy(svec[:, 0:16], PB3[:, 16:32]),
                  lambda ve: ve.tensor_copy(svec[0:4, 16:17], PB3[0:4, 32:33]),
                  lambda ve: ve.tensor_add(svec[:, 8:16], svec[:, 8:16], colsc2[:])],
                 waits=[(spe, svec_pe), (sve, gb1_v)])

            ca_l = modload(1024, rowsc[:])
            ca_v = vseq([lambda ve: ve.tensor_scalar_add(rowsc[:], rowsc[:], 1.0)],
                        waits=[(sld, ca_l)])
            cap_v = row_to_col(rowsc, caP[:], 8, PB3[:, 0:8], waits=[(sve, ca_v)])
            vseq([(lambda ve, kt=kt: ve.tensor_scalar(wstk[:, kt, :], wstk[:, kt, :],
                                                      caP[:, kt:kt + 1], None, ALU.mult))
                  for kt in range(8)], waits=[(sve, cap_v)])
            WSTK_V = sve.v

            def plain_bcast(off, dst_tile, plus1=False, width=1024):
                ld = modload(off, rowsc[:]) if off >= 0 else None
                cb_v = vseq([lambda ve, width=width: ve.tensor_copy(
                    rowscb[0:1, 0:width], rowsc[0:1, 0:width])],
                    waits=([(sld, ld)] if ld is not None else []) + [(sve, sve.v)])
                def t_b(te, cb_v=cb_v, width=width):
                    te.wait_ge(sve.sem, cb_v)
                    for g in range(0, width, 512):
                        gw = min(512, width - g)
                        te.matmul(PB0[:, g:g + gw], onesrowb[:], rowscb[0:1, g:g + gw],
                                  start=True, stop=True)
                    te.fin(spe.sem, 1)
                em("tensor", t_b)
                pv = spe.inc()
                if plus1:
                    return vseq([lambda ve, width=width: ve.tensor_scalar_add(
                        dst_tile, PB0[:, 0:width], 1.0)], waits=[(spe, pv)])
                return vseq([lambda ve, width=width: ve.tensor_copy(dst_tile, PB0[:, 0:width])],
                            waits=[(spe, pv)])

            def signmax_store(off, dst_tile):
                ld = modload(off, rowsc[:])
                s_v = sseq([lambda s: s.activation(rowsc2[:], rowsc[:], AF.Sign)],
                           waits=[(sld, ld)])
                m_v = vseq([lambda ve: ve.tensor_scalar_mul(rnkrow[:], rowsc[:], -1.0),
                            lambda ve: ve.tensor_max(rowsc[:], rowsc[:], rnkrow[:]),
                            lambda ve: ve.tensor_scalar_max(rowsc[:], rowsc[:], 0.25),
                            lambda ve: ve.tensor_mul(rowsc[:], rowsc[:], rowsc2[:])],
                           waits=[(ssc, s_v)])
                def sy(e, off=off, m_v=m_v):
                    e.wait_ge(sve.sem, m_v)
                    e.dma_start(out=DT["modbuf"][0:1, off:off + 1024], in_=rowsc[:]).then_inc(ssy.sem, 16)
                em("sync", sy)
                ssy.inc(16)
                if dst_tile is not None:
                    cb_v = vseq([lambda ve: ve.tensor_copy(rowscb[:], rowsc[:])],
                                waits=[(sve, m_v)])
                    def t_b(te, cb_v=cb_v, vv=sve.v):
                        te.wait_ge(sve.sem, max(cb_v, vv))
                        te.matmul(PB0[:, 0:512], onesrowb[:], rowscb[0:1, 0:512], start=True, stop=True)
                        te.matmul(PB0[:, 512:1024], onesrowb[:], rowscb[0:1, 512:1024], start=True, stop=True)
                        te.fin(spe.sem, 1)
                    em("tensor", t_b)
                    pv = spe.inc()
                    vseq([lambda ve, dst_tile=dst_tile: ve.tensor_copy(dst_tile, PB0[:, 0:1024])],
                         waits=[(spe, pv)])
            signmax_store(2048, ga_t)
            signmax_store(5120, None)
            plain_bcast(4096, cb1_t, plus1=True)
            plain_bcast(3072, sbt_t)
            em("sync", lambda e, pv=sld.v, pe=spe.v, vv=sve.v: (
                e.wait_ge(sld.sem, pv), e.wait_ge(spe.sem, pe), e.wait_ge(sve.sem, vv),
                e.dma_start(out=rowsc[0:1, 0:256], in_=DT["onw_row"]).then_inc(sld.sem, 16)))
            onw_l = sld.inc(16)
            ocb_v = vseq([lambda ve: ve.tensor_copy(rowscb[0:1, 0:256], rowsc[0:1, 0:256])],
                         waits=[(sld, onw_l), (sve, sve.v)])
            def t_onw(te, ocb_v=ocb_v):
                te.wait_ge(sve.sem, ocb_v)
                te.matmul(PB0[:, 0:256], onesrowb[:], rowscb[0:1, 0:256], start=True, stop=True)
                te.fin(spe.sem, 1)
            em("tensor", t_onw)
            vseq([lambda ve: ve.tensor_copy(onw_t[:], PB0[:, 0:256])], waits=[(spe, spe.inc())])

            s_v = sseq([lambda s: s.activation(rowsc[0:1, 0:4], smallst[0:1, 0:4], AF.Exp),
                        lambda s: s.activation(rowsc[0:1, 4:8], smallst[0:1, 4:8], AF.Sigmoid),
                        lambda s: s.activation(rowsc[0:1, 9:10], smallst[0:1, 12:13], AF.Sigmoid)],
                       waits=[(sld, sld.v), (sve, sve.v)])
            gv = vseq([lambda ve: ve.reciprocal(rowsc[0:1, 0:4], rowsc[0:1, 0:4]),
                       lambda ve: ve.tensor_scalar_mul(rowsc[0:1, 4:8], rowsc[0:1, 4:8], 0.05),
                       lambda ve: ve.reduce_sum(rowsc[0:1, 8:9], rowsc[0:1, 4:8], AX.X, negate=True),
                       lambda ve: ve.tensor_scalar_add(rowsc[0:1, 8:9], rowsc[0:1, 8:9], 1.0),
                       lambda ve: ve.tensor_copy(rowsc[0:1, 10:11], smallst[0:1, 13:14]),
                       lambda ve: ve.tensor_copy(rowsc[0:1, 11:15], smallst[0:1, 8:12])],
                      waits=[(ssc, s_v)])
            def t_gc(te, w=gv):
                te.wait_ge(sve.sem, w)
                te.transpose(PB3[0:4, 40:41], rowsc[0:1, 0:4], one1[:])
                te.transpose(PB3[0:4, 41:42], rowsc[0:1, 11:15], one1[:])
                te.fin(spe.sem, 1)
            em("tensor", t_gc)
            gcp = spe.inc()
            vseq([lambda ve: ve.tensor_copy(tauinvcol[:], PB3[0:4, 40:41]),
                  lambda ve: ve.tensor_copy(gb2col[:], PB3[0:4, 41:42])],
                 waits=[(spe, gcp)])
            gcb_v = vseq([lambda ve: ve.tensor_copy(rowscb[0:1, 0:32], rowsc[0:1, 0:32])],
                         waits=[(sve, gv)])
            def t_bc3(te, gcb_v=gcb_v):
                te.wait_ge(sve.sem, gcb_v)
                te.matmul(PB1[:, 0:32], onesrowb[:], rowscb[0:1, 0:32], start=True, stop=True)
                te.fin(spe.sem, 1)
            em("tensor", t_bc3)
            vseq([lambda ve: ve.tensor_copy(gate_c[:], PB1[:, 0:32])], waits=[(spe, spe.inc())])
            SETUP_DONE = dict(ve=sve.v, pe=spe.v)

        o_dma_ct = 0

        def tileA(t_i):
            nonlocal o_dma_ct
            tok0 = TILE * t_i

            def g(gp, xw=sve.v):
                gp.wait_ge(sve.sem, xw)
                for chn in range(NCH):
                    gp.dma_start(out=xtok[:, chn, :],
                                 in_=DT["hs_rows"][tok0 + 128 * chn:tok0 + 128 * (chn + 1), :]).then_inc(sdm.sem, 16)
            em("gpsimd", g)
            ld_v = sdm.inc(16 * NCH)

            sq_v = sseq([(lambda s, chn=chn: s.activation(sqscr[:, 0:1024], xtok[:, chn, :],
                                                          AF.Square, accum_out=red_b[:, chn:chn + 1]))
                         for chn in range(NCH)],
                        waits=[(sdm, ld_v), (sgp, sgp.v), (sve, sve.v)])
            ra_v = vseq([(lambda ve, chn=chn: ve.reduce_sum(red_a[:, chn:chn + 1], xtok[:, chn, :], AX.X))
                  for chn in range(NCH)], waits=[(sdm, ld_v)])
            ln_v = vseq([
                lambda ve: ve.tensor_scalar_mul(red_a[:, 0:NCH], red_a[:, 0:NCH], 1.0 / H),
                lambda ve: ve.tensor_scalar_mul(red_b[:, 0:NCH], red_b[:, 0:NCH], 1.0 / H),
                lambda ve: ve.tensor_mul(red_c[:, 0:NCH], red_a[:, 0:NCH], red_a[:, 0:NCH]),
                lambda ve: ve.tensor_sub(red_b[:, 0:NCH], red_b[:, 0:NCH], red_c[:, 0:NCH]),
            ], waits=[(ssc, sq_v), (sve, ra_v)])
            rs_v = sseq([lambda s: s.activation(red_b[:, 0:NCH], red_b[:, 0:NCH],
                                                AF.Sqrt, bias=epst[:, 0:1])], waits=[(sve, ln_v)])
            rc_v = vseq([lambda ve: ve.reciprocal(red_b[:, 0:NCH], red_b[:, 0:NCH])],
                        waits=[(ssc, rs_v)])
            xt_v = vseq([(lambda ve, chn=chn: ve.tensor_scalar(
                            xtok[:, chn, :], xtok[:, chn, :],
                            red_a[:, chn:chn + 1], red_b[:, chn:chn + 1],
                            ALU.subtract, ALU.mult)) for chn in range(NCH)],
                        waits=[(sve, rc_v)])

            for fc in range(8):
                for grp in range(2):
                    def t_b(te, fc=fc, grp=grp, xw=xt_v, vv=sve.v):
                        te.wait_ge(sve.sem, max(xw, vv))
                        for j in range(4):
                            chn = grp * 4 + j
                            te.transpose(PB2b[:, 128 * j:128 * (j + 1)],
                                         xtok[:, chn, 128 * fc:128 * (fc + 1)], ident[:])
                        te.fin(spe.sem, 1)
                    em("tensor", t_b)
                    pv = spe.inc()
                    vseq([lambda ve, fc=fc, grp=grp: ve.tensor_copy(
                        xfm[:, fc, 512 * grp:512 * (grp + 1)], PB2b[:, 0:512])],
                        waits=[(spe, pv)])
            XFM_V = sve.v

            mlist = [0, 1, 2, 3, 4, 5, 6, 7, 16]
            raw_dst = {0: (qraw, 0), 1: (qraw, 1), 2: (kraw, 0), 3: (kraw, 1),
                       4: (vraw, 0), 5: (vraw, 1), 6: (idfm, 0), 7: (idfm, 1)}
            evac_hist = []
            for mi, m in enumerate(mlist):
                pb = PB0 if mi % 2 == 0 else PB1
                wait_evac = evac_hist[mi - 2] if mi >= 2 else None

                def t_m(te, m=m, pb=pb, xv=XFM_V, we=wait_evac, sw=ssc.v):
                    te.wait_ge(sve.sem, xv)
                    te.wait_ge(ssc.sem, we if we is not None else sw)
                    w = min(128, WCOLS - 128 * m)
                    for kt in range(8):
                        for sub in range(2):
                            te.matmul(pb[0:w, 512 * sub:512 * (sub + 1)],
                                      wstk[:, kt, 128 * m:128 * m + w],
                                      xfm[:, kt, 512 * sub:512 * (sub + 1)],
                                      start=(kt == 0), stop=(kt == 7))
                    te.fin(spe.sem, 1)
                em("tensor", t_m)
                pv = spe.inc()
                if m == 16:
                    ev = sseq([lambda s, pb=pb: s.activation(
                        betarow[:], pb[0:1, 0:TILE], AF.Sigmoid, bias=svec[0:1, 16:17])],
                        waits=[(spe, pv)])
                else:
                    dst, half = raw_dst[m]
                    if dst is idfm:
                        tgt = dst[:, half, :]
                    else:
                        tgt = dst[:, half, 8:8 + TILE]
                    ev = sseq([lambda s, tgt=tgt, pb=pb, m=m: s.activation(
                        tgt, pb[:, 0:TILE], AF.Identity, bias=svec[:, m:m + 1])],
                        waits=[(spe, pv)])
                evac_hist.append(ev)
            PROJ_V = ssc.v

            bcol_v = row_to_col(betarow, betacol[:, 0:NCH], NCH, PB2[:, 0:NCH],
                                waits=[(ssc, PROJ_V)])

            conv_done = []
            prev_sv2 = PROJ_V
            conv_extra = [(sst, sst.v), (sgp, sgp.v)]
            for ti_n, (raw, dstf, coff) in enumerate(
                    [(qraw, qfm, 0), (kraw, kfm, 8), (vraw, vfm, 16)]):
                for half in range(2):
                    wcol = coff + half * 4
                    if dstf is vfm:
                        tgt = vfm[:, half, 64:64 + TILE]
                    else:
                        tgt = dstf[:, half, :]
                    vv = vseq([
                        lambda ve, raw=raw, half=half, wcol=wcol: ve.tensor_scalar(
                            sqscr[:, 0:TILE], raw[:, half, 5:5 + TILE],
                            convwt[:, wcol:wcol + 1], None, ALU.mult),
                        lambda ve, raw=raw, half=half, wcol=wcol: ve.scalar_tensor_tensor(
                            sqscr[:, 0:TILE], raw[:, half, 6:6 + TILE],
                            convwt[:, wcol + 1:wcol + 2], sqscr[:, 0:TILE],
                            ALU.mult, ALU.add),
                        lambda ve, raw=raw, half=half, wcol=wcol: ve.scalar_tensor_tensor(
                            sqscr[:, 0:TILE], raw[:, half, 7:7 + TILE],
                            convwt[:, wcol + 2:wcol + 3], sqscr[:, 0:TILE],
                            ALU.mult, ALU.add),
                        lambda ve, raw=raw, half=half, wcol=wcol: ve.scalar_tensor_tensor(
                            sqscr[:, 0:TILE], raw[:, half, 8:8 + TILE],
                            convwt[:, wcol + 3:wcol + 4], sqscr[:, 0:TILE],
                            ALU.mult, ALU.add),
                    ], waits=[(ssc, prev_sv2)] + conv_extra)
                    sv2 = sseq([lambda s, tgt=tgt: s.activation(tgt, sqscr[:, 0:TILE], AF.Silu)],
                               waits=[(sve, vv)])
                    prev_sv2 = sv2
                    conv_done.append(sv2)
            CONV_V = ssc.v

            halo_v = vseq([
                (lambda ve, raw=raw, half=half: ve.tensor_copy(
                    raw[:, half, 5:8], raw[:, half, 5 + TILE:8 + TILE]))
                for raw in (qraw, kraw, vraw) for half in range(2)
            ], waits=[(ssc, CONV_V)])
            fhalo_v = None

            for tname, src in (("q", qfm), ("k", kfm)):
                for half in range(2):
                    sv3 = sseq([lambda s, src=src, half=half: s.activation(
                        sqscr[:, 0:TILE], src[:, half, :], AF.Square)],
                        waits=[(ssc, CONV_V), (spe, spe.v)])
                    def t_q(te, half=half, sv3=sv3):
                        te.wait_ge(ssc.sem, sv3)
                        for sub in range(2):
                            te.matmul(PB3[0:1, 512 * sub:512 * (sub + 1)], onescolb[:],
                                      sqscr[:, 512 * sub:512 * (sub + 1)],
                                      start=(half == 0), stop=(half == 1))
                        te.fin(spe.sem, 1)
                    em("tensor", t_q)
                    spe.inc()
                rowdst = rowsc2[0:1, 0:TILE] if tname == "q" else rnkrow[:]
                rv = sseq([lambda s, rowdst=rowdst: s.activation(
                    rowdst, PB3[0:1, 0:TILE], AF.Sqrt, bias=epst[0:1, 1:2])],
                    waits=[(spe, spe.v)])
                if tname == "q":
                    cv = row_to_col(rowsc2, rnqcol[:, 0:NCH], NCH, PB2[:, 0:NCH],
                                    waits=[(ssc, rv)])
                    vseq([lambda ve: ve.reciprocal(rnqcol[:, 0:NCH], rnqcol[:, 0:NCH])],
                         waits=[(sve, cv)])
                else:
                    vseq([lambda ve: ve.reciprocal(rnkrow[:], rnkrow[:])],
                         waits=[(ssc, rv)])
            RN_V = sve.v
            rq_v = RN_V
            rkcb_v = vseq([lambda ve: ve.tensor_copy(rowscb[0:1, 0:TILE], rnkrow[:])],
                          waits=[(sve, RN_V)])
            def t_rk(te, w=rkcb_v):
                te.wait_ge(sve.sem, w)
                te.matmul(PB3[:, 0:512], onesrowb[:], rowscb[0:1, 0:512], start=True, stop=True)
                te.matmul(PB3[:, 512:1024], onesrowb[:], rowscb[0:1, 512:1024], start=True, stop=True)
                te.fin(spe.sem, 1)
            em("tensor", t_rk)
            prk = spe.inc()
            rkb_v = vseq([lambda ve: ve.tensor_copy(rnkbc[:, 0:TILE], PB3[:, 0:TILE])],
                         waits=[(spe, prk)])
            kh_v = vseq([
                lambda ve: ve.tensor_mul(kfm[:, 0, :], kfm[:, 0, :], rnkbc[:, 0:TILE]),
                lambda ve: ve.tensor_mul(kfm[:, 1, :], kfm[:, 1, :], rnkbc[:, 0:TILE]),
            ], waits=[(sve, rkb_v)])

            bcb_v = vseq([lambda ve: ve.tensor_copy(rowscb[0:1, 0:TILE], betarow[:])],
                         waits=[(ssc, PROJ_V), (sve, kh_v)])

            def t_bb(te, w=bcb_v, vv=sve.v):
                te.wait_ge(sve.sem, max(w, vv))
                te.matmul(PB3[:, 0:512], onesrowb[:], rowscb[0:1, 0:512],
                          start=True, stop=True)
                te.matmul(PB3[:, 512:1024], onesrowb[:], rowscb[0:1, 512:1024],
                          start=True, stop=True)
                te.fin(spe.sem, 1)
            em("tensor", t_bb)
            pv = spe.inc()
            bbc_v = vseq([lambda ve: ve.tensor_copy(scr1[:, 0:TILE], PB3[:, 0:TILE])],
                         waits=[(spe, pv)])
            betabc = scr1

            PB3f = PB3
            for cc in range(NCH):
                C = slice(128 * cc, 128 * (cc + 1))
                def t_tp(te, cc=cc, C=C, vv=sve.v, kw=kh_v):
                    te.wait_ge(sve.sem, max(vv, kw))
                    te.transpose(PB2b[:, 0:128], kfm[:, 0, C], ident[:])
                    te.transpose(PB2b[:, 128:256], kfm[:, 1, C], ident[:])
                    te.transpose(PB2b[:, 256:384], vfm[:, 0, 64 + 128 * cc:64 + 128 * (cc + 1)], ident[:])
                    te.transpose(PB2b[:, 384:512], vfm[:, 1, 64 + 128 * cc:64 + 128 * (cc + 1)], ident[:])
                    te.fin(spe.sem, 1)
                em("tensor", t_tp)
                pv = spe.inc()
                tp_v = vseq([lambda ve, cc=cc: ve.tensor_copy(ktok[:, cc, :], PB2b[:, 0:256]),
                             lambda ve, cc=cc: ve.tensor_copy(vtok[:, cc, :], PB2b[:, 256:512])],
                            waits=[(spe, pv)])
                def t_ga(te, C=C, vv=tp_v):
                    te.wait_ge(sve.sem, vv)
                    for kt in range(2):
                        te.matmul(PB3f[:, 0:128], kfm[:, kt, C], kfm[:, kt, C],
                                  start=(kt == 0), stop=(kt == 1))
                    for kt in range(2):
                        te.matmul(PB3f[:, 128:256], kfm[:, kt, C], qfm[:, kt, C],
                                  start=(kt == 0), stop=(kt == 1))
                    te.fin(spe.sem, 1)
                em("tensor", t_ga)
                pv = spe.inc()
                ga_v = vseq([
                    lambda ve, cc=cc: ve.tensor_mul(id_t[:, 0:128], masksu[:], betabc[:, 128 * cc:128 * (cc + 1)]),
                    lambda ve: ve.tensor_mul(Mt_s[:], PB3f[:, 0:128], id_t[:, 0:128]),
                    lambda ve, cc=cc: ve.scalar_tensor_tensor(
                        MtT_s[:], PB3f[:, 0:128], betacol[:, cc:cc + 1], masksl[:],
                        ALU.mult, ALU.mult),
                    lambda ve, cc=cc: ve.tensor_mul(attnT[:, cc, :], PB3f[:, 128:256], maskiu[:]),
                ], waits=[(spe, pv)])
                y0_v = vseq([lambda ve: ve.scalar_tensor_tensor(
                    Y_s[:], MtT_s[:], -1.0, ident[:], ALU.mult, ALU.add)],
                    waits=[(sve, ga_v)])

                def t_p1(te, w=y0_v):
                    te.wait_ge(sve.sem, w)
                    te.matmul(PB3f[:, 256:384], MtT_s[:], Mt_s[:], start=True, stop=True)
                    te.matmul(PB3f[:, 384:512], Mt_s[:], MtT_s[:], start=True, stop=True)
                    te.fin(spe.sem, 1)
                em("tensor", t_p1)
                pv = spe.inc()
                pp_v = vseq([lambda ve: ve.tensor_copy(Pp_s[:], PB3f[:, 256:512])],
                            waits=[(spe, pv)])
                for k in range(1, 7):
                    def t_pr(te, w=pp_v, k=k):
                        te.wait_ge(sve.sem, w)
                        te.matmul(PB3f[:, 512:640], Pp_s[:, 0:128], Y_s[:],
                                  start=True, stop=True)
                        if k < 6:
                            te.matmul(PB3f[:, 256:384], Pp_s[:, 128:256], Pp_s[:, 0:128],
                                      start=True, stop=True)
                            te.matmul(PB3f[:, 384:512], Pp_s[:, 0:128], Pp_s[:, 128:256],
                                      start=True, stop=True)
                        te.fin(spe.sem, 1)
                    em("tensor", t_pr)
                    pv = spe.inc()
                    ops = [lambda ve: ve.tensor_add(Y_s[:], PB3f[:, 512:640], Y_s[:])]
                    if k < 6:
                        ops.append(lambda ve: ve.tensor_copy(Pp_s[:], PB3f[:, 256:512]))
                    pp_v = vseq(ops, waits=[(spe, pv)])
                def t_tt(te, w=pp_v):
                    te.wait_ge(sve.sem, w)
                    te.transpose(PB2b[:, 0:128], Y_s[:], ident[:])
                    te.fin(spe.sem, 1)
                em("tensor", t_tt)
                pv = spe.inc()
                tt_v = vseq([lambda ve, cc=cc: ve.tensor_scalar(
                    TTb_s[:], PB2b[:, 0:128], betacol[:, cc:cc + 1], None, ALU.mult)],
                    waits=[(spe, pv)])
                def t_uw(te, cc=cc, w=tt_v):
                    te.wait_ge(sve.sem, w)
                    te.matmul(PB3f[:, 0:256], TTb_s[:], vtok[:, cc, :], start=True, stop=True)
                    te.matmul(PB3f[:, 256:384], ktok[:, cc, 0:128], TTb_s[:], start=True, stop=True)
                    te.matmul(PB3f[:, 384:512], ktok[:, cc, 128:256], TTb_s[:], start=True, stop=True)
                    te.fin(spe.sem, 1)
                em("tensor", t_uw)
                pv = spe.inc()
                vseq([lambda ve, cc=cc: ve.tensor_copy(u_st[:, cc, :], PB3f[:, 0:256]),
                      lambda ve, cc=cc: ve.tensor_copy(wT_st[:, cc, :], PB3f[:, 256:512])],
                     waits=[(spe, pv)])
            PRE_V = sve.v

            for cc in range(NCH):
                C = slice(128 * cc, 128 * (cc + 1))
                def t_ws(te, cc=cc, w=sve.v):
                    te.wait_ge(sve.sem, w)
                    for kt in range(2):
                        te.matmul(PB0[:, 0:256], wT_st[:, cc, 128 * kt:128 * (kt + 1)],
                                  S_st[:, kt, :], start=(kt == 0), stop=(kt == 1))
                    te.fin(spe.sem, 1)
                em("tensor", t_ws)
                pv = spe.inc()
                ua_v = vseq([lambda ve, cc=cc: ve.tensor_sub(
                    uadj[:], u_st[:, cc, :], PB0[:, 0:256])], waits=[(spe, pv)])

                def t_o(te, cc=cc, C=C, w=ua_v):
                    te.wait_ge(sve.sem, w)
                    for kt in range(2):
                        te.matmul(PB0[:, 256:512], qfm[:, kt, C], S_st[:, kt, :],
                                  start=(kt == 0), stop=False)
                    te.matmul(PB0[:, 256:512], attnT[:, cc, :], uadj[:],
                              start=False, stop=True)
                    for kt in range(2):
                        te.matmul(PB0[:, 512 + 256 * kt:768 + 256 * kt],
                                  ktok[:, cc, 128 * kt:128 * (kt + 1)], uadj[:],
                                  start=True, stop=True)
                    te.fin(spe.sem, 1)
                em("tensor", t_o)
                pv = spe.inc()
                vseq([lambda ve, cc=cc: ve.tensor_scalar(
                        dout_t[:, cc, :], PB0[:, 256:512], rnqcol[:, cc:cc + 1],
                        None, ALU.mult),
                      lambda ve: ve.tensor_add(S_st[:, 0, :], PB0[:, 512:768], S_st[:, 0, :]),
                      lambda ve: ve.tensor_add(S_st[:, 1, :], PB0[:, 768:1024], S_st[:, 1, :])],
                     waits=[(spe, pv)])
            SCAN_V = sve.v

            FSPLIT = 63
            for half in range(2):
                src = vfm[:, half, :]
                ls_ops = [
                    lambda ve, half=half: ve.tensor_scalar(
                        firslab[:, 0:TILE], vfm[:, half, 62:62 + TILE],
                        firwt[:, 66 * half + 0:66 * half + 1], None, ALU.mult),
                    lambda ve, half=half: ve.scalar_tensor_tensor(
                        firslab[:, 0:TILE], vfm[:, half, 63:63 + TILE],
                        firwt[:, 66 * half + 1:66 * half + 2], firslab[:, 0:TILE],
                        ALU.mult, ALU.add),
                    lambda ve, half=half: ve.scalar_tensor_tensor(
                        firslab[:, 0:TILE], vfm[:, half, 64:64 + TILE],
                        firwt[:, 66 * half + 2:66 * half + 3], firslab[:, 0:TILE],
                        ALU.mult, ALU.add),
                ]
                lsv = vseq(ls_ops, waits=[(sve, sve.v), (ssc, ssc.v)])
                for cc2 in range(0, NCH, 4):
                    def t_f(te, cc2=cc2, half=half, w=lsv, vv=sve.v):
                        te.wait_ge(sve.sem, max(w, vv))
                        for j in range(4):
                            te.transpose(PB2b[:, 128 * j:128 * (j + 1)],
                                         firslab[:, 128 * (cc2 + j):128 * (cc2 + j + 1)],
                                         ident[:])
                        te.fin(spe.sem, 1)
                    em("tensor", t_f)
                    pv = spe.inc()
                    vseq([lambda ve, cc2=cc2, half=half: ve.tensor_copy(
                        ls_t[:, cc2:cc2 + 4, 128 * half:128 * (half + 1)]
                        .rearrange("p a b -> p (a b)" if False else "p a b -> p a b"),
                        PB2b[:, 0:512].rearrange("p (a b) -> p a b", a=4))],
                        waits=[(spe, pv)])
                ll_ops_v = []
                for j in range(FSPLIT):
                    if j == 0:
                        ll_ops_v.append(lambda ve, half=half: ve.tensor_scalar(
                            firslab[:, 0:TILE], vfm[:, half, 2:2 + TILE],
                            firwt[:, 66 * half + 3:66 * half + 4], None, ALU.mult))
                    else:
                        ll_ops_v.append(lambda ve, half=half, j=j: ve.scalar_tensor_tensor(
                            firslab[:, 0:TILE], vfm[:, half, 2 + j:2 + j + TILE],
                            firwt[:, 66 * half + 3 + j:66 * half + 4 + j],
                            firslab[:, 0:TILE], ALU.mult, ALU.add))
                llv = vseq(ll_ops_v, waits=[(sve, sve.v)])
                gbase = sgp.v
                def g_f(gp, half=half, w=llv, gbase=gbase):
                    if FSPLIT >= 63:
                        return
                    gp.wait_ge(sve.sem, w)
                    k = 0
                    for j in range(FSPLIT, 63):
                        if j == FSPLIT:
                            ins = gp.tensor_scalar(
                                sqscr[:, 0:TILE], vfm[:, half, 2 + j:2 + j + TILE],
                                firwt[:, 66 * half + 3 + j:66 * half + 4 + j],
                                None, ALU.mult)
                        else:
                            ins = gp.scalar_tensor_tensor(
                                sqscr[:, 0:TILE], vfm[:, half, 2 + j:2 + j + TILE],
                                firwt[:, 66 * half + 3 + j:66 * half + 4 + j],
                                sqscr[:, 0:TILE], ALU.mult, ALU.add)
                        k += 1
                        ins.then_inc(sgp.sem, 1)
                        gp.wait_ge(sgp.sem, gbase + k)
                em("gpsimd", g_f)
                gv2 = sgp.inc(63 - FSPLIT)
                if FSPLIT < 63:
                    llv2 = vseq([lambda ve: ve.tensor_add(firslab[:, 0:TILE],
                                                          firslab[:, 0:TILE], sqscr[:, 0:TILE])],
                                waits=[(sgp, gv2)])
                else:
                    llv2 = llv
                for cc2 in range(0, NCH, 4):
                    def t_f2(te, cc2=cc2, half=half, w=llv2, vv=sve.v):
                        te.wait_ge(sve.sem, max(w, vv))
                        for j in range(4):
                            te.transpose(PB2b[:, 128 * j:128 * (j + 1)],
                                         firslab[:, 128 * (cc2 + j):128 * (cc2 + j + 1)],
                                         ident[:])
                        te.fin(spe.sem, 1)
                    em("tensor", t_f2)
                    pv = spe.inc()
                    vseq([lambda ve, cc2=cc2, half=half: ve.tensor_copy(
                        ll_t[:, cc2:cc2 + 4, 128 * half:128 * (half + 1)],
                        PB2b[:, 0:512].rearrange("p (a b) -> p a b", a=4))],
                        waits=[(spe, pv)])
            fh_v = vseq([lambda ve: ve.tensor_copy(vfm[:, 0, 0:64], vfm[:, 0, TILE:TILE + 64]),
                         lambda ve: ve.tensor_copy(vfm[:, 1, 0:64], vfm[:, 1, TILE:TILE + 64])],
                        waits=[(sve, sve.v)])
            FIR_V = sve.v

            sq2 = hala[:, 14560:16608]
            st_ops = []
            for bi, br in enumerate([ls_t, ll_t, dout_t, vtok]):
                br3 = br
                sv_sq = sseq([lambda s, br3=br3: s.activation(
                    sq2.rearrange("p (a b) -> p a b", a=NCH), br3[:], AF.Square)],
                    waits=[(sve, sve.v), (ssc, ssc.v)])
                stv = vseq([
                    lambda ve, br3=br3: ve.reduce_sum(red_a[:, 0:NCH], br3[:], AX.X),
                    lambda ve, br3=br3: ve.reduce_sum(red_c[:, 0:NCH], br3[:], AX.X,
                                                      apply_absolute_value=True),
                    lambda ve: ve.reduce_sum(red_b[:, 0:NCH],
                                             sq2.rearrange("p (a b) -> p a b", a=NCH), AX.X),
                ], waits=[(ssc, sv_sq), (sve, sve.v)])
                s_as = vseq([
                    lambda ve, bi=bi: ve.tensor_scalar_mul(
                        stats_c[:, :, 4 * bi + 0], red_a[:, 0:NCH], 1.0 / 256),
                    lambda ve, bi=bi: ve.tensor_scalar_mul(
                        stats_c[:, :, 4 * bi + 2], red_c[:, 0:NCH], 1.0 / 256),
                    lambda ve, bi=bi: ve.scalar_tensor_tensor(
                        stats_c[:, :, 4 * bi + 1], stats_c[:, :, 4 * bi + 0], -1.0,
                        stats_c[:, :, 4 * bi + 0], ALU.mult, ALU.mult),
                    lambda ve, bi=bi: ve.scalar_tensor_tensor(
                        stats_c[:, :, 4 * bi + 1], red_b[:, 0:NCH], 1.0 / 256,
                        stats_c[:, :, 4 * bi + 1], ALU.mult, ALU.add),
                ], waits=[(sve, stv)])
                sq_l2 = sseq([lambda s, bi=bi: s.activation(
                    stats_c[:, :, 4 * bi + 3], red_b[:, 0:NCH], AF.Sqrt)],
                    waits=[(sve, s_as)])
                st_ops.append(sq_l2)
            STATS_V = ssc.v
            for cc in range(NCH):
                cst_v = vseq([lambda ve, cc=cc: ve.tensor_copy(
                    uadj[0:128, 0:16], stats_c[:, cc, :])],
                    waits=[(ssc, STATS_V), (sve, sve.v)])
                def t_st(te, cc=cc, w=cst_v):
                    te.wait_ge(sve.sem, w)
                    te.transpose(PB2b[0:16, 0:128], uadj[0:128, 0:16], ident[:])
                    te.fin(spe.sem, 1)
                em("tensor", t_st)
                pv = spe.inc()
                vseq([lambda ve, cc=cc: ve.tensor_copy(
                    statsfm[0:16, 128 * cc:128 * (cc + 1)], PB2b[0:16, 0:128])],
                    waits=[(spe, pv)])
            STF_V = sve.v

            ghid = sqscr
            prev_ev = [None, None]
            for mi in range(8):
                m = 8 + mi
                pb = PB0 if mi % 2 == 0 else PB1
                def t_g(te, m=m, pb=pb, w=STF_V, we=prev_ev[mi % 2], xv=XFM_V):
                    te.wait_ge(sve.sem, max(w, xv))
                    if we is not None:
                        te.wait_ge(spe.sem, we)
                    for kt in range(8):
                        for sub in range(2):
                            te.matmul(pb[:, 512 * sub:512 * (sub + 1)],
                                      wstk[:, kt, 128 * m:128 * (m + 1)],
                                      xfm[:, kt, 512 * sub:512 * (sub + 1)],
                                      start=(kt == 0), stop=False)
                    for sub in range(2):
                        te.matmul(pb[:, 512 * sub:512 * (sub + 1)],
                                  gw1st[:, 128 * (m - 8):128 * (m - 7)],
                                  statsfm[:, 512 * sub:512 * (sub + 1)],
                                  start=False, stop=True)
                    te.fin(spe.sem, 1)
                em("tensor", t_g)
                pv = spe.inc()
                gel_v = sseq([lambda s, pb=pb, m=m: s.activation(
                    ghid[:, 0:TILE], pb[:, 0:TILE], AF.Gelu, bias=svec[:, m:m + 1])],
                    waits=[(spe, pv)])
                def t_l(te, mi=mi, w=gel_v):
                    te.wait_ge(ssc.sem, w)
                    for sub in range(2):
                        te.matmul(PB2[0:4, 512 * sub:512 * (sub + 1)],
                                  gw2t[:, 4 * mi:4 * (mi + 1)], ghid[:, 512 * sub:512 * (sub + 1)],
                                  start=(mi == 0), stop=(mi == 7))
                    te.fin(spe.sem, 1)
                em("tensor", t_l)
                prev_ev[mi % 2] = spe.inc()
            LOG_PE = spe.v
            lg_v = vseq([lambda ve: ve.tensor_scalar(
                scr1[0:4, 0:TILE], PB2[0:4, 0:TILE], gb2col[0:4, 0:1],
                tauinvcol[0:4, 0:1], ALU.add, ALU.mult)], waits=[(spe, LOG_PE)])
            def t_lt(te, w=lg_v):
                te.wait_ge(sve.sem, w)
                for cc in range(NCH):
                    te.transpose(PB2b[:, 4 * cc:4 * (cc + 1)],
                                 scr1[0:4, 128 * cc:128 * (cc + 1)], ident[0:4, 0:4])
                te.fin(spe.sem, 1)
            em("tensor", t_lt)
            pv = spe.inc()
            pl_v = vseq([lambda ve: ve.tensor_copy(
                probst[:].rearrange("p a b -> p (a b)"), PB2b[:, 0:32])],
                waits=[(spe, pv)])
            p3 = probst[:]
            sm1_v = vseq([lambda ve: ve.reduce_max(red_a[:, 0:NCH], p3, AX.X)],
                         waits=[(sve, pl_v)])
            sm2_v = vseq([lambda ve: ve.tensor_sub(
                p3, p3, red_a[:, 0:NCH].unsqueeze(2).broadcast_to([128, NCH, 4]))],
                waits=[(sve, sm1_v)])
            sm3_v = sseq([lambda s: s.activation(
                probst[:].rearrange("p a b -> p (a b)"),
                probst[:].rearrange("p a b -> p (a b)"), AF.Exp)], waits=[(sve, sm2_v)])
            sm4_v = vseq([lambda ve: ve.reduce_sum(red_a[:, 0:NCH], p3, AX.X),
                          lambda ve: ve.reciprocal(red_a[:, 0:NCH], red_a[:, 0:NCH]),
                          lambda ve: ve.tensor_mul(
                              p3, p3, red_a[:, 0:NCH].unsqueeze(2).broadcast_to([128, NCH, 4])),
                          lambda ve: ve.tensor_scalar(
                              p3, p3, gate_c[:, 8:9], None, ALU.mult),
                          lambda ve: ve.tensor_add(
                              p3, p3, gate_c[:, 4:8].unsqueeze(1).broadcast_to([128, NCH, 4])),
                          lambda ve: ve.tensor_scalar(
                              probst[:, :, 0:1], probst[:, :, 0:1], gate_c[:, 9:10],
                              None, ALU.add)],
                         waits=[(ssc, sm3_v)])
            PROBS_V = sm4_v

            prev_rms = ssc.v
            for cc in range(NCH):
                C = slice(128 * cc, 128 * (cc + 1))
                def t_id(te, cc=cc, C=C, w=PROBS_V, vv=sve.v, sw=prev_rms):
                    te.wait_ge(sve.sem, max(w, vv))
                    te.wait_ge(ssc.sem, sw)
                    te.transpose(PB2b[:, 0:128], idfm[:, 0, C], ident[:])
                    te.transpose(PB2b[:, 128:256], idfm[:, 1, C], ident[:])
                    te.fin(spe.sem, 1)
                em("tensor", t_id)
                pv = spe.inc()
                mx_v = vseq([
                    lambda ve, cc=cc: ve.tensor_copy(id_t[:, 0:256], PB2b[:, 0:256]),
                    lambda ve, cc=cc: ve.tensor_scalar(
                        uadj[:, 0:256], ls_t[:, cc, :], probst[:, cc, 0:1], None, ALU.mult),
                    lambda ve, cc=cc: ve.scalar_tensor_tensor(
                        uadj[:, 0:256], ll_t[:, cc, :], probst[:, cc, 1:2],
                        uadj[:, 0:256], ALU.mult, ALU.add),
                    lambda ve, cc=cc: ve.scalar_tensor_tensor(
                        uadj[:, 0:256], dout_t[:, cc, :], probst[:, cc, 2:3],
                        uadj[:, 0:256], ALU.mult, ALU.add),
                    lambda ve, cc=cc: ve.scalar_tensor_tensor(
                        uadj[:, 0:256], vtok[:, cc, :], probst[:, cc, 3:4],
                        uadj[:, 0:256], ALU.mult, ALU.add),
                    lambda ve, cc=cc: ve.scalar_tensor_tensor(
                        ls_t[:, cc, :], id_t[:, 0:256], gate_c[:, 10:11],
                        uadj[:, 0:256], ALU.mult, ALU.add),
                ], waits=[(spe, pv)])
                rms_v = sseq([lambda s, cc=cc: s.activation(
                    uadj[:, 0:256], ls_t[:, cc, :], AF.Square,
                    accum_out=mixc[:, cc:cc + 1])], waits=[(sve, mx_v)])
                prev_rms = rms_v
            RMSS_V = ssc.v
            rmsc_v = sseq([lambda s: s.activation(
                mixc[:, 0:NCH], mixc[:, 0:NCH], AF.Sqrt, bias=epst[:, 0:1],
                scale=1.0 / 256)], waits=[(ssc, RMSS_V), (sve, sve.v)])
            rmsr_v = vseq([lambda ve: ve.reciprocal(mixc[:, 0:NCH], mixc[:, 0:NCH])],
                          waits=[(ssc, rmsc_v)])
            for cc in range(NCH):
                o_v = vseq([
                    lambda ve, cc=cc: ve.tensor_scalar(
                        ls_t[:, cc, :], ls_t[:, cc, :], mixc[:, cc:cc + 1], None, ALU.mult),
                    lambda ve, cc=cc: ve.tensor_mul(ls_t[:, cc, :], ls_t[:, cc, :], onw_t[:]),
                ], waits=[(sve, rmsr_v if cc == 0 else sve.v)])
                def t_of(te, cc=cc, w=sve.v):
                    te.wait_ge(sve.sem, w)
                    te.transpose(PB2b[:, 0:128], ls_t[:, cc, 0:128], ident[:])
                    te.transpose(PB2b[:, 128:256], ls_t[:, cc, 128:256], ident[:])
                    te.fin(spe.sem, 1)
                em("tensor", t_of)
                pv = spe.inc()
                vseq([lambda ve, cc=cc: ve.tensor_copy(ofm[:, 0, 128 * cc:128 * (cc + 1)],
                                                       PB2b[:, 0:128]),
                      lambda ve, cc=cc: ve.tensor_copy(ofm[:, 1, 128 * cc:128 * (cc + 1)],
                                                       PB2b[:, 128:256])],
                     waits=[(spe, pv)])
            OFM_V = sve.v
            def sy_o(e, t_i=t_i, w=OFM_V, pw=sst.v):
                e.wait_ge(sve.sem, w)
                e.wait_ge(sst.sem, pw)
                e.dma_start(out=DT["osrc"][:, TILE * t_i:TILE * (t_i + 1)],
                            in_=ofm[:, 0, :]).then_inc(sst.sem, 16)
                e.dma_start(out=DT["osrc"][:, L + TILE * t_i:L + TILE * (t_i + 1)],
                            in_=ofm[:, 1, :]).then_inc(sst.sem, 16)
            em("sync", sy_o)
            sst.inc(32)
            return dict(ld_v=ld_v, XFM_V=XFM_V, PROJ_V=PROJ_V, CONV_V=CONV_V,
                        bcol_v=bcol_v, rq_v=rq_v, kh_v=kh_v, halo_v=halo_v,
                        OFM_V=OFM_V)

        if KSTOP >= 3:
            vseq([lambda ve: ve.memset(qraw[:, 0, 0:8], 0.0),
                  lambda ve: ve.memset(qraw[:, 1, 0:8], 0.0),
                  lambda ve: ve.memset(kraw[:, 0, 0:8], 0.0),
                  lambda ve: ve.memset(kraw[:, 1, 0:8], 0.0),
                  lambda ve: ve.memset(vraw[:, 0, 0:8], 0.0),
                  lambda ve: ve.memset(vraw[:, 1, 0:8], 0.0),
                  lambda ve: ve.memset(vfm[:, 0, 0:64], 0.0),
                  lambda ve: ve.memset(vfm[:, 1, 0:64], 0.0),
                  lambda ve: ve.memset(S_st[:].rearrange("p a b -> p (a b)"), 0.0)])

        if KSTOP >= 3:
            for _t in range(NTILE):
                tileA(_t)
                PRE_V = em_pre(PRE_JOBS[8 * _t:8 * (_t + 1)])
            dbg_store("dout", dout_t, waits=[(sve, sve.v), (ssc, ssc.v)])
            dbg_store("stats", stats_c, waits=[(ssc, ssc.v), (sve, sve.v)])
            dbg_store("probs", probst[:], waits=[(sve, sve.v)])
            dbg_store("ofm", ofm, waits=[(sve, sve.v)])
            if "osrcL" in DEBUG:
                def sy_oL(e, w=sst.v):
                    e.wait_ge(sst.sem, w)
                    e.dma_start(out=DT["dbg_osrcL"], in_=DT["osrc"][:, :]).then_inc(sst.sem, 16)
                em("sync", sy_oL)
                sst.inc(16)
            dbg_store("qfm", qfm, waits=[(ssc, ssc.v), (sve, sve.v)])
            dbg_store("kfm", kfm, waits=[(sve, sve.v)])
            dbg_store("vfm", vfm[:, :, 64:64 + TILE], waits=[(ssc, ssc.v)])
            dbg_store("idfm", idfm, waits=[(ssc, ssc.v)])
            dbg_store("beta", betarow[:], waits=[(ssc, ssc.v)])
            dbg_store("rnq", rnqcol[:, 0:NCH], waits=[(sve, sve.v)])
            dbg_store("rnk", rnkrow[:], waits=[(sve, sve.v)])


        if KSTOP >= 4:
            NOCC = False
            def g_cc(gp, w=sst.v, NOCC=NOCC):
                gp.wait_ge(sst.sem, w)
                if NOCC:
                    for j in range(4):
                        gp.dma_start(out=DT["odst"][j], in_=DT["osrc"][:, :]).then_inc(scc.sem, 16)
                else:
                    gp.collective_compute(
                        "AllGather", ALU.bypass, ins=[DT["osrc"]], outs=[DT["odst"]],
                        replica_groups=[[0, 1, 2, 3], [4, 5, 6, 7]],
                    ).then_inc(scc.sem, 1)
            em("gpsimd", g_cc)
            CC_V = scc.inc(64 if NOCC else 1)

            wo_b = wqbuf[:, 0:8192].rearrange("p (a b) -> p a b", a=8)
            def g_wo(gp, w=sve.v):
                gp.wait_ge(sve.sem, w)
                for kt in range(8):
                    gp.dma_start(out=wo_b[:, kt, :],
                                 in_=DT["wo"][128 * kt:128 * (kt + 1), :]).then_inc(sdm.sem, 16)
            em("gpsimd", g_wo)
            WO_L = sdm.inc(16 * 8)

            def g_oall(gp, w=CC_V):
                gp.wait_ge(scc.sem, w)
                qreg = gp.value_load(cidt[0:1, 0:1])
                r_off = (qreg % 4) * NTOK
                for j in range(4):
                    for d2 in range(2):
                        gp.dma_start(
                            out=oall[:, 2 * j + d2, :],
                            in_=DT["odst"][j][:, d2 * L:(d2 + 1) * L]
                            [:, bass.ds(r_off, NTOK)]
                        ).then_inc(sdm.sem, 16)
            em("gpsimd", g_oall)
            OALL_L = sdm.inc(16 * 8)

            x2f = x2c.bitcast(F32).rearrange("p a b -> p (a b)")
            h2c = big3[:, 4608:5632]
            sc_b = big3[0:8, 5632:6144]
            hsf = big3[:, 6144:8192].bitcast(F32)
            wg_b = big3[:, 8192:8256].rearrange("p (a b) -> p a b", a=8)
            wgc_v = vseq([lambda ve: ve.tensor_copy(big3[:, 8192:8256], wgt[:])],
                waits=[(sve, sve.v)])
            prev_h2dma = ssy.v
            for n in range(NCHB):
                def t_wo(te, n=n, w=OALL_L, w2=WO_L, vv=sve.v):
                    te.wait_ge(sdm.sem, max(w, w2))
                    te.wait_ge(sve.sem, vv)
                    for kt in range(8):
                        for sub in range(2):
                            te.matmul(PB0[:, 512 * sub:512 * (sub + 1)],
                                      oall[:, kt, 128 * n:128 * (n + 1)],
                                      wo_b[:, kt, 512 * sub:512 * (sub + 1)],
                                      start=(kt == 0), stop=(kt == 7))
                    te.fin(spe.sem, 1)
                em("tensor", t_wo)
                pv = spe.inc()
                def g_hs(gp, n=n, w=sve.v, pd=sdm.v):
                    gp.wait_ge(sve.sem, w)
                    gp.wait_ge(sdm.sem, pd)
                    gp.dma_start(out=hsf[:, 0:1024],
                                 in_=DT["hs_slice"][128 * n:128 * (n + 1), :]).then_inc(sdm.sem, 16)
                em("gpsimd", g_hs)
                hs_l = sdm.inc(16)
                x2_v = vseq([
                    lambda ve: ve.tensor_mul(x2f[:, 0:1024], PB0[:, 0:1024], ga_t[:]),
                    lambda ve: ve.tensor_add(x2f[:, 0:1024], x2f[:, 0:1024], hsf[:, 0:1024]),
                    lambda ve: ve.reduce_sum(red_a[:, 0:1], x2f[:, 0:1024], AX.X),
                ], waits=[(spe, pv), (sdm, hs_l)])
                def sy_x2(e, n=n, w=x2_v, pw=prev_h2dma):
                    e.wait_ge(sve.sem, w)
                    e.wait_ge(ssy.sem, pw)
                    e.dma_start(out=DT["x2buf"][128 * n:128 * (n + 1), :],
                                in_=x2f[:, 0:1024]).then_inc(ssy.sem, 16)
                em("sync", sy_x2)
                x2dma = ssy.inc(16)
                sq_v2 = sseq([lambda s: s.activation(h2c[:, 0:1024], x2f[:, 0:1024],
                                                     AF.Square, accum_out=red_b[:, 0:1])],
                             waits=[(sve, x2_v), (ssy, prev_h2dma)])
                ln_v2 = vseq([
                    lambda ve: ve.tensor_scalar_mul(red_a[:, 0:1], red_a[:, 0:1], 1.0 / H),
                    lambda ve: ve.tensor_scalar_mul(red_b[:, 0:1], red_b[:, 0:1], 1.0 / H),
                    lambda ve: ve.tensor_mul(red_c[:, 0:1], red_a[:, 0:1], red_a[:, 0:1]),
                    lambda ve: ve.tensor_sub(red_b[:, 0:1], red_b[:, 0:1], red_c[:, 0:1]),
                ], waits=[(ssc, sq_v2)])
                rs_v2 = sseq([lambda s: s.activation(red_b[:, 0:1], red_b[:, 0:1],
                                                     AF.Sqrt, bias=epst[:, 0:1])],
                             waits=[(sve, ln_v2)])
                h2_v = vseq([
                    lambda ve: ve.reciprocal(red_b[:, 0:1], red_b[:, 0:1]),
                    lambda ve: ve.tensor_scalar(x2f[:, 0:1024], x2f[:, 0:1024],
                                                red_a[:, 0:1], red_b[:, 0:1],
                                                ALU.subtract, ALU.mult),
                    lambda ve: ve.tensor_mul(h2c[:, 0:1024], x2f[:, 0:1024], cb1_t[:]),
                    lambda ve: ve.tensor_add(h2c[:, 0:1024], h2c[:, 0:1024], sbt_t[:]),
                ], waits=[(ssc, rs_v2), (ssy, x2dma)])
                def sy_h2(e, n=n, w=sve.v, pw=ssy.v):
                    e.wait_ge(sve.sem, w)
                    e.wait_ge(ssy.sem, pw)
                    e.dma_start(out=DT["h2buf"][128 * n:128 * (n + 1), :],
                                in_=h2c[:, 0:1024]).then_inc(ssy.sem, 16)
                em("sync", sy_h2)
                prev_h2dma = ssy.inc(16)
                def t_h2(te, n=n, w=sve.v, wg_w=wgc_v):
                    te.wait_ge(sve.sem, max(w, wg_w))
                    for kt in range(8):
                        te.transpose(PB2b[:, 128 * kt:128 * (kt + 1)],
                                     h2c[:, 128 * kt:128 * (kt + 1)], ident[:])
                    te.fin(spe.sem, 1)
                em("tensor", t_h2)
                pv2 = spe.inc()
                h2f_v = vseq([lambda ve: ve.tensor_copy(big3[:, 8256:9280], PB2b[:, 0:1024])],
                             waits=[(spe, pv2)])
                h2fm = big3[:, 8256:9280].rearrange("p (a b) -> p a b", a=8)
                def t_sc(te, n=n, w=h2f_v):
                    te.wait_ge(sve.sem, w)
                    for kt in range(8):
                        te.matmul(PB3[0:8, 128 * (n % 4):128 * (n % 4 + 1)],
                                  wg_b[:, kt, :], h2fm[:, kt, :],
                                  start=(kt == 0), stop=(kt == 7))
                    te.fin(spe.sem, 1)
                em("tensor", t_sc)
                pv3 = spe.inc()
                if n % 4 == 3:
                    se_v = vseq([lambda ve: ve.tensor_copy(sc_b, PB3[0:8, 0:512])],
                                waits=[(spe, pv3)])
                    def t_sct(te, n=n, w=se_v):
                        te.wait_ge(sve.sem, w)
                        for j in range(4):
                            te.transpose(PB2b[:, 8 * j:8 * (j + 1)],
                                         sc_b[:, 128 * j:128 * (j + 1)], ident[0:8, 0:8])
                        te.fin(spe.sem, 1)
                    em("tensor", t_sct)
                    pv4 = spe.inc()
                    vseq([lambda ve, n=n: ve.tensor_copy(
                        scrt[:, n - 3:n + 1, :].rearrange("p a b -> p (a b)"),
                        PB2b[:, 0:32])], waits=[(spe, pv4)])
            X2_done = sve.v
            dbg_store("x2", x2f[:, 0:1024], waits=[(sve, sve.v)])
            dbg_store("scores", scrt[:], waits=[(sve, sve.v)])

        if KSTOP >= 5:
            PE_B1 = spe.v
            rt = big3[:, 0:8192].bitcast(F32)

            def rtv(off, shape):
                n = int(np.prod(shape))
                return rt[:, off:off + n].rearrange("p (a b) -> p a b", a=shape[0])
            ohA = rtv(0, (NCHB, 8))
            ohB = rtv(128, (NCHB, 8))
            Rt3 = rtv(256, (NCHB, 8))
            stl = rtv(384, (NCHB, 8))
            iotaBC = iotaEt[:].unsqueeze(1).broadcast_to([128, NCHB, 8])
            mo_s = big3[:, 4608:5632]

            z_v = vseq([lambda ve: ve.memset(zerot, 0.0),
                        lambda ve: ve.memset(idx16[:].rearrange("p a b -> p (a b)"), 0)])

            def g_zero(gp, w=z_v, pd=sdm.v):
                gp.wait_ge(sve.sem, w)
                gp.wait_ge(sdm.sem, pd)
                bkf = DT["bucket"].rearrange("a b -> (a b)").rearrange(
                    "(p f) -> p f", p=128)
                for j in range(10):
                    gp.dma_start(out=bkf[:, 4096 * j:4096 * (j + 1)],
                                 in_=zerot).then_inc(sdm.sem, 16)
            em("gpsimd", g_zero)
            ZERO_L = sdm.inc(160)

            va = vseq([
                lambda ve: ve.reduce_max(red_a[:, 0:NCHB], scrt[:], AX.X),
                lambda ve: ve.tensor_tensor(
                    ohA, scrt[:],
                    red_a[:, 0:NCHB].unsqueeze(2).broadcast_to([128, NCHB, 8]),
                    ALU.is_equal),
                lambda ve: ve.tensor_mul(stl, ohA, iotaBC),
                lambda ve: ve.reduce_sum(red_b[:, 0:NCHB], stl, AX.X),
                lambda ve: ve.tensor_copy(routt[:, :, 0:1],
                                          red_b[:, 0:NCHB].unsqueeze(2)),
                lambda ve: ve.tensor_scalar_mul(stl, ohA, -1e9),
                lambda ve: ve.tensor_add(scrt[:], scrt[:], stl),
                lambda ve: ve.reduce_max(red_c[:, 0:NCHB], scrt[:], AX.X),
                lambda ve: ve.tensor_tensor(
                    ohB, scrt[:],
                    red_c[:, 0:NCHB].unsqueeze(2).broadcast_to([128, NCHB, 8]),
                    ALU.is_equal),
                lambda ve: ve.tensor_mul(stl, ohB, iotaBC),
                lambda ve: ve.reduce_sum(red_b[:, 0:NCHB], stl, AX.X),
                lambda ve: ve.tensor_copy(routt[:, :, 1:2],
                                          red_b[:, 0:NCHB].unsqueeze(2)),
                lambda ve: ve.tensor_sub(red_b[:, 0:NCHB], red_a[:, 0:NCHB],
                                         red_c[:, 0:NCHB]),
            ], waits=[(sve, sve.v), (sst, sst.v)])
            sg = sseq([lambda s: s.activation(red_a[:, 0:NCHB], red_b[:, 0:NCHB],
                                              AF.Sigmoid)], waits=[(sve, va)])
            vb = vseq([
                lambda ve: ve.tensor_copy(routt[:, :, 4:5],
                                          red_a[:, 0:NCHB].unsqueeze(2)),
                lambda ve: ve.tensor_scalar_mul(red_b[:, 0:NCHB],
                                                red_b[:, 0:NCHB], -1.0),
            ], waits=[(ssc, sg)])
            sg2 = sseq([lambda s: s.activation(red_a[:, 0:NCHB], red_b[:, 0:NCHB],
                                               AF.Sigmoid)], waits=[(sve, vb)])
            vg2 = vseq([lambda ve: ve.tensor_copy(routt[:, :, 5:6],
                                                  red_a[:, 0:NCHB].unsqueeze(2))],
                       waits=[(ssc, sg2)])

            def t_cs(te, w=vg2):
                te.wait_ge(sve.sem, w)
                te.matmul(PB3[:, 0:128], tricum[:], rt[:, 0:128], start=True, stop=True)
                te.matmul(PB3[:, 128:256], tricum[:], rt[:, 128:256], start=True, stop=True)
                te.matmul(PB3[0:1, 256:384], onescol[:], rt[:, 0:128], start=True, stop=True)
                te.matmul(PB3[0:1, 384:512], onescol[:], rt[:, 128:256], start=True, stop=True)
                te.fin(spe.sem, 1)
            em("tensor", t_cs)
            pcs = spe.inc()
            ev1 = vseq([
                lambda ve: ve.tensor_copy(csum1[:].rearrange("p a b -> p (a b)"),
                                          PB3[:, 0:128]),
                lambda ve: ve.tensor_copy(csum2[:].rearrange("p a b -> p (a b)"),
                                          PB3[:, 128:256]),
                lambda ve: ve.tensor_copy(rowscr8[0:1, 0:128], PB3[0:1, 256:384]),
                lambda ve: ve.tensor_copy(rowscr8[0:1, 128:256], PB3[0:1, 384:512]),
            ], waits=[(spe, pcs)])

            def sy_tot(e, w=ev1, pw=ssy.v):
                e.wait_ge(sve.sem, w)
                e.wait_ge(ssy.sem, pw)
                e.dma_start(out=DT["rsc"], in_=rowscr8[0:1, 0:256]).then_inc(ssy.sem, 16)
            em("sync", sy_tot)
            t1 = ssy.inc(16)

            def sy_tot2(e, w=t1):
                e.wait_ge(ssy.sem, w)
                e.dma_start(out=totE[0:16, 0:8],
                            in_=DT["rsc"][0:1, 0:128].rearrange(
                                "o (a b) -> (o a) b", a=16)).then_inc(ssy.sem, 16)
            em("sync", sy_tot2)
            t2 = ssy.inc(16)

            def sy_tot3(e, w=t2):
                e.wait_ge(ssy.sem, w)
                e.dma_start(out=totE[0:16, 8:16],
                            in_=DT["rsc"][0:1, 128:256].rearrange(
                                "o (a b) -> (o a) b", a=16)).then_inc(ssy.sem, 16)
            em("sync", sy_tot3)
            t3 = ssy.inc(16)

            def t_pf(te, w=t3):
                te.wait_ge(ssy.sem, w)
                te.matmul(PB1[0:16, 0:16], tri16t[:], totE[:], start=True, stop=True)
                te.matmul(PB1[0:1, 16:32], onescol[0:16, 0:1], totE[:],
                          start=True, stop=True)
                te.fin(spe.sem, 1)
            em("tensor", t_pf)
            ppf = spe.inc()
            ev2 = vseq([
                lambda ve: ve.tensor_copy(coffs[:], PB1[0:16, 0:16]),
                lambda ve: ve.tensor_copy(cntrow[:], PB1[0:1, 16:32]),
            ], waits=[(spe, ppf)])

            def sy_cnt(e, w=ev2, pw=ssy.v):
                e.wait_ge(sve.sem, w)
                e.wait_ge(ssy.sem, pw)
                e.dma_start(out=DT["cnt_src"], in_=cntrow[:]).then_inc(ssy.sem, 16)
            em("sync", sy_cnt)
            c1 = ssy.inc(16)
            NOCC = False

            def g_ccnt(gp, w=c1, NOCC=NOCC):
                gp.wait_ge(ssy.sem, w)
                if NOCC:
                    for j in range(8):
                        gp.dma_start(out=DT["cnt_dst"][j:j + 1, :],
                                     in_=DT["cnt_src"]).then_inc(scc.sem, 16)
                else:
                    gp.collective_compute(
                        "AllGather", ALU.bypass, ins=[DT["cnt_src"]],
                        outs=[DT["cnt_dst"]],
                        replica_groups=[[0, 1, 2, 3, 4, 5, 6, 7]],
                    ).then_inc(scc.sem, 1)
            em("gpsimd", g_ccnt)
            CC2 = scc.inc(128 if NOCC else 1)

            def sy_ag(e, w=CC2, pw=sld.v):
                e.wait_ge(scc.sem, w)
                e.wait_ge(sld.sem, pw)
                e.dma_start(out=agbuf[:], in_=DT["cnt_dst"]).then_inc(sld.sem, 16)
            em("sync", sy_ag)
            agl = sld.inc(16)

            def sy_cf(e, w=ev2, pw=t3):
                e.wait_ge(sve.sem, w)
                e.wait_ge(ssy.sem, max(pw, c1))
                e.dma_start(out=DT["rsc2"], in_=coffs[:]).then_inc(ssy.sem, 16)
            em("sync", sy_cf)
            cf1 = ssy.inc(16)

            def sy_cf2(e, w=cf1):
                e.wait_ge(ssy.sem, w)
                e.dma_start(out=rowscr8[0:1, 0:256],
                            in_=DT["rsc2"].rearrange("a b -> (a b)").rearrange(
                                "(o f) -> o f", o=1)).then_inc(ssy.sem, 16)
            em("sync", sy_cf2)
            cf2 = ssy.inc(16)

            def t_pf2(te, w=agl):
                te.wait_ge(sld.sem, w)
                te.matmul(PB1[0:1, 32:48], hotlt_t[:], agbuf[:], start=True, stop=True)
                te.matmul(PB1[0:1, 48:64], onescol[0:8, 0:1], agbuf[:],
                          start=True, stop=True)
                te.fin(spe.sem, 1)
            em("tensor", t_pf2)
            pp2 = spe.inc()
            ev3 = vseq([
                lambda ve: ve.tensor_copy(rowscr8[0:1, 256:272], PB1[0:1, 32:48]),
                lambda ve: ve.tensor_copy(rowscr8[0:1, 272:280], PB1[0:1, 48:56]),
                lambda ve: ve.tensor_copy(rowscr8[0:1, 280:288], cntrow[0:1, 0:8]),
            ], waits=[(spe, pp2), (ssy, cf2)])

            def t_bc(te, w=ev3, wd=cf2):
                te.wait_ge(sve.sem, w)
                te.wait_ge(ssy.sem, wd)
                te.matmul(PB2[:, 0:256], onesrow[:], rowscr8[0:1, 0:256],
                          start=True, stop=True)
                te.matmul(PB2[:, 256:288], onesrow[:], rowscr8[0:1, 256:288],
                          start=True, stop=True)
                te.fin(spe.sem, 1)
            em("tensor", t_bc)
            pbc = spe.inc()
            coT = PB2[:, 0:256].rearrange("p (a b) -> p a b", a=16)

            ev4 = vseq([
                lambda ve: ve.tensor_copy(ptile[:], PB2[:, 256:288]),
                lambda ve: ve.tensor_add(ptile[:, 8:16], ptile[:, 8:16],
                                         ptile[:, 16:24]),
                lambda ve: ve.tensor_add(Rt3, csum1[:], coT[:, :, 0:8]),
                lambda ve: ve.tensor_mul(stl, ohA, Rt3),
                lambda ve: ve.reduce_sum(red_b[:, 0:NCHB], stl, AX.X),
                lambda ve: ve.tensor_scalar(red_c[:, 0:NCHB].unsqueeze(2),
                                            routt[:, :, 0:1], 640.0, -1.0,
                                            ALU.mult, ALU.add),
                lambda ve: ve.tensor_add(slot_f[:, 0, :], red_c[:, 0:NCHB],
                                         red_b[:, 0:NCHB]),
                lambda ve: ve.tensor_add(
                    Rt3, Rt3, ptile[:, 0:8].unsqueeze(1).broadcast_to([128, NCHB, 8])),
                lambda ve: ve.tensor_mul(stl, ohA, Rt3),
                lambda ve: ve.reduce_sum(red_b[:, 0:NCHB], stl, AX.X),
                lambda ve: ve.tensor_copy(routt[:, :, 2:3],
                                          red_b[:, 0:NCHB].unsqueeze(2)),
                lambda ve: ve.tensor_scalar(red_b[:, 0:NCHB], red_b[:, 0:NCHB],
                                            4096.5, None, ALU.is_lt),
                lambda ve: ve.tensor_mul(routt[:, :, 4:5], routt[:, :, 4:5],
                                         red_b[:, 0:NCHB].unsqueeze(2)),
                lambda ve: ve.tensor_add(Rt3, csum2[:], coT[:, :, 8:16]),
                lambda ve: ve.tensor_add(
                    Rt3, Rt3, ptile[:, 24:32].unsqueeze(1).broadcast_to([128, NCHB, 8])),
                lambda ve: ve.tensor_mul(stl, ohB, Rt3),
                lambda ve: ve.reduce_sum(red_b[:, 0:NCHB], stl, AX.X),
                lambda ve: ve.tensor_scalar(red_c[:, 0:NCHB].unsqueeze(2),
                                            routt[:, :, 1:2], 640.0, -1.0,
                                            ALU.mult, ALU.add),
                lambda ve: ve.tensor_add(slot_f[:, 1, :], red_c[:, 0:NCHB],
                                         red_b[:, 0:NCHB]),
                lambda ve: ve.tensor_sub(ptile[:, 16:24], ptile[:, 8:16],
                                         ptile[:, 24:32]),
                lambda ve: ve.tensor_add(
                    Rt3, Rt3, ptile[:, 16:24].unsqueeze(1).broadcast_to([128, NCHB, 8])),
                lambda ve: ve.tensor_mul(stl, ohB, Rt3),
                lambda ve: ve.reduce_sum(red_b[:, 0:NCHB], stl, AX.X),
                lambda ve: ve.tensor_copy(routt[:, :, 3:4],
                                          red_b[:, 0:NCHB].unsqueeze(2)),
                lambda ve: ve.tensor_scalar(red_b[:, 0:NCHB], red_b[:, 0:NCHB],
                                            4096.5, None, ALU.is_lt),
                lambda ve: ve.tensor_mul(routt[:, :, 5:6], routt[:, :, 5:6],
                                         red_b[:, 0:NCHB].unsqueeze(2)),
                lambda ve: ve.tensor_scalar_min(
                    slot_f[:].rearrange("p a b -> p (a b)"),
                    slot_f[:].rearrange("p a b -> p (a b)"), float(E * CAPB - 1)),
                lambda ve: ve.tensor_copy(routt[:, :, 6:7],
                                          slot_f[:, 0, :].unsqueeze(2)),
                lambda ve: ve.tensor_copy(routt[:, :, 7:8],
                                          slot_f[:, 1, :].unsqueeze(2)),
            ], waits=[(spe, pbc)])
            dbg_store("route", routt[:, :, 0:8], waits=[(sve, ev4)])

            def t_sel(te, w=ev4):
                te.wait_ge(sve.sem, w)
                for k in range(2):
                    for a in range(8):
                        te.matmul(PB1[0:16, 128 * k + 16 * a:128 * k + 16 * (a + 1)],
                                  selpermt[:, 16 * a:16 * (a + 1)], slot_f[:, k, :],
                                  start=True, stop=True)
                te.fin(spe.sem, 1)
            em("tensor", t_sel)
            psel = spe.inc()
            idxf = idx16[:].rearrange("p a b -> p (a b)")
            sel_ops = []
            for k in range(2):
                for a in range(8):
                    sel_ops.append(lambda ve, k=k, a=a: ve.tensor_copy(
                        idxf[0:16, 128 * k:128 * (k + 1)].rearrange(
                            "p (n s) -> p n s", s=8)[:, :, a:a + 1],
                        PB1[0:16, 128 * k + 16 * a:128 * k + 16 * (a + 1)].unsqueeze(2)))
            ev5 = vseq(sel_ops, waits=[(spe, psel)])

            def sy_rep(e, w=ev5, pw=ssy.v):
                e.wait_ge(sve.sem, w)
                e.wait_ge(ssy.sem, pw)
                e.dma_start(out=idx16[16:32, :, :],
                            in_=idx16[0:16, :, :]).then_inc(ssy.sem, 16)
            em("sync", sy_rep)
            REP_L = ssy.inc(16)

            if KSTOP >= 6:
                def sy_h2r(e, w=ssy.v, pe=PE_B1):
                    e.wait_ge(ssy.sem, w)
                    e.wait_ge(spe.sem, pe)
                    e.dma_start(out=big1[:, :, :],
                                in_=DT["h2buf"].rearrange("(c p) f -> p c f", p=128)
                                ).then_inc(ssy.sem, 16)
                em("sync", sy_h2r)
                H2R_L = ssy.inc(16)
                dbg_store("h2", big1[:].rearrange("p a b -> p (a b)"),
                          waits=[(ssy, H2R_L)])

                def g_scat(gp, wz=ZERO_L, wrep=REP_L, wh2=H2R_L, pd=sdm.v):
                    gp.wait_ge(sdm.sem, max(wz, pd))
                    gp.wait_ge(ssy.sem, max(wrep, wh2))
                    gp.dma_scatter_add(DT["bucket"], big1[:, :, :], idx16[:, 0, :],
                                       NTOK, NTOK, H).then_inc(sdm.sem, 16)
                em("gpsimd", g_scat)
                SC1 = sdm.inc(16)

                def g_scat2(gp, w=SC1):
                    gp.wait_ge(sdm.sem, w)
                    gp.dma_scatter_add(DT["bucket"], big1[:, :, :], idx16[:, 1, :],
                                       NTOK, NTOK, H).then_inc(sdm.sem, 16)
                em("gpsimd", g_scat2)
                SC2 = sdm.inc(16)

                ysb = b1f[:, 0:10240].bitcast(F32).rearrange("p (a b) -> p a b", a=5)
                idispv = idxdispt[:].rearrange("p (e c) -> p e c", e=E)

                def slab(s_):
                    return wqbuf[:, 8192 * s_:8192 * (s_ + 1)].rearrange(
                        "p (a b) -> p a b", a=8)
                lastfin = {0: PE_B1, 1: PE_B1}
                YEVAC_V = sve.v
                YDMA_L = sdm.v
                gw1done = {}
                pb_act = {0: None, 1: None}
                bank_ev = {0: None, 1: None}
                yacc_prev = {}
                for e_ in range(E):
                    def g_gx(gp, e_=e_, wsc=SC2, wpe=gw1done.get(e_ - 1), pd=sdm.v):
                        gp.wait_ge(sdm.sem, max(wsc, pd))
                        if wpe is not None:
                            gp.wait_ge(spe.sem, wpe)
                        gp.dma_gather(xe_b[:, :, :], DT["bucket"], idispv[:, e_, :],
                                      CAPB, CAPB, H, elem_step=H,
                                      transpose=True).then_inc(sdm.sem, 16)
                    em("gpsimd", g_gx)
                    GX = sdm.inc(16)

                    W1L = {}

                    def em_wload(which, e2, q2):
                        sl_ = q2 % 2

                        def g_w(gp, e2=e2, q2=q2, sl_=sl_, which=which,
                                wfree=lastfin[sl_], pd=sdm.v):
                            gp.wait_ge(spe.sem, wfree)
                            gp.wait_ge(sdm.sem, pd)
                            sb_ = slab(sl_)
                            for kk in range(8):
                                gp.dma_start(
                                    out=sb_[:, kk, :],
                                    in_=DT[which][e2, q2, 128 * kk:128 * (kk + 1), :]
                                ).then_inc(sdm.sem, 16)
                        em("gpsimd", g_w)
                        return sdm.inc(128)

                    W1L[0] = em_wload("w1r", e_, 0)
                    W1L[1] = em_wload("w1r", e_, 1)
                    for q in range(4):
                        sl_ = q % 2
                        for dd in range(8):
                            par = dd % 2
                            pb = PB3 if par == 0 else PB0

                            def t_w1(te, e_=e_, q=q, dd=dd, sl_=sl_, pb=pb,
                                     wl=W1L[q], wg=GX, wev=pb_act[par], wy=YEVAC_V):
                                te.wait_ge(sdm.sem, max(wl, wg))
                                if wev is not None:
                                    te.wait_ge(ssc.sem, wev)
                                te.wait_ge(sve.sem, wy)
                                sb_ = slab(sl_)
                                for kk in range(8):
                                    te.matmul(pb[:, 0:512],
                                              sb_[:, kk, 128 * dd:128 * (dd + 1)],
                                              xe_b[:, kk, 0:512],
                                              start=(kk == 0), stop=(kk == 7))
                                for kk in range(8):
                                    te.matmul(pb[:, 512:640],
                                              sb_[:, kk, 128 * dd:128 * (dd + 1)],
                                              xe_b[:, kk, 512:640],
                                              start=(kk == 0), stop=(kk == 7))
                                te.fin(spe.sem, 1)
                            em("tensor", t_w1)
                            pw1 = spe.inc()
                            act_v = sseq([lambda s, e_=e_, q=q, dd=dd, pb=pb:
                                          s.activation(hid_b[:, 8 * q + dd, :],
                                                       pb[:, 0:640], AF.Relu,
                                                       bias=b1c[:, e_, 8 * q + dd:8 * q + dd + 1])],
                                         waits=[(spe, pw1)])
                            pb_act[par] = act_v
                        lastfin[sl_] = spe.v
                        if q + 2 < 4:
                            W1L[q + 2] = em_wload("w1b", e_, q + 2)
                    gw1done[e_] = spe.v
                    ACT_LAST = ssc.v
                    def sy_b2(e2, e_=e_, pw=sld.v, wv=YEVAC_V):
                        e2.wait_ge(sld.sem, pw)
                        e2.wait_ge(sve.sem, wv)
                        e2.dma_start(out=rowsc[:], in_=DT["b2"][e_:e_ + 1, :]
                                     ).then_inc(sld.sem, 16)
                    em("sync", sy_b2)
                    b2l = sld.inc(16)
                    b2c_v = vseq([lambda ve: ve.tensor_copy(rowscb[:], rowsc[:])],
                                 waits=[(sld, b2l)])

                    def t_b2(te, w=b2c_v, wy=YEVAC_V):
                        te.wait_ge(sve.sem, max(w, wy))
                        te.matmul(PB2[:, 0:512], onesrowb[:], rowscb[0:1, 0:512],
                                  start=True, stop=True)
                        te.matmul(PB2[:, 512:1024], onesrowb[:], rowscb[0:1, 512:1024],
                                  start=True, stop=True)
                        te.fin(spe.sem, 1)
                    em("tensor", t_b2)
                    pb2v = spe.inc()
                    b2e_v = vseq([lambda ve: ve.tensor_copy(b2t[:], PB2[:, 0:1024])],
                                 waits=[(spe, pb2v)])

                    W2L = {0: em_wload("w2b", e_, 0), 1: em_wload("w2b", e_, 1)}
                    for q in range(4):
                        sl_ = q % 2
                        for st in range(5):
                            for sub in range(2):
                                bk = (st * 2 + sub) % 2
                                poff = 512 * bk

                                def t_w2(te, e_=e_, q=q, st=st, sub=sub, sl_=sl_,
                                         poff=poff, wl=W2L[q], whid=ACT_LAST,
                                         wb2=b2e_v, wev=bank_ev[bk], wy=YEVAC_V):
                                    te.wait_ge(sdm.sem, wl)
                                    te.wait_ge(ssc.sem, whid)
                                    te.wait_ge(sve.sem, max(wb2, wev or 0, wy))
                                    sb_ = slab(sl_)
                                    for kq in range(8):
                                        te.matmul(PB0[:, poff:poff + 512],
                                                  hid_b[:, 8 * q + kq, 128 * st:128 * (st + 1)],
                                                  sb_[:, kq, 512 * sub:512 * (sub + 1)],
                                                  start=(kq == 0), stop=(kq == 7))
                                    te.fin(spe.sem, 1)
                                em("tensor", t_w2)
                                pw2 = spe.inc()
                                if q == 0:
                                    bank_ev[bk] = vseq(
                                        [lambda ve, st=st, sub=sub, poff=poff:
                                         ve.tensor_add(ysb[:, st, 512 * sub:512 * (sub + 1)],
                                                       PB0[:, poff:poff + 512],
                                                       b2t[:, 512 * sub:512 * (sub + 1)])],
                                        waits=[(spe, pw2), (sdm, YDMA_L)])
                                else:
                                    bank_ev[bk] = vseq(
                                        [lambda ve, st=st, sub=sub, poff=poff:
                                         ve.tensor_add(ysb[:, st, 512 * sub:512 * (sub + 1)],
                                                       ysb[:, st, 512 * sub:512 * (sub + 1)],
                                                       PB0[:, poff:poff + 512])],
                                        waits=[(spe, pw2), (sve, yacc_prev[(st, sub)])])
                                yacc_prev[(st, sub)] = bank_ev[bk]
                        lastfin[sl_] = spe.v
                        if q + 2 < 4:
                            W2L[q + 2] = em_wload("w2b", e_, q + 2)
                    YEVAC_V = sve.v
                    def g_y(gp, e_=e_, w=YEVAC_V, pd=sdm.v):
                        gp.wait_ge(sve.sem, w)
                        gp.wait_ge(sdm.sem, pd)
                        gp.dma_start(out=DT["ybuf"][e_ * CAPB:(e_ + 1) * CAPB, :]
                                     .rearrange("(c p) f -> p c f", p=128),
                                     in_=ysb[:, :, :]).then_inc(sdm.sem, 16)
                    em("gpsimd", g_y)
                    YDMA_L = sdm.inc(16)

                gbv = plain_bcast(5120, gb_t)
                prev_out = sst.v
                for half in range(2):
                    def g_ret(gp, half=half, wpe=spe.v, wv=sve.v, pd=max(sdm.v, YDMA_L)):
                        gp.wait_ge(spe.sem, wpe)
                        gp.wait_ge(sve.sem, wv)
                        gp.wait_ge(sdm.sem, pd)
                        gp.dma_gather(g0h[:, :, :], DT["ybuf"],
                                      idx16[:, 0, 64 * half:64 * half + 64],
                                      1024, 1024, H, elem_step=H).then_inc(sdm.sem, 16)
                        gp.dma_gather(g1h[:, :, :], DT["ybuf"],
                                      idx16[:, 1, 64 * half:64 * half + 64],
                                      1024, 1024, H, elem_step=H).then_inc(sdm.sem, 16)
                    em("gpsimd", g_ret)
                    RET_L = sdm.inc(32)
                    for nn in range(8):
                        m = 8 * half + nn

                        def sy_x2r(e2, m=m, pw=ssy.v, wv=sve.v):
                            e2.wait_ge(ssy.sem, pw)
                            e2.wait_ge(sve.sem, wv)
                            e2.dma_start(out=hsf[:, 0:1024],
                                         in_=DT["x2buf"][128 * m:128 * (m + 1), :]
                                         ).then_inc(ssy.sem, 16)
                        em("sync", sy_x2r)
                        x2l = ssy.inc(16)
                        cmb = vseq([
                            lambda ve, nn=nn, m=m: ve.tensor_scalar(
                                mo_s, g0h[:, nn, :], routt[:, m, 4:5], None, ALU.mult),
                            lambda ve, nn=nn, m=m: ve.scalar_tensor_tensor(
                                mo_s, g1h[:, nn, :], routt[:, m, 5:6], mo_s,
                                ALU.mult, ALU.add),
                            lambda ve: ve.tensor_mul(outroll[:, 0, :], mo_s, gb_t[:]),
                            lambda ve: ve.tensor_add(outroll[:, 0, :], outroll[:, 0, :],
                                                     hsf[:, 0:1024]),
                        ], waits=[(sdm, RET_L), (ssy, x2l), (sve, gbv), (sst, prev_out)])

                        def sy_out(e2, m=m, w=cmb, pw=sst.v):
                            e2.wait_ge(sve.sem, w)
                            e2.wait_ge(sst.sem, pw)
                            e2.dma_start(out=DT["out"][128 * m:128 * (m + 1), :],
                                         in_=outroll[:, 0, :]).then_inc(sst.sem, 16)
                        em("sync", sy_out)
                        prev_out = sst.inc(16)
                        if "moe" in DEBUG:
                            def sy_dm(e2, m=m, w=cmb, pw=sst.v):
                                e2.wait_ge(sve.sem, w)
                                e2.wait_ge(sst.sem, pw)
                                e2.dma_start(out=DT["dbg_moe"][:, 1024 * m:1024 * (m + 1)],
                                             in_=mo_s).then_inc(sst.sem, 16)
                            em("sync", sy_dm)
                            sst.inc(16)

        def fin(e, w=sst.v, vv=sve.v, w_ssy=ssy.v, w_sdm=sdm.v, w_sld=sld.v,
                w_scc=scc.v, w_gp=sgp.v, w_sc=ssc.v, w_pe=spe.v):
            e.wait_ge(sve.sem, vv)
            e.wait_ge(sst.sem, w)
            e.wait_ge(ssy.sem, w_ssy)
            e.wait_ge(sdm.sem, w_sdm)
            e.wait_ge(sld.sem, w_sld)
            e.wait_ge(scc.sem, w_scc)
            e.wait_ge(sgp.sem, w_gp)
            e.wait_ge(ssc.sem, w_sc)
            e.wait_ge(spe.sem, w_pe)
            e.wait_ge(swa.sem, swa.v)
            e.wait_ge(swb.sem, swb.v)
            if KSTOP < 6:
                for cc in range(NCHB):
                    e.dma_start(out=DT["out"][128 * cc:128 * (cc + 1), :],
                                in_=outroll[:, 0, :]).then_inc(sst.sem, 16)
                e.wait_ge(sst.sem, w + 16 * NCHB)
        em("sync", fin)
        if KSTOP < 6:
            sst.inc(16 * NCHB)

        P.realize(ctx.enter_context(nc.Block()))
    return nc


def kernel(**inputs):
    in_maps = make_in_maps(inputs)
    nc = build()
    mybir.codegen_inst_isa_subclasses(nc)
    res = run_bass_kernel_spmd(nc, in_maps, core_ids=list(range(NCORE)))
    out = np.zeros((B * L, H), np.float32)
    for c in range(NCORE):
        out[NTOK * c:NTOK * (c + 1)] = res.results[c]["out"]
    return out.reshape(B, L, H)
